# version 14
# speedup vs baseline: 1.4425x; 1.0014x over previous
import numpy as np
import concourse.bass as bass
import concourse.mybir as mybir
from concourse.bass_utils import run_bass_kernel_spmd

F32 = mybir.dt.float32
BF16 = mybir.dt.bfloat16
I32 = mybir.dt.int32
AF = mybir.ActivationFunctionType
ALU = mybir.AluOpType
AX = mybir.AxisListType

T_LOC = 2048
T_FULL = 4096
D = 1024
NDT = D // 128
EPS_LN = 1e-5
ENGINES = ("sync", "tensor", "vector", "scalar", "gpsimd")
TC = [slice(i * 512, (i + 1) * 512) for i in range(4)]
KC8 = [slice(i * 512, (i + 1) * 512) for i in range(8)]


class Emitter:

    def __init__(self, nc):
        self.nc = nc
        self.q = {e: [] for e in ENGINES}
        self.sems = {}
        self.cnt = {}
        self._ctx = []

    def sem(self, name):
        if name not in self.sems:
            cm = self.nc.semaphore(name)
            self.sems[name] = cm.__enter__()
            self._ctx.append(cm)
            self.cnt[name] = 0
        return self.sems[name]

    def do(self, engine, fn, waits=(), inc=None, inck=1):
        tok = None
        if inc is not None:
            self.sem(inc)
            self.cnt[inc] += inck
            tok = (inc, self.cnt[inc])
        self.q[engine].append((fn, tuple(w for w in waits if w), inc, inck))
        return tok

    def dma(self, engine, out, in_, waits=()):
        qn = "q_" + engine
        return self.do(engine, lambda e, o=out, i=in_: e.dma_start(out=o, in_=i),
                       waits=waits, inc=qn, inck=16)

    def tok(self, semname):
        return (semname, self.cnt.get(semname, 0))

    def guard(self):
        return (self.tok("v"), self.tok("act"), self.tok("gv"))

    def flush(self):
        nc, q = self.nc, self.q
        self.q = {e: [] for e in ENGINES}
        sems = self.sems
        used = [e for e in ENGINES if q[e]]
        if not used:
            return

        def runner(ename):
            def run(eng):
                seen = {}
                for fn, waits, inc, inck in q[ename]:
                    for sname, val in waits:
                        if val > seen.get(sname, 0):
                            eng.wait_ge(sems[sname], val)
                            seen[sname] = val
                    ins = fn(eng)
                    if inc is not None:
                        ins.then_inc(sems[inc], inck)
            return run

        with nc.Block() as block:
            for e in used:
                getattr(block, e)(runner(e))


class Arena:
    def __init__(self, nc, name, nbytes):
        self.nc = nc
        self.backing = nc.alloc_sbuf_tensor(name, [128, nbytes], mybir.dt.uint8)
        self.base = nc.lookup_mloc(self.backing).addr
        self.off = 0
        self.nbytes = nbytes
        self.n = 0

    def alloc(self, shape, dtype, name=None):
        per_part = int(np.prod(shape[1:])) * mybir.dt.size(dtype)
        off = (self.off + 31) & ~31
        assert off + per_part <= self.nbytes, \
            f"arena overflow: {off}+{per_part} > {self.nbytes} ({name})"
        self.off = off + per_part
        self.n += 1
        return self.nc.alloc_sbuf_tensor_at(name or f"ar{self.n}", shape, dtype,
                                            offset=self.base + off)

    def mark(self):
        return self.off

    def reset(self, mark=0):
        self.off = mark

    def alias(self, at, shape, dtype, name):
        per_part = int(np.prod(shape[1:])) * mybir.dt.size(dtype)
        at = (at + 31) & ~31
        assert at + per_part <= self.nbytes
        self.n += 1
        return self.nc.alloc_sbuf_tensor_at(name, shape, dtype,
                                            offset=self.base + at)


def build_program(dump=None, upto=None):
    nc = bass.Bass("TRN2", target_bir_lowering=False)
    E = Emitter(nc)

    DT = nc.dram_tensor
    x_in = DT("x_in", [D, T_LOC], F32, kind="ExternalInput")
    l0_wq = DT("l0_wq", [2048, D], F32, kind="ExternalInput")
    l0_wkvdT = DT("l0_wkvdT", [D, 64], F32, kind="ExternalInput")
    l0_wku = DT("l0_wku", [2048, 64], F32, kind="ExternalInput")
    l0_wvuT = DT("l0_wvuT", [64, 2048], F32, kind="ExternalInput")
    l0_woT = DT("l0_woT", [2048, D], F32, kind="ExternalInput")
    l0_fc1q = DT("l0_fc1q", [D, 4096], I32, kind="ExternalInput")
    l0_fc2q = DT("l0_fc2q", [4096, D], I32, kind="ExternalInput")
    l0_fc1s = DT("l0_fc1s", [8, 4096], F32, kind="ExternalInput")
    l0_fc2s = DT("l0_fc2s", [32, D], F32, kind="ExternalInput")
    ln0_g = DT("ln0_g", [128, 8], F32, kind="ExternalInput")
    ln0_b = DT("ln0_b", [128, 8], F32, kind="ExternalInput")
    fc1b_c = DT("fc1b_c", [128, 32], F32, kind="ExternalInput")
    fc2b_c = DT("fc2b_c", [128, 8], F32, kind="ExternalInput")
    l1_wqp = DT("l1_wqp", [D, 512], F32, kind="ExternalInput")
    l1_wkp = DT("l1_wkp", [D, 512], F32, kind="ExternalInput")
    l1_wv = DT("l1_wv", [D, 1024], F32, kind="ExternalInput")
    l1_wout = DT("l1_wout", [1024, D], F32, kind="ExternalInput")
    l1_upT = DT("l1_upT", [D, 3072], F32, kind="ExternalInput")
    l1_gateT = DT("l1_gateT", [D, 3072], F32, kind="ExternalInput")
    l1_downT = DT("l1_downT", [3072, D], F32, kind="ExternalInput")
    l1_up_rm = DT("l1_up_rm", [3072, D], F32, kind="ExternalInput")
    l1_gate_rm = DT("l1_gate_rm", [3072, D], F32, kind="ExternalInput")
    l1_down_rm = DT("l1_down_rm", [D, 3072], F32, kind="ExternalInput")
    c_ident = DT("c_ident", [128, 128], F32, kind="ExternalInput")
    c_Ekq = DT("c_Ekq", [128, 4], F32, kind="ExternalInput")
    c_Eexp = DT("c_Eexp", [4, 128], F32, kind="ExternalInput")
    c_E64 = DT("c_E64", [128, 2], F32, kind="ExternalInput")
    c_E2 = DT("c_E2", [2, 128], F32, kind="ExternalInput")
    c_Esel = DT("c_Esel", [32, 1024], F32, kind="ExternalInput")

    y_out = DT("y_out", [D, T_LOC], F32, kind="ExternalOutput")
    dbg_out = DT("dbg_out", [128, 16384], F32, kind="ExternalOutput") if dump else None

    ag0_in = DT("ag0_in", [64, T_LOC // 2], I32)
    ag0_out = DT("ag0_out", [128, T_LOC // 2], I32)
    qn_hbm = DT("qn_hbm", [512, T_LOC], BF16)
    outn_hbm = DT("outn_hbm", [1024, T_LOC], BF16)
    ag_kn_in = DT("ag_kn_in", [512, T_LOC // 2], I32)
    ag_kn_out = DT("ag_kn_out", [1024, T_LOC // 2], I32)
    ag_vn_in = DT("ag_vn_in", [T_LOC, 512], I32)
    ag_vn_out = DT("ag_vn_out", [2 * T_LOC, 512], I32)
    ar2_in = DT("ar2_in", [1, 1], F32)
    ar2_out = DT("ar2_out", [1, 1], F32)
    ar2b_in = DT("ar2b_in", [1, 1], F32)
    ar2b_out = DT("ar2b_out", [1, 1], F32)
    ar1_in = DT("ar1_in", [1, 1], F32)
    ar1_out = DT("ar1_out", [1, 1], F32)
    ar3_in = DT("ar3_in", [1, 1], F32)
    ar3_out = DT("ar3_out", [1, 1], F32)
    ar4_in = DT("ar4_in", [1, 1], F32)
    ar4_out = DT("ar4_out", [1, 1], F32)
    up_hbm = DT("up_hbm", [3072, T_LOC], BF16)
    gate_hbm = DT("gate_hbm", [3072, T_LOC], BF16)
    r2_hbm = DT("r2_hbm", [3072, T_LOC], I32)
    m3_hbm = DT("m3_hbm", [3072, T_LOC], F32)
    m4_hbm = DT("m4_hbm", [3072, T_LOC], F32)
    x_spill = DT("x_spill", [D, T_LOC], F32)

    A = nc.alloc_sbuf_tensor
    xs = A("xs", [128, NDT * T_LOC], F32)
    stat_r = A("stat_r", [1, 2 * T_LOC], F32)
    srw = A("srw", [1, 1536], F32)
    ident = A("ident", [128, 128], F32)
    ident_b = A("ident_b", [128, 128], BF16)
    ones_f = A("ones_f", [128, 128], F32)
    ones_b = A("ones_b", [128, 1], BF16)
    Ekq = A("Ekq", [128, 4], F32)
    Eexp = A("Eexp", [4, 128], F32)
    E64 = A("E64", [128, 2], F32)
    E2 = A("E2", [2, 128], F32)
    Esel = A("Esel", [32, 1024], F32)
    g0col = A("g0col", [128, 8], F32)
    b0col = A("b0col", [128, 8], F32)
    fc1bcol = A("fc1bcol", [128, 32], F32)
    fc2bcol = A("fc2bcol", [128, 8], F32)
    scal = A("scal", [1, 32], F32)
    brow_s = A("brow_s", [1, 1024], F32)
    half_f = A("half_f", [1, 128], F32)
    scol = A("scol", [128, 32], F32)
    kvfull = A("kvfull", [64, T_FULL], BF16)
    qtil_all = A("qtil_all", [128, 4 * T_LOC], BF16)
    sq_a = A("sq_a", [128, 512], F32)
    sq_b = A("sq_b", [128, 512], F32)
    bc_a = A("bc_a", [128, 512], F32)
    bc_b = A("bc_b", [128, 512], F32)

    pA = nc.alloc_psum_tensor("pA", [128, 2048], F32)
    pB = nc.alloc_psum_tensor("pB", [128, 2048], F32)

    rem = (nc.sbuf_bytes_remaining - 512) & ~31
    ar = Arena(nc, "arena", rem)

    def xsl(d, t=None):
        if t is None:
            return xs[:, d * T_LOC:(d + 1) * T_LOC]
        return xs[:, d * T_LOC + t.start: d * T_LOC + t.stop]

    def MU(t=None):
        return stat_r[0:1, t.start:t.stop] if t else stat_r[0:1, 0:T_LOC]

    def M2(t=None):
        return stat_r[0:1, T_LOC + t.start:T_LOC + t.stop] if t else stat_r[0:1, T_LOC:2 * T_LOC]

    def dtap(label, emitfns):
        if dump == label:
            for fn in emitfns:
                fn()
            E.do("gpsimd", lambda e: e.nop(), waits=[E.tok("q_gpsimd")])

    for d in range(NDT):
        E.dma("sync", xsl(d), x_in[d * 128:(d + 1) * 128, :])
    for dst, src in ((ident, c_ident), (Ekq, c_Ekq), (Eexp, c_Eexp),
                     (E64, c_E64), (E2, c_E2), (Esel, c_Esel), (g0col, ln0_g), (b0col, ln0_b),
                     (fc1bcol, fc1b_c), (fc2bcol, fc2b_c)):
        E.dma("sync", dst[:, :], src[:, :])
    t_loads = E.tok("q_sync")
    E.do("vector", lambda e: e.memset(ones_f[:, :], 1.0))
    E.do("vector", lambda e: e.memset(ones_b[:, :], 1.0))
    E.do("vector", lambda e: e.memset(half_f[:, :], 0.5), inc="v")
    E.do("vector", lambda e: e.tensor_scalar_mul(
        out=fc2bcol[:, :], in0=fc2bcol[:, :], scalar1=0.5),
        waits=[t_loads], inc="v")
    E.do("vector", lambda e: e.tensor_copy(out=ident_b[:, :], in_=ident[:, :]),
         inc="v")
    E.do("sync", lambda e: e.nop(), waits=[t_loads])
    E.flush()

    def emit_stats():
        sqs = [sq_a, sq_b]
        for ci, tc in enumerate(TC):
            s1 = pA[0:1, 0:512]
            s2 = pA[0:1, 512:1024]
            for d in range(NDT):
                tsq = E.do("scalar",
                           lambda e, d=d, tc=tc, s=sqs[d % 2]:
                           e.activation(out=s[:, :], in_=xsl(d, tc), func=AF.Square),
                           waits=[E.tok("mm")] if d >= 2 else (), inc="act")
                E.do("tensor",
                     lambda e, d=d, tc=tc, s1=s1: e.matmul(
                         s1, ones_f[:, 0:1], xsl(d, tc),
                         start=(d == 0), stop=(d == NDT - 1)),
                     waits=[E.tok("v")] if d == 0 else (), inc="mm")
                E.do("tensor",
                     lambda e, d=d, s2=s2, s=sqs[d % 2]: e.matmul(
                         s2, ones_f[:, 0:1], s[:, :],
                         start=(d == 0), stop=(d == NDT - 1)),
                     waits=[tsq], inc="mm")
            tmm = E.tok("mm")
            E.do("vector", lambda e, tc=tc, s1=s1: e.tensor_scalar(
                out=MU(tc), in0=s1, scalar1=1.0 / D,
                scalar2=None, op0=ALU.mult), waits=[tmm], inc="v")
            E.do("vector", lambda e, tc=tc, s2=s2: e.tensor_scalar(
                out=M2(tc), in0=s2, scalar1=1.0 / D,
                scalar2=None, op0=ALU.mult), inc="v")
        return E.tok("v")

    def emit_ln_bcast(tc, kind):
        g = E.guard()
        s0 = srw[0:1, 0:512]
        s1 = srw[0:1, 512:1024]
        s2 = srw[0:1, 1024:1536]
        E.do("vector", lambda e: e.tensor_tensor(
            out=s0, in0=MU(tc), in1=MU(tc), op=ALU.mult), waits=g)
        E.do("vector", lambda e: e.tensor_sub(out=s2, in0=M2(tc), in1=s0))
        E.do("vector", lambda e: e.tensor_scalar_add(out=s0, in0=s2, scalar1=EPS_LN),
             inc="v")
        E.do("scalar", lambda e: e.activation(out=s0, in_=s0, func=AF.Sqrt),
             waits=[E.tok("v")], inc="act")
        E.do("vector", lambda e: e.reciprocal(out=s0, in_=s0),
             waits=[E.tok("act")], inc="v")
        E.do("vector", lambda e: e.scalar_tensor_tensor(
            out=s1, in0=MU(tc), scalar=-1.0, in1=s0, op0=ALU.mult, op1=ALU.mult),
            inc="v")
        if kind == "acbc":
            E.do("vector", lambda e: e.tensor_tensor(out=s2, in0=s2, in1=s0, op=ALU.mult))
            E.do("vector", lambda e: e.tensor_tensor(out=s2, in0=s2, in1=s0, op=ALU.mult))
            E.do("vector", lambda e: e.tensor_scalar_add(out=s2, in0=s2, scalar1=EPS_LN),
                 inc="v")
            E.do("scalar", lambda e: e.activation(out=s2, in_=s2, func=AF.Sqrt),
                 waits=[E.tok("v")], inc="act")
            E.do("vector", lambda e: e.reciprocal(out=s2, in_=s2),
                 waits=[E.tok("act")], inc="v")
            E.do("vector", lambda e: e.tensor_tensor(out=s0, in0=s0, in1=s2, op=ALU.mult))
            E.do("vector", lambda e: e.tensor_tensor(out=s1, in0=s1, in1=s2, op=ALU.mult),
                 inc="v")
        tlast = E.tok("v")
        out = None
        for row, dest, reg in ((s0, bc_a, pB[:, 1024:1536]),
                               (s1, bc_b, pB[:, 1536:2048])):
            tm = E.do("tensor", lambda e, row=row, reg=reg: e.matmul(
                reg, ones_f[0:1, :], row, start=True, stop=True),
                waits=g + (tlast,), inc="mm")
            out = E.do("scalar", lambda e, reg=reg, dest=dest: e.activation(
                out=dest[:, :], in_=reg, func=AF.Copy), waits=[tm], inc="act")
        return out

    def vch(fn):
        return E.do("vector", fn, waits=[E.tok("v")], inc="v")

    emit_stats()
    E.flush()
    dtap("stats0", [lambda: E.dma("gpsimd", dbg_out[0:1, 0:4096], stat_r[:, :])])

    base_mark = ar.mark()
    h_b = ar.alloc([128, NDT * T_LOC], BF16, "h_b")
    mark3 = ar.mark()
    xn_b = ar.alloc([128, NDT * T_LOC], BF16, "xn_b")
    wkvd_f = ar.alloc([128, 64], F32, "wkvd_f")
    wkvd_b = ar.alloc([128, NDT * 64], BF16, "wkvd_b")
    m_kv = ar.mark()
    kv_own = ar.alloc([64, T_LOC], BF16, "kv_own")
    m_stg = ar.mark()
    stg_f2 = ar.alloc([128, 2048], F32, "stg_f2")
    mh_b = ar.alias(m_stg, [128, NDT * 64], BF16, "mh_b")
    wqo_b = ar.alias(m_kv, [128, 2 * 1024], BF16, "wqo_b")
    wku_b = ar.alias(m_kv + 4 * 1024 + 32, [128, 2 * 64], BF16, "wku_b")

    def hsl(d, t):
        return h_b[:, d * T_LOC + t.start: d * T_LOC + t.stop]

    def xnsl(d, t):
        return xn_b[:, d * T_LOC + t.start: d * T_LOC + t.stop]

    def qtsl(hh, t):
        p = (hh % 2) * 64
        c = (hh // 2) * T_LOC
        return qtil_all[p:p + 64, c + t.start:c + t.stop]

    for d in range(NDT):
        tl = E.dma("sync", wkvd_f[:, :], l0_wkvdT[d * 128:(d + 1) * 128, :],
                   waits=[E.tok("v")])
        E.do("vector", lambda e, d=d: e.tensor_copy(
            out=wkvd_b[:, d * 64:(d + 1) * 64], in_=wkvd_f[:, :]),
            waits=[tl], inc="v")
    t_wkv = E.tok("v")

    for ci, tc in enumerate(TC):
        tb = emit_ln_bcast(tc, "ab")
        kvp = pA[0:64, 0:512]
        for d in range(NDT):
            E.do("vector", lambda e, d=d, tc=tc: e.tensor_tensor(
                out=sq_a[:, :], in0=xsl(d, tc), in1=bc_a[:, :], op=ALU.mult),
                waits=[tb], inc="v")
            tx = E.do("vector", lambda e, d=d, tc=tc: e.tensor_add(
                out=xnsl(d, tc), in0=sq_a[:, :], in1=bc_b[:, :]), inc="v")
            E.do("tensor", lambda e, d=d, tc=tc, kvp=kvp: e.matmul(
                kvp, wkvd_b[:, d * 64:(d + 1) * 64], xnsl(d, tc),
                start=(d == 0), stop=(d == NDT - 1)),
                waits=[tx, t_wkv] + ([E.tok("act")] if d == 0 else []), inc="mm")
        E.do("scalar", lambda e, tc=tc, kvp=kvp: e.activation(
            out=kv_own[:, tc], in_=kvp, func=AF.Copy),
            waits=[E.tok("mm")], inc="act")
    t_kv = E.tok("act")
    t_xnb = E.tok("v")
    tq = E.dma("gpsimd", ag0_in[:, :], kv_own[:, :].bitcast(I32), waits=[t_kv])
    tcc0 = E.do("gpsimd", lambda e: e.collective_compute(
        "AllGather", ALU.bypass,
        replica_groups=[[0, 1], [2, 3], [4, 5], [6, 7]],
        ins=[ag0_in.ap().opt()], outs=[ag0_out.ap().opt()]),
        waits=[tq], inc="cc")

    for ci, tc in enumerate(TC):
        tb = emit_ln_bcast(tc, "acbc")
        for d in range(NDT):
            E.do("vector", lambda e, d=d, tc=tc: e.tensor_tensor(
                out=sq_a[:, :], in0=xsl(d, tc), in1=bc_a[:, :], op=ALU.mult),
                waits=[tb], inc="v")
            E.do("vector", lambda e: e.tensor_add(
                out=sq_a[:, :], in0=sq_a[:, :], in1=bc_b[:, :]))
            E.do("vector", lambda e, d=d, tc=tc: e.tensor_scalar(
                out=hsl(d, tc), in0=sq_a[:, :], scalar1=g0col[:, d:d + 1],
                scalar2=b0col[:, d:d + 1], op0=ALU.mult, op1=ALU.add), inc="v")
    for ci, tc in enumerate(TC):
        for d in range(NDT):
            E.do("vector", lambda e, d=d, tc=tc: e.tensor_add(
                out=xsl(d, tc), in0=xsl(d, tc), in1=xnsl(d, tc)))
            E.do("vector", lambda e, d=d, tc=tc: e.tensor_scalar_add(
                out=xsl(d, tc), in0=xsl(d, tc), scalar1=fc2bcol[:, d:d + 1]),
                inc="v")
    t_h = E.tok("v")

    for hh in range(8):
        for dh in range(2):
            E.dma("sync", stg_f2[:, dh * 1024:(dh + 1) * 1024],
                  l0_wq[hh * 256 + dh * 128: hh * 256 + (dh + 1) * 128, :],
                  waits=[E.tok("v"), E.tok("mm"), tq] if dh == 0 else ())
        t_ldq = E.tok("q_sync")
        t_wq = E.do("vector", lambda e: e.tensor_copy(
            out=wqo_b[:, :], in_=stg_f2[:, :]), waits=[t_ldq], inc="v")
        for dh in range(2):
            E.dma("sync", stg_f2[:, dh * 64:(dh + 1) * 64],
                  l0_wku[hh * 256 + dh * 128: hh * 256 + (dh + 1) * 128, :],
                  waits=[t_wq] if dh == 0 else ())
        t_ldk = E.tok("q_sync")
        t_wk = E.do("vector", lambda e: e.tensor_copy(
            out=wku_b[:, :], in_=stg_f2[:, 0:128]), waits=[t_ldk], inc="v")
        for d in range(NDT):
            reg = pB[:, (d % 2) * 512:(d % 2) * 512 + 64]
            for dh in range(2):
                E.do("tensor", lambda e, d=d, dh=dh, reg=reg: e.matmul(
                    reg, wqo_b[:, dh * 1024 + d * 128: dh * 1024 + (d + 1) * 128],
                    wku_b[:, dh * 64:(dh + 1) * 64],
                    start=(dh == 0), stop=(dh == 1)),
                    waits=[t_wq, t_wk, E.tok("act")] if dh == 0 else (), inc="mm")
            E.do("scalar", lambda e, d=d, reg=reg: e.activation(
                out=mh_b[:, d * 64:(d + 1) * 64], in_=reg, func=AF.Copy),
                waits=[E.tok("mm")], inc="act")
        t_mh = E.tok("act")
        for ci, tc in enumerate(TC):
            reg = pA[0:64, 1024 + (ci % 2) * 512: 1536 + (ci % 2) * 512]
            for d in range(NDT):
                E.do("tensor", lambda e, d=d, tc=tc, reg=reg: e.matmul(
                    reg, mh_b[:, d * 64:(d + 1) * 64], xnsl(d, tc),
                    start=(d == 0), stop=(d == NDT - 1)),
                    waits=[t_mh, t_xnb, E.tok("act")] if d == 0 else (), inc="mm")
            E.do("scalar", lambda e, hh=hh, tc=tc, reg=reg: e.activation(
                out=qtsl(hh, tc), in_=reg, func=AF.Copy),
                waits=[E.tok("mm")], inc="act")
    E.flush()
    dtap("kv", [lambda: E.dma("gpsimd", dbg_out[0:64, 0:2048],
                              kv_own[:, :].bitcast(I32))])

    ar.reset(mark3)
    w1q = ar.alloc([128, NDT * 1024], BF16, "w1q")
    w2q = ar.alloc([128, 8 * 1024], BF16, "w2q")
    m_stg3 = ar.mark()
    stg_i = ar.alloc([128, 2 * 1024], I32, "stg_i")
    ybuf = ar.alias(m_stg3, [128, 8 * 512], BF16, "ybuf")
    sc_bc = ar.alloc([128, 512], F32, "sc_bc")

    def dequant_tile(dst, src_i, s_hbm, lhs_row, wload):
        tsd = E.dma("sync", brow_s[0:1, :], s_hbm, waits=[E.tok("mm")])
        for half in range(2):
            reg = pB[:, half * 512:(half + 1) * 512]
            hs = slice(half * 512, (half + 1) * 512)
            tm = E.do("tensor", lambda e, reg=reg, lhs_row=lhs_row, hs=hs:
                      e.matmul(reg, lhs_row[0:1, :], brow_s[0:1, hs],
                               start=True, stop=True),
                      waits=[tsd, E.tok("act"), E.tok("v")], inc="mm")
            E.do("scalar", lambda e, reg=reg: e.activation(
                out=sc_bc[:, :], in_=reg, func=AF.Copy),
                waits=[tm], inc="act")
            tqb = E.do("scalar", lambda e, dst=dst, src_i=src_i, hs=hs: e.activation(
                out=dst[:, hs], in_=src_i[:, hs], func=AF.Copy, bias=-128.0),
                waits=[wload], inc="act")
            E.do("vector", lambda e, dst=dst, hs=hs: e.tensor_tensor(
                out=dst[:, hs], in0=dst[:, hs], in1=sc_bc[:, :], op=ALU.mult),
                waits=[E.tok("act")], inc="v")
        return E.tok("v")

    t_w2 = None
    silu_toks = [None, None]
    xadd_toks = [None, None]
    for Q in range(4):
        fofs = Q * 1024
        for w in range(4):
            for dd in range(2):
                d = w * 2 + dd
                E.dma("sync", stg_i[:, dd * 1024:(dd + 1) * 1024],
                      l0_fc1q[d * 128:(d + 1) * 128, fofs:fofs + 1024],
                      waits=[E.tok("v")] if dd == 0 and (w > 0 or t_w2) else ())
            t_ld1 = E.tok("q_sync")
            for dd in range(2):
                d = w * 2 + dd
                dequant_tile(w1q[:, d * 1024:(d + 1) * 1024],
                             stg_i[:, dd * 1024:(dd + 1) * 1024],
                             l0_fc1s[d:d + 1, fofs:fofs + 1024],
                             ones_f, t_ld1)
        t_w1 = E.tok("v")
        for w in range(4):
            for jj in range(2):
                j = w * 2 + jj
                E.dma("gpsimd", stg_i[:, jj * 1024:(jj + 1) * 1024],
                      l0_fc2q[fofs + j * 128:fofs + (j + 1) * 128, :],
                      waits=[E.tok("v")] if jj == 0 else ())
            t_ld2 = E.tok("q_gpsimd")
            for jj in range(2):
                j = w * 2 + jj
                dequant_tile(w2q[:, j * 1024:(j + 1) * 1024],
                             stg_i[:, jj * 1024:(jj + 1) * 1024],
                             l0_fc2s[Q * 8 + j:Q * 8 + j + 1, :],
                             half_f, t_ld2)
        t_w2 = E.tok("v")

        for ci, tc in enumerate(TC):
            for j in range(8):
                ftile = Q * 8 + j
                yreg = pA[:, (j % 2) * 512:(j % 2) * 512 + 512]
                for d in range(NDT):
                    E.do("tensor", lambda e, d=d, j=j, tc=tc, yreg=yreg: e.matmul(
                        yreg, w1q[:, d * 1024 + j * 128: d * 1024 + (j + 1) * 128],
                        hsl(d, tc), start=(d == 0), stop=(d == NDT - 1)),
                        waits=([t_w2, t_h, silu_toks[j % 2]]
                               if d == 0 else ()), inc="mm")
                tym = E.tok("mm")
                silu_toks[j % 2] = E.do(
                    "scalar", lambda e, j=j, yreg=yreg, ftile=ftile: e.activation(
                        out=ybuf[:, j * 512:(j + 1) * 512], in_=yreg,
                        func=AF.Silu, bias=fc1bcol[:, ftile:ftile + 1]),
                    waits=[tym], inc="act")
            t_y = E.tok("act")
            for d in range(NDT):
                zreg = pB[:, 512 + (d % 2) * 512: 1024 + (d % 2) * 512]
                for j in range(8):
                    E.do("tensor", lambda e, d=d, j=j, zreg=zreg: e.matmul(
                        zreg, w2q[:, j * 1024 + d * 128: j * 1024 + (d + 1) * 128],
                        ybuf[:, j * 512:(j + 1) * 512],
                        start=(j == 0), stop=(j == 7)),
                        waits=[t_y, xadd_toks[d % 2]] if j == 0 else (), inc="mm")
                tzm = E.tok("mm")
                xadd_toks[d % 2] = E.do(
                    "vector", lambda e, d=d, tc=tc, zreg=zreg: e.tensor_add(
                        out=xsl(d, tc), in0=xsl(d, tc), in1=zreg),
                    waits=[tzm], inc="v")
    E.flush()
    dtap("ffn", [lambda d=d: E.dma("gpsimd", dbg_out[:, d * 2048:(d + 1) * 2048], xsl(d))
                 for d in range(NDT)])
    if upto == "ffn":
        _finish(E, nc, y_out, xsl)
        return nc

    ar.reset(base_mark)
    stg_f = ar.alloc([128, 2048], F32, "stg_f")
    kvT = ar.alloc([128, 32 * 66], BF16, "kvT")
    wvu_b = ar.alloc([64, 2048], BF16, "wvu_b")
    wo_b = ar.alloc([128, 2 * 1024], BF16, "wo_b")
    P_b = ar.alloc([128, 32 * 512], BF16, "P_b")
    on_b = ar.alloc([128, 2 * 512], BF16, "on_b")
    un_b = ar.alloc([64, 1024], BF16, "un_b")
    rz_bc = ar.alloc([64, 512], F32, "rz_bc")
    rzrow = ar.alloc([1, 512], F32, "rzrow")
    qt_l = ar.alloc([64, T_LOC], BF16, "qt_l")

    tw2 = E.dma("sync", stg_f[0:64, :], l0_wvuT[:, :], waits=[E.tok("v")])
    E.do("vector", lambda e: e.tensor_copy(out=wvu_b[:, :], in_=stg_f[0:64, :]),
         waits=[tw2], inc="v")
    t_wvu = E.tok("v")

    E.do("vector", lambda e: e.memset(kvT[:, :], 1.0), inc="v")
    tg1 = E.dma("gpsimd", kvfull[:, 0:T_LOC].bitcast(I32), ag0_out[0:64, :],
                waits=[tcc0])
    tg2 = E.dma("gpsimd", kvfull[:, T_LOC:T_FULL].bitcast(I32), ag0_out[64:128, :])
    t_kvf = E.do("vector", lambda e: e.engine_nop(), waits=[tg1, tg2], inc="v")
    for kt in range(32):
        reg = pA[:, (kt % 2) * 512:(kt % 2) * 512 + 32].bitcast(BF16)
        E.do("tensor", lambda e, kt=kt, reg=reg: e.transpose(
            reg, kvfull[0:64, kt * 128:(kt + 1) * 128], ident_b[0:64, 0:64]),
            waits=[t_kvf, E.tok("act")], inc="mm")
        E.do("scalar", lambda e, kt=kt, reg=reg: e.activation(
            out=kvT[:, kt * 66:kt * 66 + 64], in_=reg, func=AF.Copy),
            waits=[E.tok("mm")], inc="act")
    t_kvT = E.tok("act")
    E.flush()

    exp_toks = [None, None]
    for hh in range(8):
        for o in range(2):
            E.dma("sync", stg_f[:, o * 1024:(o + 1) * 1024],
                  l0_woT[hh * 256 + o * 128: hh * 256 + (o + 1) * 128, :],
                  waits=[E.tok("v"), E.tok("mm")] if o == 0 else ())
        t_ldo = E.tok("q_sync")
        t_wo = E.do("vector", lambda e: e.tensor_copy(
            out=wo_b[:, :], in_=stg_f[:, :]), waits=[t_ldo], inc="v")
        t_qtl = E.do("vector", lambda e, hh=hh: e.tensor_copy(
            out=qt_l[:, :], in_=qtil_all[(hh % 2) * 64:(hh % 2) * 64 + 64,
                                         (hh // 2) * T_LOC:(hh // 2 + 1) * T_LOC]),
            waits=[E.tok("mm")], inc="v")

        pend = None
        for ci, tc in enumerate(TC):
            uz = pA[0:65, 1024 + (ci % 2) * 512:1536 + (ci % 2) * 512]
            unb = un_b[:, (ci % 2) * 512:(ci % 2) * 512 + 512]
            exp_k = [None] * 32

            def emit_uz(kt, uz=uz, exp_k=exp_k, ew=()):
                E.do("tensor", lambda e, kt=kt, uz=uz: e.matmul(
                    uz, kvT[:, kt * 66:kt * 66 + 65],
                    P_b[:, kt * 512:(kt + 1) * 512],
                    start=(kt == 0), stop=(kt == 31)),
                    waits=[exp_k[kt]] + list(ew), inc="mm")

            for kt in range(32):
                sreg = pA[:, (kt % 2) * 512:(kt % 2) * 512 + 512]
                tm = E.do("tensor", lambda e, kt=kt, tc=tc, sreg=sreg: e.matmul(
                    sreg, kvfull[:, kt * 128:(kt + 1) * 128], qt_l[:, tc],
                    start=True, stop=True),
                    waits=[t_kvf, t_kvT, t_qtl, exp_toks[kt % 2]] +
                          ([E.tok("v")] if kt == 0 else []), inc="mm")
                exp_toks[kt % 2] = exp_k[kt] = E.do(
                    "scalar", lambda e, kt=kt, sreg=sreg: e.activation(
                        out=P_b[:, kt * 512:(kt + 1) * 512], in_=sreg,
                        func=AF.Exp, scale=1.0 / 16.0), waits=[tm], inc="act")
                if kt >= 2:
                    emit_uz(kt - 2, ew=[t_kvT, E.tok("v")] if kt == 2 else ())
            emit_uz(30)
            emit_uz(31)
            t_exp = E.tok("act")
            if dump == "attn1" and hh == 0 and ci == 0:
                E.do("gpsimd", lambda e: e.nop(), waits=[t_exp])
                E.dma("gpsimd", dbg_out[:, 4096:12288].bitcast(BF16), P_b[:, :])
                E.do("gpsimd", lambda e: e.nop(), waits=[E.tok("q_gpsimd")])
            t_uz = E.tok("mm")
            if pend is not None:
                pend()
            trz = E.do("vector", lambda e, uz=uz: e.reciprocal(
                out=rzrow[:, :], in_=uz[64:65, :]), waits=[t_uz], inc="v")
            tbm = E.do("tensor", lambda e: e.matmul(
                pB[0:64, 0:512], ones_f[0:1, 0:64], rzrow[:, :],
                start=True, stop=True), waits=[trz, E.tok("act")], inc="mm")
            trzb = E.do("scalar", lambda e: e.activation(
                out=rz_bc[:, :], in_=pB[0:64, 0:512], func=AF.Copy),
                waits=[tbm, E.tok("v")], inc="act")
            t_un = E.do("vector", lambda e, uz=uz, unb=unb: e.tensor_tensor(
                out=unb, in0=uz[0:64, :], in1=rz_bc[:, :], op=ALU.mult),
                waits=[trzb], inc="v")

            def pend(ci=ci, tc=tc, unb=unb, t_un=t_un, trzb=trzb):
                for dv in range(2):
                    oreg = pB[:, dv * 512:(dv + 1) * 512]
                    E.do("tensor", lambda e, dv=dv, oreg=oreg, unb=unb:
                         e.matmul(
                             oreg,
                             wvu_b[:, hh * 256 + dv * 128: hh * 256 + (dv + 1) * 128],
                             unb, start=True, stop=True),
                         waits=[t_un, t_wvu, trzb] if dv == 0 else (), inc="mm")
                    E.do("scalar", lambda e, dv=dv, oreg=oreg: e.activation(
                        out=on_b[:, dv * 512:(dv + 1) * 512], in_=oreg,
                        func=AF.Copy), waits=[E.tok("mm")], inc="act")
                t_on = E.tok("act")
                for d in range(NDT):
                    wreg = pB[:, 1024 + (d % 2) * 512: 1536 + (d % 2) * 512]
                    for o in range(2):
                        E.do("tensor", lambda e, d=d, o=o, wreg=wreg: e.matmul(
                            wreg,
                            wo_b[:, o * 1024 + d * 128: o * 1024 + (d + 1) * 128],
                            on_b[:, o * 512:(o + 1) * 512],
                            start=(o == 0), stop=(o == 1)),
                            waits=[t_on, t_wo, E.tok("v")] if o == 0 else (),
                            inc="mm")
                    E.do("vector", lambda e, d=d, tc=tc, wreg=wreg: e.tensor_add(
                        out=xsl(d, tc), in0=xsl(d, tc), in1=wreg),
                        waits=[E.tok("mm")], inc="v")
        pend()
    E.flush()

    dtap("l0", [lambda d=d: E.dma("gpsimd", dbg_out[:, d * 2048:(d + 1) * 2048], xsl(d))
                for d in range(NDT)])
    if upto == "l0":
        _finish(E, nc, y_out, xsl)
        return nc

    ar.reset(0)
    xn1_b = ar.alloc([128, NDT * T_LOC], BF16, "xn1_b")

    def xn1sl(d, t=None):
        if t is None:
            return xn1_b[:, d * T_LOC:(d + 1) * T_LOC]
        return xn1_b[:, d * T_LOC + t.start: d * T_LOC + t.stop]

    emit_stats()
    for ci, tc in enumerate(TC):
        tb = emit_ln_bcast(tc, "ab")
        for d in range(NDT):
            E.do("vector", lambda e, d=d, tc=tc: e.tensor_tensor(
                out=sq_a[:, :], in0=xsl(d, tc), in1=bc_a[:, :], op=ALU.mult),
                waits=[tb], inc="v")
            E.do("vector", lambda e, d=d, tc=tc: e.tensor_add(
                out=xn1sl(d, tc), in0=sq_a[:, :], in1=bc_b[:, :]), inc="v")
    t_xn1 = E.tok("v")
    for d in range(NDT):
        E.do("vector", lambda e, d=d: e.tensor_reduce(
            out=sq_a[:, 0:1] if d == 0 else sq_a[:, 1:2], in_=xn1sl(d),
            axis=AX.X, op=ALU.max, apply_absolute_value=True),
            waits=[t_xn1, E.tok("v")] if d == 0 else [E.tok("v")], inc="v")
        if d > 0:
            vch(lambda e: e.tensor_max(
                out=sq_a[:, 0:1], in0=sq_a[:, 0:1], in1=sq_a[:, 1:2]))
    tmx = E.do("vector", lambda e: e.engine_nop(), inc="v")
    tt1 = E.do("tensor", lambda e: e.transpose(
        pB[0:1, 0:128], sq_a[:, 0:1], ident[:, :]),
        waits=[tmx, E.tok("act")], inc="mm")
    tt2 = E.do("scalar", lambda e: e.activation(
        out=brow_s[0:1, 0:128], in_=pB[0:1, 0:128], func=AF.Copy),
        waits=[tt1], inc="act")
    E.do("vector", lambda e: e.tensor_reduce(
        out=scal[0:1, 0:1], in_=brow_s[0:1, 0:128], axis=AX.X, op=ALU.max),
        waits=[tt2], inc="v")
    tg1v = E.tok("v")
    tq1 = E.dma("gpsimd", ar1_in[:, :], scal[0:1, 0:1], waits=[tg1v])
    tar1 = E.do("gpsimd", lambda e: e.collective_compute(
        "AllReduce", ALU.max, replica_groups=[list(range(8))],
        ins=[ar1_in.ap().opt()], outs=[ar1_out.ap().opt()]),
        waits=[tq1], inc="cc")
    mark_aproj = ar.mark()
    wslab = ar.alloc([128, 8 * 128], BF16, "wslab")
    wstg = ar.alloc([128, 512], F32, "wstg")
    pstg = ar.alloc([128, 1024], BF16, "pstg")
    sqv = ar.alloc([128, 512], F32, "sqv")
    nsc = ar.alloc([4, 512], F32, "nsc")
    nv_t = ar.alloc([128, 16], F32, "nv_t")
    wv_b = ar.alloc([128, 8 * 1024], BF16, "wv_b")

    sqv_w = ar.alias(43072, [128, 2048], F32, "sqv_w")
    nsc_w = ar.alias(51264, [4, 2048], F32, "nsc_w")

    t_mult = [None] * 4

    def proj_padded(w_hbm, out_tile_fn):
        t_dma = [None, None]
        for ot in range(4):
            for d in range(NDT):
                tw = E.dma("sync", wstg[:, 0:128],
                           w_hbm[d * 128:(d + 1) * 128, ot * 128:(ot + 1) * 128],
                           waits=[E.tok("v")])
                E.do("vector", lambda e, d=d: e.tensor_copy(
                    out=wslab[:, d * 128:(d + 1) * 128], in_=wstg[:, 0:128]),
                    waits=[tw], inc="v")
            t_ws = E.tok("v")
            for ti, tc in enumerate(TC):
                for d in range(NDT):
                    E.do("tensor", lambda e, d=d, tc=tc, ti=ti: e.matmul(
                        pA[:, ti * 512:(ti + 1) * 512],
                        wslab[:, d * 128:(d + 1) * 128], xn1sl(d, tc),
                        start=(d == 0), stop=(d == NDT - 1)),
                        waits=([t_ws, t_xn1] +
                               ([t_mult[ti]] if t_mult[ti] else
                                [E.tok("act"), E.tok("v")]))
                        if d == 0 else (), inc="mm")
            tpp = E.tok("mm")
            tsq0 = E.do("scalar", lambda e: e.activation(
                out=sqv_w[:, 0:1024], in_=pA[:, 0:1024], func=AF.Square),
                waits=[tpp], inc="act")
            tsq = E.do("scalar", lambda e: e.activation(
                out=sqv_w[:, 1024:2048], in_=pA[:, 1024:2048], func=AF.Square),
                inc="act")
            for ti in range(4):
                E.do("tensor", lambda e, ti=ti: e.matmul(
                    pB[0:4, ti * 512:(ti + 1) * 512], Ekq[:, :],
                    sqv_w[:, ti * 512:(ti + 1) * 512], start=True, stop=True),
                    waits=[tsq0, tsq, E.tok("v")] if ti == 0 else (), inc="mm")
            tnm = E.tok("mm")
            E.do("scalar", lambda e: e.activation(
                out=nsc_w[:, :], in_=pB[0:4, 0:2048], func=AF.Sqrt),
                waits=[tnm], inc="act")
            E.do("vector", lambda e: e.engine_nop(), waits=[E.tok("act")],
                 inc="v")
            vch(lambda e: e.tensor_scalar_max(
                out=nsc_w[:, :], in0=nsc_w[:, :], scalar1=1e-12))
            vch(lambda e: e.reciprocal(out=nsc_w[:, :], in_=nsc_w[:, :]))
            trc = E.tok("v")
            for ti in range(4):
                tex = E.do("tensor", lambda e, ti=ti: e.matmul(
                    pB[:, ti * 512:(ti + 1) * 512], Eexp[:, :],
                    nsc_w[:, ti * 512:(ti + 1) * 512], start=True, stop=True),
                    waits=([trc, E.tok("act")] if ti == 0 else []), inc="mm")
                tev = E.do("scalar", lambda e, ti=ti: e.activation(
                    out=sqv_w[:, ti * 512:(ti + 1) * 512],
                    in_=pB[:, ti * 512:(ti + 1) * 512], func=AF.Copy),
                    waits=[tex], inc="act")
                t_mult[ti] = tno = E.do(
                    "vector", lambda e, ti=ti: e.tensor_tensor(
                        out=pstg[:, (ti % 2) * 512:(ti % 2) * 512 + 512],
                        in0=pA[:, ti * 512:(ti + 1) * 512],
                        in1=sqv_w[:, ti * 512:(ti + 1) * 512], op=ALU.mult),
                    waits=[tev] + ([t_dma[ti % 2]] if t_dma[ti % 2] else []),
                    inc="v")
                t_dma[ti % 2] = out_tile_fn(ot, ti, tno)

    def q_out(ot, ti, tok):
        return E.dma(
            "gpsimd", qn_hbm[ot * 128:(ot + 1) * 128, ti * 512:(ti + 1) * 512],
            pstg[:, (ti % 2) * 512:(ti % 2) * 512 + 512], waits=[tok])

    def k_out(ot, ti, tok):
        return E.dma(
            "gpsimd", ag_kn_in[ot * 128:(ot + 1) * 128, ti * 256:(ti + 1) * 256],
            pstg[:, (ti % 2) * 512:(ti % 2) * 512 + 512].bitcast(I32),
            waits=[tok])

    proj_padded(l1_wqp, q_out)
    proj_padded(l1_wkp, k_out)

    for d in range(NDT):
        tw = E.dma("sync", wstg[:, 0:512],
                   l1_wv[d * 128:(d + 1) * 128, 0:512], waits=[E.tok("v")])
        E.do("vector", lambda e, d=d: e.tensor_copy(
            out=wv_b[:, d * 1024:d * 1024 + 512], in_=wstg[:, 0:512]),
            waits=[tw], inc="v")
        tw = E.dma("sync", wstg[:, 0:512],
                   l1_wv[d * 128:(d + 1) * 128, 512:1024], waits=[E.tok("v")])
        E.do("vector", lambda e, d=d: e.tensor_copy(
            out=wv_b[:, d * 1024 + 512:(d + 1) * 1024], in_=wstg[:, 0:512]),
            waits=[tw], inc="v")
    t_wv = E.tok("v")
    sqv_v = ar.alias(59456, [128, 1024], F32, "sqv_v")
    for vt in range(16):
        for dvc in range(2):
            reg = pA[:, 1024 + dvc * 512:1536 + dvc * 512]
            for d in range(NDT):
                E.do("tensor", lambda e, d=d, vt=vt, dvc=dvc, reg=reg: e.matmul(
                    reg, xn1_b[:, d * T_LOC + vt * 128:d * T_LOC + (vt + 1) * 128],
                    wv_b[:, d * 1024 + dvc * 512:d * 1024 + (dvc + 1) * 512],
                    start=(d == 0), stop=(d == NDT - 1)),
                    waits=[t_wv, t_xn1, E.tok("act"), E.tok("v")]
                    if (dvc == 0 and d == 0) else (), inc="mm")
        tpp = E.tok("mm")
        tsq = E.do("scalar", lambda e: e.activation(
            out=sqv_v[:, :], in_=pA[:, 1024:2048], func=AF.Square),
            waits=[tpp], inc="act")
        E.do("vector", lambda e: e.tensor_reduce(
            out=nv_t[:, 0:16],
            in_=sqv_v[:, :].rearrange("p (a b) -> p a b", b=64),
            axis=AX.X, op=ALU.add),
            waits=[tsq, E.tok("v")], inc="v")
        E.do("scalar", lambda e: e.activation(
            out=pstg[:, 0:1024], in_=pA[:, 1024:2048], func=AF.Copy),
            waits=[E.tok("v")], inc="act")
        E.do("vector", lambda e: e.engine_nop(), waits=[E.tok("act")], inc="v")
        vch(lambda e: e.tensor_copy(out=sqv[:, 0:16], in_=nv_t[:, :]))
        E.do("scalar", lambda e: e.activation(
            out=sqv[:, 0:16], in_=sqv[:, 0:16], func=AF.Sqrt),
            waits=[E.tok("v")], inc="act")
        E.do("vector", lambda e: e.engine_nop(), waits=[E.tok("act")], inc="v")
        vch(lambda e: e.tensor_scalar_max(
            out=sqv[:, 0:16], in0=sqv[:, 0:16], scalar1=1e-12))
        vch(lambda e: e.reciprocal(out=sqv[:, 0:16], in_=sqv[:, 0:16]))
        tno = vch(lambda e: e.tensor_tensor(
            out=pstg[:, :].rearrange("p (a b) -> p a b", b=64),
            in0=pstg[:, :].rearrange("p (a b) -> p a b", b=64),
            in1=sqv[:, 0:16].unsqueeze(2).broadcast_to([128, 16, 64]),
            op=ALU.mult))
        E.dma("gpsimd", ag_vn_in[vt * 128:(vt + 1) * 128, :],
              pstg[:, :].bitcast(I32), waits=[tno])
    t_vdone = E.tok("q_gpsimd")
    tag_k = E.do("gpsimd", lambda e: e.collective_compute(
        "AllGather", ALU.bypass,
        replica_groups=[[0, 1], [2, 3], [4, 5], [6, 7]],
        ins=[ag_kn_in.ap().opt()], outs=[ag_kn_out.ap().opt()]),
        waits=[t_vdone], inc="cc")
    tag_v = E.do("gpsimd", lambda e: e.collective_compute(
        "AllGather", ALU.bypass,
        replica_groups=[[0, 1], [2, 3], [4, 5], [6, 7]],
        ins=[ag_vn_in.ap().opt()], outs=[ag_vn_out.ap().opt()]),
        inc="cc")
    ar.reset(mark_aproj)
    E.flush()

    def sc(i):
        return scal[0:1, i:i + 1]

    def bcast_col(src, col):
        reg = pB[:, 1536:1544]
        tm = E.do("tensor", lambda e, src=src, reg=reg: e.matmul(
            reg[:, 0:1], ones_f[0:1, :], src, start=True, stop=True),
            waits=(E.tok("v"), E.tok("act")), inc="mm")
        return E.do("scalar", lambda e, col=col, reg=reg: e.activation(
            out=scol[:, col:col + 1], in_=reg[:, 0:1], func=AF.Copy),
            waits=[tm], inc="act")

    r1_b = ar.alloc([128, NDT * T_LOC], BF16, "r1_b")
    m_stg6 = ar.mark()
    stg_w = ar.alloc([128, 2 * 512], F32, "stg_w")
    stg_q = ar.alloc([128, 512], I32, "stg_q")
    w_big = ar.alloc([128, 8 * 512], BF16, "w_big")
    ub_s = ar.alias(m_stg6, [128, 2048], BF16, "ub_s")

    E.do("gpsimd", lambda e: e.nop(), waits=[tar1])
    tld = E.dma("gpsimd", sc(0), ar1_out[:, 0:1])
    E.do("vector", lambda e: e.engine_nop(), waits=[tld], inc="v")
    bcast_col(sc(0), 16)
    E.do("vector", lambda e: e.engine_nop(), waits=[E.tok("act")], inc="v")
    vch(lambda e: e.tensor_scalar(
        out=scol[:, 17:18], in0=scol[:, 16:17], scalar1=1.0 / 32767.0,
        scalar2=1e-12, op0=ALU.mult, op1=ALU.max))
    vch(lambda e: e.reciprocal(out=scol[:, 0:1], in_=scol[:, 17:18]))
    t_c1 = E.tok("v")
    stg_r1 = ar.alias(m_stg6, [128, 2048], I32, "stg_r1")
    for d in range(NDT):
        E.do("vector", lambda e, d=d: e.tensor_scalar(
            out=stg_r1[:, :], in0=xn1sl(d), scalar1=scol[:, 0:1],
            scalar2=None, op0=ALU.mult),
            waits=[t_c1] if d == 0 else (), inc="v")
        E.do("vector", lambda e, d=d: e.tensor_copy(
            out=r1_b[:, d * T_LOC:(d + 1) * T_LOC], in_=stg_r1[:, :]), inc="v")
    t_r1 = E.tok("v")
    if dump == "l1sc":
        E.do("gpsimd", lambda e: e.nop(), waits=[t_r1, E.tok("act")])
        E.dma("gpsimd", dbg_out[4:5, 0:32], scal[0:1, 0:32])
        E.do("gpsimd", lambda e: e.nop(), waits=[E.tok("q_gpsimd")])
    sTu = ar.alias(0, [32, 3072], F32, "sTu")
    rTu = ar.alias(12 * 1024 + 32, [32, 3072], F32, "rTu")

    def lpbq_scales(w_rm, sT, rT, nrt, first_wait=()):
        t_red = [None, None]
        t_tt = [None, None]
        for rt in range(nrt):
            pp = rt % 2
            hs = slice(pp * 1024, (pp + 1) * 1024)
            tw = E.dma("sync", stg_f2[:, hs], w_rm[rt * 128:(rt + 1) * 128, :],
                       waits=((E.tok("v"), E.tok("mm")) + tuple(first_wait))
                       if rt < 2 else [t_red[pp]])
            t_red[pp] = E.do("vector", lambda e, pp=pp, hs=hs: e.tensor_reduce(
                out=sq_a[:, pp * 32:(pp + 1) * 32],
                in_=stg_f2[:, hs].rearrange("p (a b) -> p a b", b=32),
                axis=AX.X, op=ALU.max, apply_absolute_value=True),
                waits=[tw], inc="v")
            E.do("vector", lambda e, pp=pp: e.tensor_scalar(
                out=sq_a[:, 64 + pp * 32:64 + (pp + 1) * 32],
                in0=sq_a[:, pp * 32:(pp + 1) * 32], scalar1=1.0 / 7.0,
                scalar2=1e-12, op0=ALU.mult, op1=ALU.max),
                waits=[E.tok("v")] + ([t_tt[pp]] if t_tt[pp] else []), inc="v")
            tv = E.tok("v")
            t_tt[pp] = tt = E.do("tensor", lambda e, pp=pp: e.transpose(
                pB[0:32, 0:128], sq_a[:, 64 + pp * 32:64 + (pp + 1) * 32],
                ident[:, :]), waits=[tv, E.tok("act")], inc="mm")
            E.do("scalar", lambda e, rt=rt, sT=sT: e.activation(
                out=sT[0:32, rt * 128:(rt + 1) * 128], in_=pB[0:32, 0:128],
                func=AF.Copy), waits=[tt], inc="act")
        tse = E.tok("act")
        return E.do("vector", lambda e, sT=sT, rT=rT: e.reciprocal(
            out=rT[:, :], in_=sT[:, :]), waits=[tse], inc="v")

    E.do("vector", lambda e: e.memset(scol[:, 1:3], 0.0), inc="v")

    def l1_mm_pass(wT_hbm, sT, rT, hbm_dst, maxcol, t_sc):
        for oc in range(6):
            for it in range(NDT):
                tw = E.dma("sync", stg_w[:, 0:512],
                           wT_hbm[it * 128:(it + 1) * 128,
                                  oc * 512:(oc + 1) * 512],
                           waits=[E.tok("v"), E.tok("act"), E.tok("q_gpsimd")]
                           if it == 0 else [E.tok("v")])
                rreg = pB[:, 1024:1536]
                sreg2 = pB[:, 1536:2048]
                tm1 = E.do("tensor", lambda e, it=it, oc=oc, rT=rT, rreg=rreg:
                           e.matmul(rreg, Esel[:, it * 128:(it + 1) * 128],
                                    rT[:, oc * 512:(oc + 1) * 512],
                                    start=True, stop=True),
                           waits=[t_sc, E.tok("v"), E.tok("act")], inc="mm")
                tm2 = E.do("tensor", lambda e, it=it, oc=oc, sT=sT, sreg2=sreg2:
                           e.matmul(sreg2, Esel[:, it * 128:(it + 1) * 128],
                                    sT[:, oc * 512:(oc + 1) * 512],
                                    start=True, stop=True), inc="mm")
                E.do("vector", lambda e, rreg=rreg: e.tensor_tensor(
                    out=stg_w[:, 512:1024], in0=stg_w[:, 0:512], in1=rreg,
                    op=ALU.mult), waits=[tm1, tw], inc="v")
                E.do("vector", lambda e: e.tensor_copy(
                    out=stg_q[:, :], in_=stg_w[:, 512:1024]), inc="v")
                tse2 = E.do("scalar", lambda e, sreg2=sreg2: e.activation(
                    out=stg_w[:, 512:1024], in_=sreg2, func=AF.Copy),
                    waits=[tm2, E.tok("v")], inc="act")
                E.do("vector", lambda e, it=it: e.tensor_tensor(
                    out=w_big[:, it * 512:(it + 1) * 512], in0=stg_q[:, :],
                    in1=stg_w[:, 512:1024], op=ALU.mult),
                    waits=[tse2], inc="v")
            t_wb = E.tok("v")
            for otl in range(4):
                ot = oc * 4 + otl
                for ci, tc in enumerate(TC):
                    reg = pA[:, (ci % 2) * 512:(ci % 2) * 512 + 512]
                    for it in range(NDT):
                        E.do("tensor", lambda e, it=it, otl=otl, tc=tc, reg=reg:
                             e.matmul(reg,
                                      w_big[:, it * 512 + otl * 128:
                                            it * 512 + (otl + 1) * 128],
                                      r1_b[:, it * T_LOC + tc.start:
                                           it * T_LOC + tc.stop],
                                      start=(it == 0), stop=(it == NDT - 1)),
                             waits=([t_wb, t_r1, E.tok("act")]
                                    if it == 0 else ()), inc="mm")
                    E.do("scalar", lambda e, tc=tc, reg=reg: e.activation(
                        out=ub_s[:, tc], in_=reg, func=AF.Copy),
                        waits=[E.tok("mm")], inc="act")
                tev = E.tok("act")
                E.do("vector", lambda e: e.tensor_reduce(
                    out=sq_a[:, 0:1], in_=ub_s[:, :], axis=AX.X, op=ALU.max,
                    apply_absolute_value=True), waits=[tev, E.tok("v")], inc="v")
                vch(lambda e, maxcol=maxcol: e.tensor_max(
                    out=scol[:, maxcol:maxcol + 1],
                    in0=scol[:, maxcol:maxcol + 1], in1=sq_a[:, 0:1]))
                E.dma("gpsimd", hbm_dst[ot * 128:(ot + 1) * 128, :], ub_s[:, :],
                      waits=[tev])
                E.do("scalar", lambda e: e.nop(), waits=[E.tok("q_gpsimd")])
        return E.tok("v")

    t_sc_u = lpbq_scales(l1_up_rm, sTu, rTu, 24, first_wait=[t_r1])
    t_up = l1_mm_pass(l1_upT, sTu, rTu, up_hbm, 1, t_sc_u)
    if dump == "l1sc":
        E.do("gpsimd", lambda e: e.nop(), waits=[t_up, E.tok("act"), E.tok("mm")])
        E.dma("gpsimd", dbg_out[5:6, 0:32], scal[0:1, 0:32])
        E.do("gpsimd", lambda e: e.nop(), waits=[E.tok("q_gpsimd")])
    t_sc_g = lpbq_scales(l1_gate_rm, sTu, rTu, 24, first_wait=[t_up])
    t_gate = l1_mm_pass(l1_gateT, sTu, rTu, gate_hbm, 2, t_sc_g)

    def partition_max_to(srccol, dst, wait):
        tt1 = E.do("tensor", lambda e, srccol=srccol: e.transpose(
            pB[0:1, 0:128], srccol, ident[:, :]),
            waits=[wait, E.tok("act")], inc="mm")
        tt2 = E.do("scalar", lambda e: e.activation(
            out=brow_s[0:1, 0:128], in_=pB[0:1, 0:128], func=AF.Copy),
            waits=[tt1], inc="act")
        return E.do("vector", lambda e, dst=dst: e.tensor_reduce(
            out=dst, in_=brow_s[0:1, 0:128], axis=AX.X, op=ALU.max),
            waits=[tt2], inc="v")

    tv4 = partition_max_to(scol[:, 1:2], sc(4), t_gate)
    tq3 = E.dma("gpsimd", ar2_in[:, :], sc(4), waits=[tv4])
    tar2a = E.do("gpsimd", lambda e: e.collective_compute(
        "AllReduce", ALU.max, replica_groups=[list(range(8))],
        ins=[ar2_in.ap().opt()], outs=[ar2_out.ap().opt()]),
        waits=[tq3], inc="cc")
    tv5 = partition_max_to(scol[:, 2:3], sc(5), t_gate)
    tq3b = E.dma("gpsimd", ar2b_in[:, :], sc(5), waits=[tv5])
    tar2 = E.do("gpsimd", lambda e: e.collective_compute(
        "AllReduce", ALU.max, replica_groups=[list(range(8))],
        ins=[ar2b_in.ap().opt()], outs=[ar2b_out.ap().opt()]),
        waits=[tq3b], inc="cc")
    E.flush()
    if dump == "l1ug":
        E.do("gpsimd", lambda e: e.nop(), waits=[E.tok("cc")])
        E.dma("gpsimd", dbg_out[0:1, 0:1], ar2_out[:, :])
        E.dma("gpsimd", dbg_out[0:1, 1:2], ar2b_out[:, :])
        E.dma("gpsimd", dbg_out[:, 4096:5120].bitcast(BF16), up_hbm[0:128, :])
        E.dma("gpsimd", dbg_out[:, 5120:6144].bitcast(BF16),
              gate_hbm[20 * 128:21 * 128, :])
        E.do("gpsimd", lambda e: e.nop(), waits=[E.tok("q_gpsimd")])

    ta4 = E.dma("gpsimd", sc(4), ar2_out[:, :], waits=[tar2a])
    tld = E.dma("gpsimd", sc(5), ar2b_out[:, :], waits=[tar2])
    E.do("vector", lambda e: e.engine_nop(), waits=[ta4, tld], inc="v")
    bcast_col(sc(4), 18)
    bcast_col(sc(5), 19)
    E.do("vector", lambda e: e.engine_nop(), waits=[E.tok("act")], inc="v")
    vch(lambda e: e.tensor_tensor(
        out=scol[:, 18:19], in0=scol[:, 18:19], in1=scol[:, 17:18], op=ALU.mult))
    vch(lambda e: e.tensor_scalar(
        out=scol[:, 20:21], in0=scol[:, 18:19], scalar1=1.0 / 32767.0,
        scalar2=1e-12, op0=ALU.mult, op1=ALU.max))
    vch(lambda e: e.reciprocal(out=scol[:, 3:4], in_=scol[:, 20:21]))
    vch(lambda e: e.tensor_tensor(
        out=scol[:, 3:4], in0=scol[:, 3:4], in1=scol[:, 17:18], op=ALU.mult))
    vch(lambda e: e.tensor_tensor(
        out=scol[:, 19:20], in0=scol[:, 19:20], in1=scol[:, 17:18], op=ALU.mult))
    vch(lambda e: e.tensor_scalar(
        out=scol[:, 5:6], in0=scol[:, 19:20], scalar1=1.0 / 32767.0,
        scalar2=1e-12, op0=ALU.mult, op1=ALU.max))
    vch(lambda e: e.reciprocal(out=scol[:, 4:5], in_=scol[:, 5:6]))
    vch(lambda e: e.tensor_tensor(
        out=scol[:, 4:5], in0=scol[:, 4:5], in1=scol[:, 17:18], op=ALU.mult))
    t_sc2 = E.tok("v")

    if dump == "l1sc":
        E.do("gpsimd", lambda e: e.nop(), waits=[E.tok("v"), E.tok("act")])
        E.dma("gpsimd", dbg_out[6:7, 0:32], scal[0:1, 0:32])
        E.do("gpsimd", lambda e: e.nop(), waits=[E.tok("q_gpsimd")])
    pa_g2 = [ar.alias(0, [128, 2048], BF16, "pa_g2a"),
             ar.alias(4096, [128, 2048], BF16, "pa_g2b")]
    pb_u2 = [ar.alias(8192, [128, 2048], BF16, "pb_u2a"),
             ar.alias(12288, [128, 2048], BF16, "pb_u2b")]
    pa_sg2 = [ar.alias(16384, [128, 2048], F32, "pa_sg2a"),
              ar.alias(24576, [128, 2048], F32, "pa_sg2b")]
    pa_rg2 = [ar.alias(32768, [128, 2048], I32, "pa_rg2a"),
              ar.alias(40960, [128, 2048], I32, "pa_rg2b")]
    pa_m32 = [ar.alias(49152, [128, 2048], F32, "pa_m32a"),
              ar.alias(57344, [128, 2048], F32, "pa_m32b")]
    pa_q2 = ar.alias(65536, [128, 2048], I32, "pa_q2")
    pa_m3 = ar.alias(24704, [128, 2048], F32, "pa_m3")
    stg_q2 = ar.alias(41 * 1024 + 128, [128, 2048], I32, "stg_q2")

    t_g_rd = [None, None]
    t_m3st = [None, None]
    for rt in range(24):
        pp = rt % 2
        twg = E.dma("sync", pa_g2[pp][:, :], gate_hbm[rt * 128:(rt + 1) * 128, :],
                    waits=([E.tok("mm"), E.tok("v"), E.tok("act"),
                            E.tok("q_gpsimd")] if rt == 0 else
                           ([t_g_rd[pp]] if t_g_rd[pp] else [])))
        t_g_rd[pp] = E.do("vector", lambda e, pp=pp: e.tensor_scalar(
            out=pa_rg2[pp][:, :], in0=pa_g2[pp][:, :], scalar1=scol[:, 4:5],
            scalar2=None, op0=ALU.mult), waits=[twg, t_sc2], inc="v")
        tsg = E.do("scalar", lambda e, pp=pp: e.activation(
            out=pa_sg2[pp][:, :], in_=pa_rg2[pp][:, :], func=AF.Sigmoid,
            scale=scol[:, 5:6]), waits=[t_g_rd[pp]], inc="act")
        E.do("vector", lambda e, pp=pp: e.tensor_scalar(
            out=pa_q2[:, :], in0=pa_sg2[pp][:, :], scalar1=65536.0,
            scalar2=65535.0, op0=ALU.mult, op1=ALU.min),
            waits=[tsg] + ([E.tok("q_gpsimd")] if rt == 0 else []), inc="v")
        E.do("vector", lambda e, pp=pp: e.tensor_tensor(
            out=pa_m32[pp][:, :], in0=pa_rg2[pp][:, :], in1=pa_q2[:, :],
            op=ALU.mult), waits=[t_m3st[pp]] if t_m3st[pp] else (), inc="v")
        tva = E.tok("v")
        E.do("vector", lambda e, rt=rt, pp=pp: e.tensor_reduce(
            out=sq_a[:, rt:rt + 1], in_=pa_m32[pp][:, :], axis=AX.X, op=ALU.max,
            apply_absolute_value=True), inc="v")
        t_m3st[pp] = E.dma("gpsimd", m3_hbm[rt * 128:(rt + 1) * 128, :],
                           pa_m32[pp][:, :], waits=[tva])
    E.do("vector", lambda e: e.tensor_reduce(
        out=sq_b[:, 0:1], in_=sq_a[:, 0:24], axis=AX.X, op=ALU.max), inc="v")
    tt1 = E.do("tensor", lambda e: e.transpose(
        pB[0:1, 0:128], sq_b[:, 0:1], ident[:, :]),
        waits=[E.tok("v"), E.tok("act")], inc="mm")
    tt2 = E.do("scalar", lambda e: e.activation(
        out=brow_s[0:1, 0:128], in_=pB[0:1, 0:128], func=AF.Copy),
        waits=[tt1], inc="act")
    E.do("vector", lambda e: e.tensor_reduce(
        out=sc(10), in_=brow_s[0:1, 0:128], axis=AX.X, op=ALU.max),
        waits=[tt2], inc="v")
    tq4 = E.dma("gpsimd", ar3_in[:, :], sc(10), waits=[E.tok("v")])
    tar3 = E.do("gpsimd", lambda e: e.collective_compute(
        "AllReduce", ALU.max, replica_groups=[list(range(8))],
        ins=[ar3_in.ap().opt()], outs=[ar3_out.ap().opt()]),
        waits=[tq4], inc="cc")

    tld = E.dma("gpsimd", sc(10), ar3_out[:, :], waits=[tar3])
    E.do("vector", lambda e: e.engine_nop(), waits=[tld], inc="v")
    bcast_col(sc(10), 21)
    E.do("vector", lambda e: e.engine_nop(), waits=[E.tok("act")], inc="v")
    vch(lambda e: e.tensor_tensor(
        out=scol[:, 22:23], in0=scol[:, 21:22], in1=scol[:, 5:6], op=ALU.mult))
    vch(lambda e: e.tensor_scalar(
        out=scol[:, 22:23], in0=scol[:, 22:23], scalar1=1.0 / 65536.0,
        scalar2=None, op0=ALU.mult))
    vch(lambda e: e.tensor_scalar(
        out=scol[:, 23:24], in0=scol[:, 22:23], scalar1=1.0 / 32767.0,
        scalar2=1e-12, op0=ALU.mult, op1=ALU.max))
    vch(lambda e: e.reciprocal(out=scol[:, 24:25], in_=scol[:, 23:24]))
    vch(lambda e: e.tensor_scalar(
        out=scol[:, 7:8], in0=scol[:, 5:6], scalar1=1.0 / 65536.0,
        scalar2=None, op0=ALU.mult))
    vch(lambda e: e.tensor_tensor(
        out=scol[:, 7:8], in0=scol[:, 7:8], in1=scol[:, 24:25], op=ALU.mult))
    t_c3 = E.tok("v")

    t_m4st = [t_m3st[0], t_m3st[1]]
    t_b_rd = [None, None]
    t_q_rd = [None, None]
    for rt in range(24):
        pp = rt % 2
        tw3 = E.dma("sync", pa_sg2[pp][:, :], m3_hbm[rt * 128:(rt + 1) * 128, :],
                    waits=([E.tok("v"), E.tok("q_gpsimd")] if rt < 2 else
                           [t_q_rd[pp]]))
        twu = E.dma("sync", pb_u2[pp][:, :], up_hbm[rt * 128:(rt + 1) * 128, :],
                    waits=[t_b_rd[pp]] if t_b_rd[pp] else ())
        t_b_rd[pp] = E.do("vector", lambda e, pp=pp: e.tensor_scalar(
            out=pa_rg2[pp][:, :], in0=pb_u2[pp][:, :], scalar1=scol[:, 3:4],
            scalar2=None, op0=ALU.mult), waits=[twu], inc="v")
        t_q_rd[pp] = E.do("vector", lambda e, pp=pp: e.tensor_scalar(
            out=pa_q2[:, :], in0=pa_sg2[pp][:, :], scalar1=scol[:, 7:8],
            scalar2=None, op0=ALU.mult), waits=[tw3, t_c3], inc="v")
        E.do("vector", lambda e, pp=pp: e.tensor_tensor(
            out=pa_m32[pp][:, :], in0=pa_q2[:, :], in1=pa_rg2[pp][:, :],
            op=ALU.mult), waits=[t_m4st[pp]], inc="v")
        E.do("vector", lambda e, rt=rt, pp=pp: e.tensor_reduce(
            out=sq_a[:, rt:rt + 1], in_=pa_m32[pp][:, :], axis=AX.X, op=ALU.max,
            apply_absolute_value=True), inc="v")
        t_m4st[pp] = E.dma("gpsimd", m4_hbm[rt * 128:(rt + 1) * 128, :],
                           pa_m32[pp][:, :], waits=[E.tok("v")])
    E.do("vector", lambda e: e.tensor_reduce(
        out=sq_b[:, 0:1], in_=sq_a[:, 0:24], axis=AX.X, op=ALU.max), inc="v")
    tt1 = E.do("tensor", lambda e: e.transpose(
        pB[0:1, 0:128], sq_b[:, 0:1], ident[:, :]),
        waits=[E.tok("v"), E.tok("act")], inc="mm")
    tt2 = E.do("scalar", lambda e: e.activation(
        out=brow_s[0:1, 0:128], in_=pB[0:1, 0:128], func=AF.Copy),
        waits=[tt1], inc="act")
    E.do("vector", lambda e: e.tensor_reduce(
        out=sc(14), in_=brow_s[0:1, 0:128], axis=AX.X, op=ALU.max),
        waits=[tt2], inc="v")
    tq5 = E.dma("gpsimd", ar4_in[:, :], sc(14), waits=[E.tok("v")])
    tar4 = E.do("gpsimd", lambda e: e.collective_compute(
        "AllReduce", ALU.max, replica_groups=[list(range(8))],
        ins=[ar4_in.ap().opt()], outs=[ar4_out.ap().opt()]),
        waits=[tq5], inc="cc")

    sTd = ar.alias(0, [32, 3 * 1024], F32, "sTd")
    rTd_c = [ar.alias(19072, [32, 1024], F32, "rTd0"),
             ar.alias(32896, [32, 1024], F32, "rTd1"),
             ar.alias(36992, [32, 1024], F32, "rTd2")]
    stg_d2a = ar.alias(24576, [128, 2048], F32, "stg_d2a")
    stg_d2b = ar.alias(40960, [128, 1024], F32, "stg_d2b")
    stg_d = [(stg_f2[:, 0:2048], stg_w[:, 0:1024]),
             (stg_d2a[:, :], stg_d2b[:, :])]
    t_redd = [None, None]
    t_ttd = [None, None]
    for rt in range(NDT):
        pp = rt % 2
        sA, sB = stg_d[pp]
        tw = E.dma("sync", sA, l1_down_rm[rt * 128:(rt + 1) * 128, 0:2048],
                   waits=[E.tok("v"), E.tok("mm")] if rt < 2 else [t_redd[pp]])
        tw2b = E.dma("sync", sB, l1_down_rm[rt * 128:(rt + 1) * 128, 2048:3072])
        E.do("vector", lambda e, pp=pp, sA=sA: e.tensor_reduce(
            out=sq_a[:, pp * 96:pp * 96 + 64],
            in_=sA.rearrange("p (a b) -> p a b", b=32),
            axis=AX.X, op=ALU.max, apply_absolute_value=True),
            waits=[tw], inc="v")
        t_redd[pp] = E.do("vector", lambda e, pp=pp, sB=sB: e.tensor_reduce(
            out=sq_a[:, pp * 96 + 64:pp * 96 + 96],
            in_=sB.rearrange("p (a b) -> p a b", b=32),
            axis=AX.X, op=ALU.max, apply_absolute_value=True),
            waits=[tw2b], inc="v")
        E.do("vector", lambda e, pp=pp: e.tensor_scalar(
            out=sq_a[:, 192 + pp * 96:192 + (pp + 1) * 96],
            in0=sq_a[:, pp * 96:(pp + 1) * 96], scalar1=1.0 / 7.0,
            scalar2=1e-12, op0=ALU.mult, op1=ALU.max),
            waits=[E.tok("v")] + ([t_ttd[pp]] if t_ttd[pp] else []), inc="v")
        tv = E.tok("v")
        t_ttd[pp] = tt = E.do("tensor", lambda e, pp=pp: e.transpose(
            pB[0:96, 0:128], sq_a[:, 192 + pp * 96:192 + (pp + 1) * 96],
            ident[:, :]), waits=[tv, E.tok("act")], inc="mm")
        for g3 in range(3):
            E.do("scalar", lambda e, rt=rt, g3=g3: e.activation(
                out=sTd[0:32, g3 * 1024 + rt * 128: g3 * 1024 + (rt + 1) * 128],
                in_=pB[g3 * 32:(g3 + 1) * 32, 0:128], func=AF.Copy),
                waits=[tt], inc="act")
    tse = E.tok("act")
    for g3 in range(3):
        E.do("vector", lambda e, g3=g3: e.reciprocal(
            out=rTd_c[g3][:, :], in_=sTd[:, g3 * 1024:(g3 + 1) * 1024]),
            waits=[tse] if g3 == 0 else (), inc="v")
    t_sc_d = E.tok("v")

    tld = E.dma("gpsimd", sc(14), ar4_out[:, :], waits=[tar4])
    E.do("vector", lambda e: e.engine_nop(), waits=[tld], inc="v")
    bcast_col(sc(14), 25)
    E.do("vector", lambda e: e.engine_nop(), waits=[E.tok("act")], inc="v")
    vch(lambda e: e.tensor_tensor(
        out=scol[:, 26:27], in0=scol[:, 23:24], in1=scol[:, 20:21], op=ALU.mult))
    vch(lambda e: e.tensor_tensor(
        out=scol[:, 27:28], in0=scol[:, 26:27], in1=scol[:, 25:26], op=ALU.mult))
    vch(lambda e: e.tensor_scalar(
        out=scol[:, 10:11], in0=scol[:, 27:28], scalar1=1.0 / 32767.0,
        scalar2=1e-12, op0=ALU.mult, op1=ALU.max))
    vch(lambda e: e.reciprocal(out=scol[:, 9:10], in_=scol[:, 10:11]))
    vch(lambda e: e.tensor_tensor(
        out=scol[:, 9:10], in0=scol[:, 9:10], in1=scol[:, 26:27], op=ALU.mult))
    t_c4 = E.tok("v")
    t_s4 = t_c4

    if dump == "l1sc":
        E.do("gpsimd", lambda e: e.nop(), waits=[E.tok("v"), E.tok("act")])
        E.dma("gpsimd", dbg_out[0:1, 0:32], scol[0:1, 0:32])
        E.dma("gpsimd", dbg_out[0:1, 4096:6144], m3_hbm[0:1, :])
        E.dma("gpsimd", dbg_out[0:1, 6144:8192].bitcast(I32) if False else dbg_out[1:2, 4096:6144], r2_hbm[0:1, :])
        E.dma("gpsimd", dbg_out[2:3, 4096:6144], m4_hbm[0:1, :])
        E.do("gpsimd", lambda e: e.nop(), waits=[E.tok("q_gpsimd")])

    o_bc4 = ar.alias(59 * 1024, [128, 4 * 2048], BF16, "o_bc4")
    wd_b = ar.alias(50 * 1024, [128, 4 * 1024], BF16, "wd_b")
    stg_wC = ar.alias(12 * 1024 + 512, [128, 1024], F32, "stg_wC")
    stg_qC = ar.alias(16 * 1024 + 640, [128, 512], I32, "stg_qC")
    t_mmC = None
    for oc in range(6):
        for itl in range(4):
            rt = oc * 4 + itl
            tw = E.dma("sync", pa_m3[:, :], m4_hbm[rt * 128:(rt + 1) * 128, :],
                       waits=[E.tok("v"), E.tok("mm"), E.tok("q_gpsimd")]
                       if oc == 0 and itl == 0 else [E.tok("v")])
            E.do("vector", lambda e: e.tensor_scalar(
                out=stg_q2[:, :], in0=pa_m3[:, :], scalar1=scol[:, 9:10],
                scalar2=None, op0=ALU.mult), waits=[tw, t_c4, t_s4], inc="v")
            E.do("vector", lambda e, itl=itl: e.tensor_copy(
                out=o_bc4[:, itl * 2048:(itl + 1) * 2048], in_=stg_q2[:, :]),
                waits=[t_mmC] if (t_mmC and itl == 0) else (), inc="v")
        for itl in range(4):
            it = oc * 4 + itl
            for hc in range(2):
                tw = E.dma("sync", stg_wC[:, 0:512],
                           l1_downT[it * 128:(it + 1) * 128,
                                    hc * 512:(hc + 1) * 512],
                           waits=[E.tok("v")])
                rreg = pB[:, 1024:1536]
                sreg2 = pB[:, 1536:2048]
                tm1 = E.do("tensor", lambda e, it=it, hc=hc, rreg=rreg:
                           e.matmul(rreg, Esel[:, (it % 8) * 128:(it % 8 + 1) * 128],
                                    rTd_c[it // 8][:, hc * 512:(hc + 1) * 512],
                                    start=True, stop=True),
                           waits=[t_sc_d, E.tok("v"), E.tok("act")], inc="mm")
                tm2 = E.do("tensor", lambda e, it=it, hc=hc, sreg2=sreg2:
                           e.matmul(sreg2, Esel[:, (it % 8) * 128:(it % 8 + 1) * 128],
                                    sTd[:, (it // 8) * 1024 + hc * 512:
                                        (it // 8) * 1024 + (hc + 1) * 512],
                                    start=True, stop=True), inc="mm")
                E.do("vector", lambda e, rreg=rreg: e.tensor_tensor(
                    out=stg_wC[:, 512:1024], in0=stg_wC[:, 0:512], in1=rreg,
                    op=ALU.mult), waits=[tm1, tw], inc="v")
                E.do("vector", lambda e: e.tensor_copy(
                    out=stg_qC[:, :], in_=stg_wC[:, 512:1024]), inc="v")
                tse2 = E.do("scalar", lambda e, sreg2=sreg2: e.activation(
                    out=stg_wC[:, 512:1024], in_=sreg2, func=AF.Copy),
                    waits=[tm2, E.tok("v")], inc="act")
                E.do("vector", lambda e, itl=itl, hc=hc: e.tensor_tensor(
                    out=wd_b[:, itl * 1024 + hc * 512:itl * 1024 + (hc + 1) * 512],
                    in0=stg_qC[:, :], in1=stg_wC[:, 512:1024], op=ALU.mult),
                    waits=[tse2], inc="v")
        t_wd = E.tok("v")
        for ci, tc in enumerate(TC):
            for d in range(NDT):
                zreg = pA[:, 512 + (d % 2) * 512:1024 + (d % 2) * 512] if False \
                    else pA[:, (d % 2) * 512:(d % 2) * 512 + 512]
                for itl in range(4):
                    E.do("tensor", lambda e, d=d, itl=itl, tc=tc, zreg=zreg:
                         e.matmul(zreg,
                                  wd_b[:, itl * 1024 + d * 128:
                                       itl * 1024 + (d + 1) * 128],
                                  o_bc4[:, itl * 2048 + tc.start:
                                        itl * 2048 + tc.stop],
                                  start=(itl == 0), stop=(itl == 3)),
                         waits=[t_wd, E.tok("v")] if itl == 0 else (), inc="mm")
                E.do("vector", lambda e, d=d, tc=tc, zreg=zreg:
                     e.scalar_tensor_tensor(
                         out=xsl(d, tc), in0=zreg, scalar=scol[:, 10:11],
                         in1=xsl(d, tc), op0=ALU.mult, op1=ALU.add),
                     waits=[E.tok("mm")], inc="v")
        t_mmC = E.tok("mm")
    E.flush()
    if dump == "l1ffn":
        for d in range(NDT):
            E.dma("gpsimd", dbg_out[:, d * 2048:(d + 1) * 2048], xsl(d))
        E.do("gpsimd", lambda e: e.nop(), waits=[E.tok("q_gpsimd")])
    if upto == "l1ffn":
        _finish(E, nc, y_out, xsl)
        return nc

    for d in range(NDT):
        E.dma("gpsimd", x_spill[d * 128:(d + 1) * 128, :], xsl(d),
              waits=[E.tok("v"), E.tok("mm")] if d == 0 else ())
    t_spill = E.tok("q_gpsimd")

    ar.reset(0)
    qn_s = ar.alloc([128, T_LOC], BF16, "qn_s")
    kn_s = ar.alloc([128, T_FULL], BF16, "kn_s")
    P2q = ar.alloc([128, 4 * 1024], BF16, "P2q")
    rstg2 = ar.alloc([128, 2 * 1024], BF16, "rstg2")
    sqv8 = ar.alloc([128, 512], F32, "sqv8")
    fsc = ar.alloc([2, 512], F32, "fsc")
    fsc2 = ar.alloc([2, 512], F32, "fsc2")
    fex = ar.alloc([128, 512], F32, "fex")
    on_st = ar.alloc([128, 512], BF16, "on_st")
    vn_b = nc.alloc_sbuf_tensor_at("vn_b", [128, 32 * 1024], BF16,
                                   offset=nc.lookup_mloc(xs).addr)

    for r in range(2):
        for vt in range(16):
            E.dma("sync",
                  vn_b[:, (r * 16 + vt) * 1024:(r * 16 + vt + 1) * 1024]
                  .bitcast(I32),
                  ag_vn_out[r * 2048 + vt * 128: r * 2048 + (vt + 1) * 128, :],
                  waits=[tag_v, t_spill] if (r == 0 and vt == 0) else ())
    t_kvld = E.tok("q_sync")
    E.do("vector", lambda e: e.engine_nop(), waits=[t_kvld], inc="v")

    QC4 = [slice(i * 512, (i + 1) * 512) for i in range(4)]
    relu_toks8 = [None, None]
    rstg_toks = [None, None]
    pv_toks = [None, None, None, None]
    for pr in range(8):
        slab = pr // 2
        ha = slab * 4 + (pr % 2) * 2
        sa = 32 * (ha % 4)
        sb = sa + 32
        if pr % 2 == 0:
            E.dma("sync", qn_s[:, :], qn_hbm[slab * 128:(slab + 1) * 128, :],
                  waits=[E.tok("mm"), E.tok("v")])
            for r in range(2):
                E.dma("sync",
                      kn_s[:, r * T_LOC:(r + 1) * T_LOC].bitcast(I32),
                      ag_kn_out[r * 512 + slab * 128:
                                r * 512 + (slab + 1) * 128, :],
                      waits=[tag_k] if (pr == 0 and r == 0) else ())
            t_ld8v = E.tok("q_sync")
            E.do("vector", lambda e: e.engine_nop(), waits=[t_ld8v], inc="v")
        t_ld8 = E.tok("v")
        for ci, qc in enumerate(QC4):
            ov = pB[:, 512:1024]
            sq_toks = [None] * 4

            def emit_pv(j, ha=ha, ov=ov, sq_toks=sq_toks):
                sl = j % 4
                for hj in range(2):
                    pv_toks[sl] = E.do(
                        "tensor", lambda e, j=j, hj=hj, sl=sl, ha=ha, ov=ov:
                        e.matmul(
                            ov[hj * 64:(hj + 1) * 64, :],
                            vn_b[:, j * 1024 + (ha + hj) * 64:
                                 j * 1024 + (ha + hj + 1) * 64],
                            P2q[:, sl * 1024 + hj * 512:
                                sl * 1024 + (hj + 1) * 512],
                            start=(j == 0), stop=(j == 31),
                            tile_position=(0, hj * 64)),
                        waits=[sq_toks[sl]] if hj == 0 else (), inc="mm")

            for kt in range(32):
                par = kt % 2
                sl = kt % 4
                for hj, sp in ((0, sa), (1, sb)):
                    reg = pA[:, par * 1024 + hj * 512:
                             par * 1024 + (hj + 1) * 512]
                    E.do("tensor", lambda e, kt=kt, sp=sp, qc=qc, reg=reg:
                         e.matmul(
                             reg,
                             kn_s[sp:sp + 32, kt * 128:(kt + 1) * 128],
                             qn_s[sp:sp + 32, qc],
                             start=True, stop=True, tile_position=(sp, 0)),
                         waits=(([t_ld8] if kt == 0 else []) +
                                ([relu_toks8[par]] if relu_toks8[par] else []))
                         if hj == 0 else (), inc="mm")
                tmm = E.tok("mm")
                relu_toks8[par] = tr = E.do(
                    "scalar", lambda e, par=par: e.activation(
                        out=rstg2[:, par * 1024:(par + 1) * 1024],
                        in_=pA[:, par * 1024:(par + 1) * 1024], func=AF.Relu),
                    waits=[tmm] + ([rstg_toks[par]] if rstg_toks[par] else []),
                    inc="act")
                E.do("vector", lambda e, par=par, sl=sl: e.tensor_tensor(
                    out=P2q[:, sl * 1024:sl * 1024 + 512],
                    in0=rstg2[:, par * 1024:par * 1024 + 512],
                    in1=rstg2[:, par * 1024:par * 1024 + 512], op=ALU.mult),
                    waits=[tr] + ([pv_toks[sl]] if pv_toks[sl] else []),
                    inc="v")
                rstg_toks[par] = sq_toks[sl] = E.do(
                    "vector", lambda e, par=par, sl=sl: e.tensor_tensor(
                        out=P2q[:, sl * 1024 + 512:(sl + 1) * 1024],
                        in0=rstg2[:, par * 1024 + 512:(par + 1) * 1024],
                        in1=rstg2[:, par * 1024 + 512:(par + 1) * 1024],
                        op=ALU.mult), inc="v")
                if kt >= 2:
                    emit_pv(kt - 2)
            emit_pv(30)
            emit_pv(31)
            t_pv = E.tok("mm")
            tsq = E.do("scalar", lambda e, ov=ov: e.activation(
                out=sqv8[:, :], in_=ov, func=AF.Square), waits=[t_pv], inc="act")
            tnm = E.do("tensor", lambda e: e.matmul(
                pB[0:2, 0:512], E64[:, :], sqv8[:, :], start=True, stop=True),
                waits=[tsq, E.tok("v")], inc="mm")
            E.do("scalar", lambda e: e.activation(
                out=fsc[:, :], in_=pB[0:2, 0:512], func=AF.Sqrt),
                waits=[tnm], inc="act")
            E.do("vector", lambda e: e.engine_nop(), waits=[E.tok("act")], inc="v")
            vch(lambda e: e.tensor_scalar_max(
                out=fsc2[:, :], in0=fsc[:, :], scalar1=1e-12))
            vch(lambda e: e.reciprocal(out=fsc2[:, :], in_=fsc2[:, :]))
            tth = E.do("scalar", lambda e: e.activation(
                out=fsc[:, :], in_=fsc[:, :], func=AF.Tanh),
                waits=[E.tok("v")], inc="act")
            E.do("vector", lambda e: e.engine_nop(), waits=[tth], inc="v")
            vch(lambda e: e.tensor_tensor(
                out=fsc[:, :], in0=fsc[:, :], in1=fsc2[:, :], op=ALU.mult))
            tf = E.tok("v")
            tfe = E.do("tensor", lambda e: e.matmul(
                pB[:, 1024:1536], E2[:, :], fsc[:, :], start=True, stop=True),
                waits=[tf, E.tok("act")], inc="mm")
            tfc = E.do("scalar", lambda e: e.activation(
                out=fex[:, :], in_=pB[:, 1024:1536], func=AF.Copy),
                waits=[tfe], inc="act")
            ton = E.do("vector", lambda e, ov=ov: e.tensor_tensor(
                out=on_st[:, :], in0=ov, in1=fex[:, :], op=ALU.mult),
                waits=[tfc], inc="v")
            E.dma("gpsimd",
                  outn_hbm[pr * 128:(pr + 1) * 128, ci * 512:(ci + 1) * 512],
                  on_st[:, :], waits=[ton])
            E.do("vector", lambda e: e.engine_nop(),
                 waits=[E.tok("q_gpsimd")], inc="v")
    E.flush()

    for d in range(NDT):
        E.dma("sync", xsl(d), x_spill[d * 128:(d + 1) * 128, :],
              waits=[E.tok("q_gpsimd"), E.tok("mm"), E.tok("v")]
              if d == 0 else ())
    t_xre = E.tok("q_sync")
    wout_b = ar.alloc([128, 8 * 1024], BF16, "wout_b")
    onld = ar.alloc([128, 8 * 512], BF16, "onld")
    wstg8 = ar.alias(0, [128, 1024], F32, "wstg8")
    onld2 = [onld, ar.alias(4096, [128, 8 * 512], BF16, "onld_b")]
    for kt in range(NDT):
        tw = E.dma("sync", wstg8[:, 0:1024],
                   l1_wout[kt * 128:(kt + 1) * 128, :], waits=[E.tok("v")])
        E.do("vector", lambda e, kt=kt: e.tensor_copy(
            out=wout_b[:, kt * 1024:(kt + 1) * 1024], in_=wstg8[:, 0:1024]),
            waits=[tw], inc="v")
    t_wo8 = E.tok("v")
    t_mmw = [None, None]
    for ci, tc in enumerate(TC):
        ob = onld2[ci % 2]
        for kt in range(NDT):
            E.dma("sync", ob[:, kt * 512:(kt + 1) * 512],
                  outn_hbm[kt * 128:(kt + 1) * 128, tc],
                  waits=([E.tok("v")] +
                         ([t_mmw[ci % 2]] if t_mmw[ci % 2] else [E.tok("mm")]))
                  if kt == 0 else ())
        t_on8 = E.tok("q_sync")
        for d in range(NDT):
            reg = pB[:, 1024 + (d % 2) * 512:1536 + (d % 2) * 512]
            for kt in range(NDT):
                E.do("tensor", lambda e, d=d, kt=kt, reg=reg, ob=ob: e.matmul(
                    reg, wout_b[:, kt * 1024 + d * 128: kt * 1024 + (d + 1) * 128],
                    ob[:, kt * 512:(kt + 1) * 512],
                    start=(kt == 0), stop=(kt == NDT - 1)),
                    waits=[t_on8, t_wo8, t_xre, E.tok("v")]
                    if kt == 0 else (), inc="mm")
            E.do("vector", lambda e, d=d, tc=tc, reg=reg: e.tensor_add(
                out=xsl(d, tc), in0=xsl(d, tc), in1=reg),
                waits=[E.tok("mm")], inc="v")
        t_mmw[ci % 2] = E.tok("mm")
    E.flush()

    _finish(E, nc, y_out, xsl)
    return nc

def _finish(E, nc, y_out, xsl):
    for d in range(NDT):
        E.dma("gpsimd", y_out[d * 128:(d + 1) * 128, :], xsl(d),
              waits=[E.tok("v"), E.tok("act"), E.tok("mm")] if d == 0 else ())
    tst = E.tok("q_gpsimd")
    E.do("gpsimd", lambda e: e.nop(), waits=[tst])
    E.flush()


_CACHE = {}


def _consts():
    ident = np.eye(128, dtype=np.float32)
    Ekq = np.zeros((128, 4), np.float32)
    for p in range(128):
        if p % 32 < 16:
            Ekq[p, p // 32] = 1.0
    Eexp = np.zeros((4, 128), np.float32)
    for p in range(128):
        Eexp[p // 32, p] = 1.0
    E64 = np.zeros((128, 2), np.float32)
    E64[np.arange(128), np.arange(128) // 64] = 1.0
    E2 = np.zeros((2, 128), np.float32)
    E2[np.arange(128) // 64, np.arange(128)] = 1.0
    Esel = np.zeros((32, 1024), np.float32)
    for it in range(8):
        for p in range(128):
            Esel[4 * it + p // 32, it * 128 + p] = 1.0
    return ident, Ekq, Eexp, E64, E2, Esel


def _colmajor(v):
    return np.ascontiguousarray(v.reshape(-1, 128).T)


def kernel(L0_attn_x, L0_attn_Wq, L0_attn_Wkv_down, L0_attn_Wk_up, L0_attn_Wv_up,
           L0_attn_Wo, L0_ffn_ln_gamma, L0_ffn_ln_beta, L0_ffn_fc1_q,
           L0_ffn_fc1_scales, L0_ffn_fc1_bias, L0_ffn_fc2_q, L0_ffn_fc2_scales,
           L0_ffn_fc2_bias, L1_attn_Wq, L1_attn_Wkv, L1_attn_Wout,
           L1_ffn_w_gate, L1_ffn_w_up, L1_ffn_w_down, _dump=None, _upto=None):
    f32 = np.float32
    asc = np.ascontiguousarray

    ident, Ekq, Eexp, E64, E2, Esel = _consts()
    shared = {
        "l0_wq": asc(np.asarray(L0_attn_Wq, f32)),
        "l0_wkvdT": asc(np.asarray(L0_attn_Wkv_down, f32).T),
        "l0_wku": asc(np.asarray(L0_attn_Wk_up, f32)),
        "l0_wvuT": asc(np.asarray(L0_attn_Wv_up, f32).T),
        "l0_woT": asc(np.asarray(L0_attn_Wo, f32).T),
        "l0_fc1q": asc(np.asarray(L0_ffn_fc1_q, np.int32)),
        "l0_fc2q": asc(np.asarray(L0_ffn_fc2_q, np.int32)),
        "l0_fc1s": asc(np.asarray(L0_ffn_fc1_scales, f32)),
        "l0_fc2s": asc(np.asarray(L0_ffn_fc2_scales, f32)),
        "ln0_g": _colmajor(np.asarray(L0_ffn_ln_gamma, f32)),
        "ln0_b": _colmajor(np.asarray(L0_ffn_ln_beta, f32)),
        "fc1b_c": _colmajor(np.asarray(L0_ffn_fc1_bias, f32)),
        "fc2b_c": _colmajor(np.asarray(L0_ffn_fc2_bias, f32)),
        "l1_wout": asc(np.asarray(L1_attn_Wout, f32)),
        "l1_upT": asc(np.asarray(L1_ffn_w_up, f32).T),
        "l1_gateT": asc(np.asarray(L1_ffn_w_gate, f32).T),
        "l1_downT": asc(np.asarray(L1_ffn_w_down, f32).T),
        "l1_up_rm": asc(np.asarray(L1_ffn_w_up, f32)),
        "l1_gate_rm": asc(np.asarray(L1_ffn_w_gate, f32)),
        "l1_down_rm": asc(np.asarray(L1_ffn_w_down, f32)),
        "c_ident": ident, "c_Ekq": Ekq, "c_Eexp": Eexp, "c_E64": E64,
        "c_E2": E2, "c_Esel": Esel,
    }
    Wq1 = np.asarray(L1_attn_Wq, f32)
    Wkv1 = np.asarray(L1_attn_Wkv, f32)
    wqp = np.zeros((D, 512), f32)
    wkp = np.zeros((D, 512), f32)
    wv = np.zeros((D, 1024), f32)
    for h in range(16):
        wqp[:, h * 32:h * 32 + 16] = Wq1[:, h * 16:(h + 1) * 16]
        wkp[:, h * 32:h * 32 + 16] = Wkv1[:, h * 80:h * 80 + 16]
        wv[:, h * 64:(h + 1) * 64] = Wkv1[:, h * 80 + 16:(h + 1) * 80]
    shared["l1_wqp"] = wqp
    shared["l1_wkp"] = wkp
    shared["l1_wv"] = wv

    x = np.asarray(L0_attn_x, f32)
    in_maps = []
    for c in range(8):
        b, half = c // 2, c % 2
        m = dict(shared)
        m["x_in"] = asc(x[b, half * T_LOC:(half + 1) * T_LOC, :].T)
        in_maps.append(m)

    key = (_dump, _upto)
    if key not in _CACHE:
        _CACHE[key] = build_program(dump=_dump, upto=_upto)
    res = run_bass_kernel_spmd(_CACHE[key], in_maps, core_ids=list(range(8)))

    out = np.empty((4, T_FULL, D), f32)
    for c in range(8):
        b, half = c // 2, c % 2
        out[b, half * T_LOC:(half + 1) * T_LOC, :] = res.results[c]["y_out"].T
    if _dump:
        return out, [res.results[c].get("dbg_out") for c in range(8)]
    return out


# version 15
# speedup vs baseline: 1.4496x; 1.0050x over previous
import numpy as np
import concourse.bass as bass
import concourse.mybir as mybir
from concourse.bass_utils import run_bass_kernel_spmd

F32 = mybir.dt.float32
BF16 = mybir.dt.bfloat16
I32 = mybir.dt.int32
AF = mybir.ActivationFunctionType
ALU = mybir.AluOpType
AX = mybir.AxisListType

T_LOC = 2048
T_FULL = 4096
D = 1024
NDT = D // 128
EPS_LN = 1e-5
ENGINES = ("sync", "tensor", "vector", "scalar", "gpsimd")
TC = [slice(i * 512, (i + 1) * 512) for i in range(4)]
KC8 = [slice(i * 512, (i + 1) * 512) for i in range(8)]


class Emitter:

    def __init__(self, nc):
        self.nc = nc
        self.q = {e: [] for e in ENGINES}
        self.sems = {}
        self.cnt = {}
        self._ctx = []

    def sem(self, name):
        if name not in self.sems:
            cm = self.nc.semaphore(name)
            self.sems[name] = cm.__enter__()
            self._ctx.append(cm)
            self.cnt[name] = 0
        return self.sems[name]

    def do(self, engine, fn, waits=(), inc=None, inck=1):
        tok = None
        if inc is not None:
            self.sem(inc)
            self.cnt[inc] += inck
            tok = (inc, self.cnt[inc])
        self.q[engine].append((fn, tuple(w for w in waits if w), inc, inck))
        return tok

    def dma(self, engine, out, in_, waits=()):
        qn = "q_" + engine
        return self.do(engine, lambda e, o=out, i=in_: e.dma_start(out=o, in_=i),
                       waits=waits, inc=qn, inck=16)

    def tok(self, semname):
        return (semname, self.cnt.get(semname, 0))

    def guard(self):
        return (self.tok("v"), self.tok("act"), self.tok("gv"))

    def flush(self):
        nc, q = self.nc, self.q
        self.q = {e: [] for e in ENGINES}
        sems = self.sems
        used = [e for e in ENGINES if q[e]]
        if not used:
            return

        def runner(ename):
            def run(eng):
                seen = {}
                for fn, waits, inc, inck in q[ename]:
                    for sname, val in waits:
                        if val > seen.get(sname, 0):
                            eng.wait_ge(sems[sname], val)
                            seen[sname] = val
                    ins = fn(eng)
                    if inc is not None:
                        ins.then_inc(sems[inc], inck)
            return run

        with nc.Block() as block:
            for e in used:
                getattr(block, e)(runner(e))


class Arena:
    def __init__(self, nc, name, nbytes):
        self.nc = nc
        self.backing = nc.alloc_sbuf_tensor(name, [128, nbytes], mybir.dt.uint8)
        self.base = nc.lookup_mloc(self.backing).addr
        self.off = 0
        self.nbytes = nbytes
        self.n = 0

    def alloc(self, shape, dtype, name=None):
        per_part = int(np.prod(shape[1:])) * mybir.dt.size(dtype)
        off = (self.off + 31) & ~31
        assert off + per_part <= self.nbytes, \
            f"arena overflow: {off}+{per_part} > {self.nbytes} ({name})"
        self.off = off + per_part
        self.n += 1
        return self.nc.alloc_sbuf_tensor_at(name or f"ar{self.n}", shape, dtype,
                                            offset=self.base + off)

    def mark(self):
        return self.off

    def reset(self, mark=0):
        self.off = mark

    def alias(self, at, shape, dtype, name):
        per_part = int(np.prod(shape[1:])) * mybir.dt.size(dtype)
        at = (at + 31) & ~31
        assert at + per_part <= self.nbytes
        self.n += 1
        return self.nc.alloc_sbuf_tensor_at(name, shape, dtype,
                                            offset=self.base + at)


def build_program(dump=None, upto=None):
    nc = bass.Bass("TRN2", target_bir_lowering=False)
    E = Emitter(nc)

    DT = nc.dram_tensor
    x_in = DT("x_in", [D, T_LOC], F32, kind="ExternalInput")
    l0_wq = DT("l0_wq", [2048, D], F32, kind="ExternalInput")
    l0_wkvdT = DT("l0_wkvdT", [D, 64], F32, kind="ExternalInput")
    l0_wku = DT("l0_wku", [2048, 64], F32, kind="ExternalInput")
    l0_wvuT = DT("l0_wvuT", [64, 2048], F32, kind="ExternalInput")
    l0_woT = DT("l0_woT", [2048, D], F32, kind="ExternalInput")
    l0_fc1q = DT("l0_fc1q", [D, 4096], I32, kind="ExternalInput")
    l0_fc2q = DT("l0_fc2q", [4096, D], I32, kind="ExternalInput")
    l0_fc1s = DT("l0_fc1s", [8, 4096], F32, kind="ExternalInput")
    l0_fc2s = DT("l0_fc2s", [32, D], F32, kind="ExternalInput")
    ln0_g = DT("ln0_g", [128, 8], F32, kind="ExternalInput")
    ln0_b = DT("ln0_b", [128, 8], F32, kind="ExternalInput")
    fc1b_c = DT("fc1b_c", [128, 32], F32, kind="ExternalInput")
    fc2b_c = DT("fc2b_c", [128, 8], F32, kind="ExternalInput")
    l1_wqp = DT("l1_wqp", [D, 512], F32, kind="ExternalInput")
    l1_wkp = DT("l1_wkp", [D, 512], F32, kind="ExternalInput")
    l1_wv = DT("l1_wv", [D, 1024], F32, kind="ExternalInput")
    l1_wout = DT("l1_wout", [1024, D], F32, kind="ExternalInput")
    l1_upT = DT("l1_upT", [D, 3072], F32, kind="ExternalInput")
    l1_gateT = DT("l1_gateT", [D, 3072], F32, kind="ExternalInput")
    l1_downT = DT("l1_downT", [3072, D], F32, kind="ExternalInput")
    l1_up_rm = DT("l1_up_rm", [3072, D], F32, kind="ExternalInput")
    l1_gate_rm = DT("l1_gate_rm", [3072, D], F32, kind="ExternalInput")
    l1_down_rm = DT("l1_down_rm", [D, 3072], F32, kind="ExternalInput")
    c_ident = DT("c_ident", [128, 128], F32, kind="ExternalInput")
    c_Ekq = DT("c_Ekq", [128, 4], F32, kind="ExternalInput")
    c_Eexp = DT("c_Eexp", [4, 128], F32, kind="ExternalInput")
    c_E64 = DT("c_E64", [128, 2], F32, kind="ExternalInput")
    c_E2 = DT("c_E2", [2, 128], F32, kind="ExternalInput")
    c_Esel = DT("c_Esel", [32, 1024], F32, kind="ExternalInput")

    y_out = DT("y_out", [D, T_LOC], F32, kind="ExternalOutput")
    dbg_out = DT("dbg_out", [128, 16384], F32, kind="ExternalOutput") if dump else None

    ag0_in = DT("ag0_in", [64, T_LOC // 2], I32)
    ag0_out = DT("ag0_out", [128, T_LOC // 2], I32)
    qn_hbm = DT("qn_hbm", [512, T_LOC], BF16)
    outn_hbm = DT("outn_hbm", [1024, T_LOC], BF16)
    ag_kn_in = DT("ag_kn_in", [512, T_LOC // 2], I32)
    ag_kn_out = DT("ag_kn_out", [1024, T_LOC // 2], I32)
    ag_vn_in = DT("ag_vn_in", [T_LOC, 512], I32)
    ag_vn_out = DT("ag_vn_out", [2 * T_LOC, 512], I32)
    ar2_in = DT("ar2_in", [1, 1], F32)
    ar2_out = DT("ar2_out", [1, 1], F32)
    ar2b_in = DT("ar2b_in", [1, 1], F32)
    ar2b_out = DT("ar2b_out", [1, 1], F32)
    ar1_in = DT("ar1_in", [1, 1], F32)
    ar1_out = DT("ar1_out", [1, 1], F32)
    ar3_in = DT("ar3_in", [1, 1], F32)
    ar3_out = DT("ar3_out", [1, 1], F32)
    ar4_in = DT("ar4_in", [1, 1], F32)
    ar4_out = DT("ar4_out", [1, 1], F32)
    up_hbm = DT("up_hbm", [3072, T_LOC], BF16)
    gate_hbm = DT("gate_hbm", [3072, T_LOC], BF16)
    r2_hbm = DT("r2_hbm", [3072, T_LOC], I32)
    m3_hbm = DT("m3_hbm", [3072, T_LOC], F32)
    m4_hbm = DT("m4_hbm", [3072, T_LOC], F32)
    x_spill = DT("x_spill", [D, T_LOC], F32)

    A = nc.alloc_sbuf_tensor
    xs = A("xs", [128, NDT * T_LOC], F32)
    stat_r = A("stat_r", [1, 2 * T_LOC], F32)
    srw = A("srw", [1, 1536], F32)
    ident = A("ident", [128, 128], F32)
    ident_b = A("ident_b", [128, 128], BF16)
    ones_f = A("ones_f", [128, 128], F32)
    ones_b = A("ones_b", [128, 1], BF16)
    Ekq = A("Ekq", [128, 4], F32)
    Eexp = A("Eexp", [4, 128], F32)
    E64 = A("E64", [128, 2], F32)
    E2 = A("E2", [2, 128], F32)
    Esel = A("Esel", [32, 1024], F32)
    g0col = A("g0col", [128, 8], F32)
    b0col = A("b0col", [128, 8], F32)
    fc1bcol = A("fc1bcol", [128, 32], F32)
    fc2bcol = A("fc2bcol", [128, 8], F32)
    scal = A("scal", [1, 32], F32)
    brow_s = A("brow_s", [1, 1024], F32)
    half_f = A("half_f", [1, 128], F32)
    scol = A("scol", [128, 32], F32)
    kvfull = A("kvfull", [64, T_FULL], BF16)
    qtil_all = A("qtil_all", [128, 4 * T_LOC], BF16)
    sq_a = A("sq_a", [128, 512], F32)
    sq_b = A("sq_b", [128, 512], F32)
    bc_a = A("bc_a", [128, 512], F32)
    bc_b = A("bc_b", [128, 512], F32)

    pA = nc.alloc_psum_tensor("pA", [128, 2048], F32)
    pB = nc.alloc_psum_tensor("pB", [128, 2048], F32)

    rem = (nc.sbuf_bytes_remaining - 512) & ~31
    ar = Arena(nc, "arena", rem)

    def xsl(d, t=None):
        if t is None:
            return xs[:, d * T_LOC:(d + 1) * T_LOC]
        return xs[:, d * T_LOC + t.start: d * T_LOC + t.stop]

    def MU(t=None):
        return stat_r[0:1, t.start:t.stop] if t else stat_r[0:1, 0:T_LOC]

    def M2(t=None):
        return stat_r[0:1, T_LOC + t.start:T_LOC + t.stop] if t else stat_r[0:1, T_LOC:2 * T_LOC]

    def dtap(label, emitfns):
        if dump == label:
            for fn in emitfns:
                fn()
            E.do("gpsimd", lambda e: e.nop(), waits=[E.tok("q_gpsimd")])

    for d in range(NDT):
        E.dma("sync", xsl(d), x_in[d * 128:(d + 1) * 128, :])
    for dst, src in ((ident, c_ident), (Ekq, c_Ekq), (Eexp, c_Eexp),
                     (E64, c_E64), (E2, c_E2), (Esel, c_Esel), (g0col, ln0_g), (b0col, ln0_b),
                     (fc1bcol, fc1b_c), (fc2bcol, fc2b_c)):
        E.dma("sync", dst[:, :], src[:, :])
    t_loads = E.tok("q_sync")
    E.do("vector", lambda e: e.memset(ones_f[:, :], 1.0))
    E.do("vector", lambda e: e.memset(ones_b[:, :], 1.0))
    E.do("vector", lambda e: e.memset(half_f[:, :], 0.5), inc="v")
    E.do("vector", lambda e: e.tensor_scalar_mul(
        out=fc2bcol[:, :], in0=fc2bcol[:, :], scalar1=0.5),
        waits=[t_loads], inc="v")
    E.do("vector", lambda e: e.tensor_copy(out=ident_b[:, :], in_=ident[:, :]),
         inc="v")
    E.do("sync", lambda e: e.nop(), waits=[t_loads])
    E.flush()

    def emit_stats():
        sqs = [sq_a, sq_b]
        for ci, tc in enumerate(TC):
            s1 = pA[0:1, 0:512]
            s2 = pA[0:1, 512:1024]
            for d in range(NDT):
                tsq = E.do("scalar",
                           lambda e, d=d, tc=tc, s=sqs[d % 2]:
                           e.activation(out=s[:, :], in_=xsl(d, tc), func=AF.Square),
                           waits=[E.tok("mm")] if d >= 2 else (), inc="act")
                E.do("tensor",
                     lambda e, d=d, tc=tc, s1=s1: e.matmul(
                         s1, ones_f[:, 0:1], xsl(d, tc),
                         start=(d == 0), stop=(d == NDT - 1)),
                     waits=[E.tok("v")] if d == 0 else (), inc="mm")
                E.do("tensor",
                     lambda e, d=d, s2=s2, s=sqs[d % 2]: e.matmul(
                         s2, ones_f[:, 0:1], s[:, :],
                         start=(d == 0), stop=(d == NDT - 1)),
                     waits=[tsq], inc="mm")
            tmm = E.tok("mm")
            E.do("vector", lambda e, tc=tc, s1=s1: e.tensor_scalar(
                out=MU(tc), in0=s1, scalar1=1.0 / D,
                scalar2=None, op0=ALU.mult), waits=[tmm], inc="v")
            E.do("vector", lambda e, tc=tc, s2=s2: e.tensor_scalar(
                out=M2(tc), in0=s2, scalar1=1.0 / D,
                scalar2=None, op0=ALU.mult), inc="v")
        return E.tok("v")

    def emit_ln_bcast(tc, kind):
        g = E.guard()
        s0 = srw[0:1, 0:512]
        s1 = srw[0:1, 512:1024]
        s2 = srw[0:1, 1024:1536]
        E.do("vector", lambda e: e.tensor_tensor(
            out=s0, in0=MU(tc), in1=MU(tc), op=ALU.mult), waits=g)
        E.do("vector", lambda e: e.tensor_sub(out=s2, in0=M2(tc), in1=s0))
        E.do("vector", lambda e: e.tensor_scalar_add(out=s0, in0=s2, scalar1=EPS_LN),
             inc="v")
        E.do("scalar", lambda e: e.activation(out=s0, in_=s0, func=AF.Sqrt),
             waits=[E.tok("v")], inc="act")
        E.do("vector", lambda e: e.reciprocal(out=s0, in_=s0),
             waits=[E.tok("act")], inc="v")
        E.do("vector", lambda e: e.scalar_tensor_tensor(
            out=s1, in0=MU(tc), scalar=-1.0, in1=s0, op0=ALU.mult, op1=ALU.mult),
            inc="v")
        if kind == "acbc":
            E.do("vector", lambda e: e.tensor_tensor(out=s2, in0=s2, in1=s0, op=ALU.mult))
            E.do("vector", lambda e: e.tensor_tensor(out=s2, in0=s2, in1=s0, op=ALU.mult))
            E.do("vector", lambda e: e.tensor_scalar_add(out=s2, in0=s2, scalar1=EPS_LN),
                 inc="v")
            E.do("scalar", lambda e: e.activation(out=s2, in_=s2, func=AF.Sqrt),
                 waits=[E.tok("v")], inc="act")
            E.do("vector", lambda e: e.reciprocal(out=s2, in_=s2),
                 waits=[E.tok("act")], inc="v")
            E.do("vector", lambda e: e.tensor_tensor(out=s0, in0=s0, in1=s2, op=ALU.mult))
            E.do("vector", lambda e: e.tensor_tensor(out=s1, in0=s1, in1=s2, op=ALU.mult),
                 inc="v")
        tlast = E.tok("v")
        out = None
        for row, dest, reg in ((s0, bc_a, pB[:, 1024:1536]),
                               (s1, bc_b, pB[:, 1536:2048])):
            tm = E.do("tensor", lambda e, row=row, reg=reg: e.matmul(
                reg, ones_f[0:1, :], row, start=True, stop=True),
                waits=g + (tlast,), inc="mm")
            out = E.do("scalar", lambda e, reg=reg, dest=dest: e.activation(
                out=dest[:, :], in_=reg, func=AF.Copy), waits=[tm], inc="act")
        return out

    def vch(fn):
        return E.do("vector", fn, waits=[E.tok("v")], inc="v")

    emit_stats()
    E.flush()
    dtap("stats0", [lambda: E.dma("gpsimd", dbg_out[0:1, 0:4096], stat_r[:, :])])

    base_mark = ar.mark()
    h_b = ar.alloc([128, NDT * T_LOC], BF16, "h_b")
    mark3 = ar.mark()
    xn_b = ar.alloc([128, NDT * T_LOC], BF16, "xn_b")
    wkvd_f = ar.alloc([128, 64], F32, "wkvd_f")
    wkvd_b = ar.alloc([128, NDT * 64], BF16, "wkvd_b")
    m_kv = ar.mark()
    kv_own = ar.alloc([64, T_LOC], BF16, "kv_own")
    m_stg = ar.mark()
    stg_f2 = ar.alloc([128, 2048], F32, "stg_f2")
    mh_b = ar.alias(m_stg, [128, NDT * 64], BF16, "mh_b")
    wqo_b = ar.alias(m_kv, [128, 2 * 1024], BF16, "wqo_b")
    wku_b = ar.alias(m_kv + 4 * 1024 + 32, [128, 2 * 64], BF16, "wku_b")

    def hsl(d, t):
        return h_b[:, d * T_LOC + t.start: d * T_LOC + t.stop]

    def xnsl(d, t):
        return xn_b[:, d * T_LOC + t.start: d * T_LOC + t.stop]

    def qtsl(hh, t):
        p = (hh % 2) * 64
        c = (hh // 2) * T_LOC
        return qtil_all[p:p + 64, c + t.start:c + t.stop]

    for d in range(NDT):
        tl = E.dma("sync", wkvd_f[:, :], l0_wkvdT[d * 128:(d + 1) * 128, :],
                   waits=[E.tok("v")])
        E.do("vector", lambda e, d=d: e.tensor_copy(
            out=wkvd_b[:, d * 64:(d + 1) * 64], in_=wkvd_f[:, :]),
            waits=[tl], inc="v")
    t_wkv = E.tok("v")

    for ci, tc in enumerate(TC):
        tb = emit_ln_bcast(tc, "ab")
        kvp = pA[0:64, 0:512]
        for d in range(NDT):
            E.do("vector", lambda e, d=d, tc=tc: e.tensor_tensor(
                out=sq_a[:, :], in0=xsl(d, tc), in1=bc_a[:, :], op=ALU.mult),
                waits=[tb], inc="v")
            tx = E.do("vector", lambda e, d=d, tc=tc: e.tensor_add(
                out=xnsl(d, tc), in0=sq_a[:, :], in1=bc_b[:, :]), inc="v")
            E.do("tensor", lambda e, d=d, tc=tc, kvp=kvp: e.matmul(
                kvp, wkvd_b[:, d * 64:(d + 1) * 64], xnsl(d, tc),
                start=(d == 0), stop=(d == NDT - 1)),
                waits=[tx, t_wkv] + ([E.tok("act")] if d == 0 else []), inc="mm")
        E.do("scalar", lambda e, tc=tc, kvp=kvp: e.activation(
            out=kv_own[:, tc], in_=kvp, func=AF.Copy),
            waits=[E.tok("mm")], inc="act")
    t_kv = E.tok("act")
    t_xnb = E.tok("v")
    tq = E.dma("gpsimd", ag0_in[:, :], kv_own[:, :].bitcast(I32), waits=[t_kv])
    tcc0 = E.do("gpsimd", lambda e: e.collective_compute(
        "AllGather", ALU.bypass,
        replica_groups=[[0, 1], [2, 3], [4, 5], [6, 7]],
        ins=[ag0_in.ap().opt()], outs=[ag0_out.ap().opt()]),
        waits=[tq], inc="cc")

    for ci, tc in enumerate(TC):
        tb = emit_ln_bcast(tc, "acbc")
        for d in range(NDT):
            E.do("vector", lambda e, d=d, tc=tc: e.tensor_tensor(
                out=sq_a[:, :], in0=xsl(d, tc), in1=bc_a[:, :], op=ALU.mult),
                waits=[tb], inc="v")
            E.do("vector", lambda e: e.tensor_add(
                out=sq_a[:, :], in0=sq_a[:, :], in1=bc_b[:, :]))
            E.do("vector", lambda e, d=d, tc=tc: e.tensor_scalar(
                out=hsl(d, tc), in0=sq_a[:, :], scalar1=g0col[:, d:d + 1],
                scalar2=b0col[:, d:d + 1], op0=ALU.mult, op1=ALU.add), inc="v")
    for ci, tc in enumerate(TC):
        for d in range(NDT):
            E.do("vector", lambda e, d=d, tc=tc: e.tensor_add(
                out=xsl(d, tc), in0=xsl(d, tc), in1=xnsl(d, tc)))
            E.do("vector", lambda e, d=d, tc=tc: e.tensor_scalar_add(
                out=xsl(d, tc), in0=xsl(d, tc), scalar1=fc2bcol[:, d:d + 1]),
                inc="v")
    t_h = E.tok("v")

    for hh in range(8):
        for dh in range(2):
            E.dma("sync", stg_f2[:, dh * 1024:(dh + 1) * 1024],
                  l0_wq[hh * 256 + dh * 128: hh * 256 + (dh + 1) * 128, :],
                  waits=[E.tok("v"), E.tok("mm"), tq] if dh == 0 else ())
        t_ldq = E.tok("q_sync")
        t_wq = E.do("vector", lambda e: e.tensor_copy(
            out=wqo_b[:, :], in_=stg_f2[:, :]), waits=[t_ldq], inc="v")
        for dh in range(2):
            E.dma("sync", stg_f2[:, dh * 64:(dh + 1) * 64],
                  l0_wku[hh * 256 + dh * 128: hh * 256 + (dh + 1) * 128, :],
                  waits=[t_wq] if dh == 0 else ())
        t_ldk = E.tok("q_sync")
        t_wk = E.do("vector", lambda e: e.tensor_copy(
            out=wku_b[:, :], in_=stg_f2[:, 0:128]), waits=[t_ldk], inc="v")
        for d in range(NDT):
            reg = pB[:, (d % 2) * 512:(d % 2) * 512 + 64]
            for dh in range(2):
                E.do("tensor", lambda e, d=d, dh=dh, reg=reg: e.matmul(
                    reg, wqo_b[:, dh * 1024 + d * 128: dh * 1024 + (d + 1) * 128],
                    wku_b[:, dh * 64:(dh + 1) * 64],
                    start=(dh == 0), stop=(dh == 1)),
                    waits=[t_wq, t_wk, E.tok("act")] if dh == 0 else (), inc="mm")
            E.do("scalar", lambda e, d=d, reg=reg: e.activation(
                out=mh_b[:, d * 64:(d + 1) * 64], in_=reg, func=AF.Copy),
                waits=[E.tok("mm")], inc="act")
        t_mh = E.tok("act")
        for ci, tc in enumerate(TC):
            reg = pA[0:64, 1024 + (ci % 2) * 512: 1536 + (ci % 2) * 512]
            for d in range(NDT):
                E.do("tensor", lambda e, d=d, tc=tc, reg=reg: e.matmul(
                    reg, mh_b[:, d * 64:(d + 1) * 64], xnsl(d, tc),
                    start=(d == 0), stop=(d == NDT - 1)),
                    waits=[t_mh, t_xnb, E.tok("act")] if d == 0 else (), inc="mm")
            E.do("scalar", lambda e, hh=hh, tc=tc, reg=reg: e.activation(
                out=qtsl(hh, tc), in_=reg, func=AF.Copy),
                waits=[E.tok("mm")], inc="act")
    E.flush()
    dtap("kv", [lambda: E.dma("gpsimd", dbg_out[0:64, 0:2048],
                              kv_own[:, :].bitcast(I32))])

    ar.reset(mark3)
    w1q = ar.alloc([128, NDT * 1024], BF16, "w1q")
    w2q = ar.alloc([128, 8 * 1024], BF16, "w2q")
    m_stg3 = ar.mark()
    stg_i = ar.alloc([128, 2 * 1024], I32, "stg_i")
    ybuf = ar.alias(m_stg3, [128, 8 * 512], BF16, "ybuf")
    sc_bc = ar.alloc([128, 512], F32, "sc_bc")

    def dequant_tile(dst, src_i, s_hbm, lhs_row, wload):
        tsd = E.dma("sync", brow_s[0:1, :], s_hbm, waits=[E.tok("mm")])
        for half in range(2):
            reg = pB[:, half * 512:(half + 1) * 512]
            hs = slice(half * 512, (half + 1) * 512)
            tm = E.do("tensor", lambda e, reg=reg, lhs_row=lhs_row, hs=hs:
                      e.matmul(reg, lhs_row[0:1, :], brow_s[0:1, hs],
                               start=True, stop=True),
                      waits=[tsd, E.tok("act"), E.tok("v")], inc="mm")
            E.do("scalar", lambda e, reg=reg: e.activation(
                out=sc_bc[:, :], in_=reg, func=AF.Copy),
                waits=[tm], inc="act")
            tqb = E.do("scalar", lambda e, dst=dst, src_i=src_i, hs=hs: e.activation(
                out=dst[:, hs], in_=src_i[:, hs], func=AF.Copy, bias=-128.0),
                waits=[wload], inc="act")
            E.do("vector", lambda e, dst=dst, hs=hs: e.tensor_tensor(
                out=dst[:, hs], in0=dst[:, hs], in1=sc_bc[:, :], op=ALU.mult),
                waits=[E.tok("act")], inc="v")
        return E.tok("v")

    t_w2 = None
    silu_toks = [None, None]
    xadd_toks = [None, None]
    for Q in range(4):
        fofs = Q * 1024
        for w in range(4):
            for dd in range(2):
                d = w * 2 + dd
                E.dma("sync", stg_i[:, dd * 1024:(dd + 1) * 1024],
                      l0_fc1q[d * 128:(d + 1) * 128, fofs:fofs + 1024],
                      waits=[E.tok("v")] if dd == 0 and (w > 0 or t_w2) else ())
            t_ld1 = E.tok("q_sync")
            for dd in range(2):
                d = w * 2 + dd
                dequant_tile(w1q[:, d * 1024:(d + 1) * 1024],
                             stg_i[:, dd * 1024:(dd + 1) * 1024],
                             l0_fc1s[d:d + 1, fofs:fofs + 1024],
                             ones_f, t_ld1)
        t_w1 = E.tok("v")
        for w in range(4):
            for jj in range(2):
                j = w * 2 + jj
                E.dma("gpsimd", stg_i[:, jj * 1024:(jj + 1) * 1024],
                      l0_fc2q[fofs + j * 128:fofs + (j + 1) * 128, :],
                      waits=[E.tok("v")] if jj == 0 else ())
            t_ld2 = E.tok("q_gpsimd")
            for jj in range(2):
                j = w * 2 + jj
                dequant_tile(w2q[:, j * 1024:(j + 1) * 1024],
                             stg_i[:, jj * 1024:(jj + 1) * 1024],
                             l0_fc2s[Q * 8 + j:Q * 8 + j + 1, :],
                             half_f, t_ld2)
        t_w2 = E.tok("v")

        for ci, tc in enumerate(TC):
            for j in range(8):
                ftile = Q * 8 + j
                yreg = pA[:, (j % 2) * 512:(j % 2) * 512 + 512]
                for d in range(NDT):
                    E.do("tensor", lambda e, d=d, j=j, tc=tc, yreg=yreg: e.matmul(
                        yreg, w1q[:, d * 1024 + j * 128: d * 1024 + (j + 1) * 128],
                        hsl(d, tc), start=(d == 0), stop=(d == NDT - 1)),
                        waits=([t_w2, t_h, silu_toks[j % 2]]
                               if d == 0 else ()), inc="mm")
                tym = E.tok("mm")
                silu_toks[j % 2] = E.do(
                    "scalar", lambda e, j=j, yreg=yreg, ftile=ftile: e.activation(
                        out=ybuf[:, j * 512:(j + 1) * 512], in_=yreg,
                        func=AF.Silu, bias=fc1bcol[:, ftile:ftile + 1]),
                    waits=[tym], inc="act")
            t_y = E.tok("act")
            for d in range(NDT):
                zreg = pB[:, 512 + (d % 2) * 512: 1024 + (d % 2) * 512]
                for j in range(8):
                    E.do("tensor", lambda e, d=d, j=j, zreg=zreg: e.matmul(
                        zreg, w2q[:, j * 1024 + d * 128: j * 1024 + (d + 1) * 128],
                        ybuf[:, j * 512:(j + 1) * 512],
                        start=(j == 0), stop=(j == 7)),
                        waits=[t_y, xadd_toks[d % 2]] if j == 0 else (), inc="mm")
                tzm = E.tok("mm")
                xadd_toks[d % 2] = E.do(
                    "vector", lambda e, d=d, tc=tc, zreg=zreg: e.tensor_add(
                        out=xsl(d, tc), in0=xsl(d, tc), in1=zreg),
                    waits=[tzm], inc="v")
    E.flush()
    dtap("ffn", [lambda d=d: E.dma("gpsimd", dbg_out[:, d * 2048:(d + 1) * 2048], xsl(d))
                 for d in range(NDT)])
    if upto == "ffn":
        _finish(E, nc, y_out, xsl)
        return nc

    ar.reset(base_mark)
    stg_f = ar.alloc([128, 2048], F32, "stg_f")
    kvT = ar.alloc([128, 32 * 66], BF16, "kvT")
    wvu_b = ar.alloc([64, 2048], BF16, "wvu_b")
    wo_b = ar.alloc([128, 2 * 1024], BF16, "wo_b")
    P_b = ar.alloc([128, 32 * 512], BF16, "P_b")
    on_b = ar.alloc([128, 2 * 512], BF16, "on_b")
    un_b = ar.alloc([64, 1024], BF16, "un_b")
    rz_bc = ar.alloc([64, 512], F32, "rz_bc")
    rzrow = ar.alloc([1, 512], F32, "rzrow")
    qt_l = ar.alloc([64, T_LOC], BF16, "qt_l")

    tw2 = E.dma("sync", stg_f[0:64, :], l0_wvuT[:, :], waits=[E.tok("v")])
    E.do("vector", lambda e: e.tensor_copy(out=wvu_b[:, :], in_=stg_f[0:64, :]),
         waits=[tw2], inc="v")
    t_wvu = E.tok("v")

    E.do("vector", lambda e: e.memset(kvT[:, :], 1.0), inc="v")
    tg1 = E.dma("gpsimd", kvfull[:, 0:T_LOC].bitcast(I32), ag0_out[0:64, :],
                waits=[tcc0])
    tg2 = E.dma("gpsimd", kvfull[:, T_LOC:T_FULL].bitcast(I32), ag0_out[64:128, :])
    t_kvf = E.do("vector", lambda e: e.engine_nop(), waits=[tg1, tg2], inc="v")
    for kt in range(32):
        reg = pA[:, (kt % 2) * 512:(kt % 2) * 512 + 32].bitcast(BF16)
        E.do("tensor", lambda e, kt=kt, reg=reg: e.transpose(
            reg, kvfull[0:64, kt * 128:(kt + 1) * 128], ident_b[0:64, 0:64]),
            waits=[t_kvf, E.tok("act")], inc="mm")
        E.do("scalar", lambda e, kt=kt, reg=reg: e.activation(
            out=kvT[:, kt * 66:kt * 66 + 64], in_=reg, func=AF.Copy),
            waits=[E.tok("mm")], inc="act")
    t_kvT = E.tok("act")
    E.flush()

    exp_toks = [None, None]
    for hh in range(8):
        for o in range(2):
            E.dma("sync", stg_f[:, o * 1024:(o + 1) * 1024],
                  l0_woT[hh * 256 + o * 128: hh * 256 + (o + 1) * 128, :],
                  waits=[E.tok("v"), E.tok("mm")] if o == 0 else ())
        t_ldo = E.tok("q_sync")
        t_wo = E.do("vector", lambda e: e.tensor_copy(
            out=wo_b[:, :], in_=stg_f[:, :]), waits=[t_ldo], inc="v")
        t_qtl = E.do("vector", lambda e, hh=hh: e.tensor_copy(
            out=qt_l[:, :], in_=qtil_all[(hh % 2) * 64:(hh % 2) * 64 + 64,
                                         (hh // 2) * T_LOC:(hh // 2 + 1) * T_LOC]),
            waits=[E.tok("mm")], inc="v")

        pend = None
        for ci, tc in enumerate(TC):
            uz = pA[0:65, 1024 + (ci % 2) * 512:1536 + (ci % 2) * 512]
            unb = un_b[:, (ci % 2) * 512:(ci % 2) * 512 + 512]
            exp_k = [None] * 32

            def emit_uz(kt, uz=uz, exp_k=exp_k, ew=()):
                E.do("tensor", lambda e, kt=kt, uz=uz: e.matmul(
                    uz, kvT[:, kt * 66:kt * 66 + 65],
                    P_b[:, kt * 512:(kt + 1) * 512],
                    start=(kt == 0), stop=(kt == 31)),
                    waits=[exp_k[kt]] + list(ew), inc="mm")

            for kt in range(32):
                sreg = pA[:, (kt % 2) * 512:(kt % 2) * 512 + 512]
                tm = E.do("tensor", lambda e, kt=kt, tc=tc, sreg=sreg: e.matmul(
                    sreg, kvfull[:, kt * 128:(kt + 1) * 128], qt_l[:, tc],
                    start=True, stop=True),
                    waits=[t_kvf, t_kvT, t_qtl, exp_toks[kt % 2]] +
                          ([E.tok("v")] if kt == 0 else []), inc="mm")
                exp_toks[kt % 2] = exp_k[kt] = E.do(
                    "scalar", lambda e, kt=kt, sreg=sreg: e.activation(
                        out=P_b[:, kt * 512:(kt + 1) * 512], in_=sreg,
                        func=AF.Exp, scale=1.0 / 16.0), waits=[tm], inc="act")
                if kt >= 2:
                    emit_uz(kt - 2, ew=[t_kvT, E.tok("v")] if kt == 2 else ())
            emit_uz(30)
            emit_uz(31)
            t_exp = E.tok("act")
            if dump == "attn1" and hh == 0 and ci == 0:
                E.do("gpsimd", lambda e: e.nop(), waits=[t_exp])
                E.dma("gpsimd", dbg_out[:, 4096:12288].bitcast(BF16), P_b[:, :])
                E.do("gpsimd", lambda e: e.nop(), waits=[E.tok("q_gpsimd")])
            t_uz = E.tok("mm")
            if pend is not None:
                pend()
            trz = E.do("vector", lambda e, uz=uz: e.reciprocal(
                out=rzrow[:, :], in_=uz[64:65, :]), waits=[t_uz], inc="v")
            tbm = E.do("tensor", lambda e: e.matmul(
                pB[0:64, 0:512], ones_f[0:1, 0:64], rzrow[:, :],
                start=True, stop=True), waits=[trz, E.tok("act")], inc="mm")
            trzb = E.do("scalar", lambda e: e.activation(
                out=rz_bc[:, :], in_=pB[0:64, 0:512], func=AF.Copy),
                waits=[tbm, E.tok("v")], inc="act")
            t_un = E.do("vector", lambda e, uz=uz, unb=unb: e.tensor_tensor(
                out=unb, in0=uz[0:64, :], in1=rz_bc[:, :], op=ALU.mult),
                waits=[trzb], inc="v")

            def pend(ci=ci, tc=tc, unb=unb, t_un=t_un, trzb=trzb):
                for dv in range(2):
                    oreg = pB[:, dv * 512:(dv + 1) * 512]
                    E.do("tensor", lambda e, dv=dv, oreg=oreg, unb=unb:
                         e.matmul(
                             oreg,
                             wvu_b[:, hh * 256 + dv * 128: hh * 256 + (dv + 1) * 128],
                             unb, start=True, stop=True),
                         waits=[t_un, t_wvu, trzb] if dv == 0 else (), inc="mm")
                    E.do("scalar", lambda e, dv=dv, oreg=oreg: e.activation(
                        out=on_b[:, dv * 512:(dv + 1) * 512], in_=oreg,
                        func=AF.Copy), waits=[E.tok("mm")], inc="act")
                t_on = E.tok("act")
                for d in range(NDT):
                    wreg = pB[:, 1024 + (d % 2) * 512: 1536 + (d % 2) * 512]
                    for o in range(2):
                        E.do("tensor", lambda e, d=d, o=o, wreg=wreg: e.matmul(
                            wreg,
                            wo_b[:, o * 1024 + d * 128: o * 1024 + (d + 1) * 128],
                            on_b[:, o * 512:(o + 1) * 512],
                            start=(o == 0), stop=(o == 1)),
                            waits=[t_on, t_wo, E.tok("v")] if o == 0 else (),
                            inc="mm")
                    E.do("vector", lambda e, d=d, tc=tc, wreg=wreg: e.tensor_add(
                        out=xsl(d, tc), in0=xsl(d, tc), in1=wreg),
                        waits=[E.tok("mm")], inc="v")
        pend()
    E.flush()

    dtap("l0", [lambda d=d: E.dma("gpsimd", dbg_out[:, d * 2048:(d + 1) * 2048], xsl(d))
                for d in range(NDT)])
    if upto == "l0":
        _finish(E, nc, y_out, xsl)
        return nc

    ar.reset(0)
    xn1_b = ar.alloc([128, NDT * T_LOC], BF16, "xn1_b")

    def xn1sl(d, t=None):
        if t is None:
            return xn1_b[:, d * T_LOC:(d + 1) * T_LOC]
        return xn1_b[:, d * T_LOC + t.start: d * T_LOC + t.stop]

    emit_stats()
    for ci, tc in enumerate(TC):
        tb = emit_ln_bcast(tc, "ab")
        for d in range(NDT):
            E.do("vector", lambda e, d=d, tc=tc: e.tensor_tensor(
                out=sq_a[:, :], in0=xsl(d, tc), in1=bc_a[:, :], op=ALU.mult),
                waits=[tb], inc="v")
            E.do("vector", lambda e, d=d, tc=tc: e.tensor_add(
                out=xn1sl(d, tc), in0=sq_a[:, :], in1=bc_b[:, :]), inc="v")
    t_xn1 = E.tok("v")
    for d in range(NDT):
        E.do("vector", lambda e, d=d: e.tensor_reduce(
            out=sq_a[:, 0:1] if d == 0 else sq_a[:, 1:2], in_=xn1sl(d),
            axis=AX.X, op=ALU.max, apply_absolute_value=True),
            waits=[t_xn1, E.tok("v")] if d == 0 else [E.tok("v")], inc="v")
        if d > 0:
            vch(lambda e: e.tensor_max(
                out=sq_a[:, 0:1], in0=sq_a[:, 0:1], in1=sq_a[:, 1:2]))
    tmx = E.do("vector", lambda e: e.engine_nop(), inc="v")
    tt1 = E.do("tensor", lambda e: e.transpose(
        pB[0:1, 0:128], sq_a[:, 0:1], ident[:, :]),
        waits=[tmx, E.tok("act")], inc="mm")
    tt2 = E.do("scalar", lambda e: e.activation(
        out=brow_s[0:1, 0:128], in_=pB[0:1, 0:128], func=AF.Copy),
        waits=[tt1], inc="act")
    E.do("vector", lambda e: e.tensor_reduce(
        out=scal[0:1, 0:1], in_=brow_s[0:1, 0:128], axis=AX.X, op=ALU.max),
        waits=[tt2], inc="v")
    tg1v = E.tok("v")
    tq1 = E.dma("gpsimd", ar1_in[:, :], scal[0:1, 0:1], waits=[tg1v])
    tar1 = E.do("gpsimd", lambda e: e.collective_compute(
        "AllReduce", ALU.max, replica_groups=[list(range(8))],
        ins=[ar1_in.ap().opt()], outs=[ar1_out.ap().opt()]),
        waits=[tq1], inc="cc")
    mark_aproj = ar.mark()
    wslab = ar.alloc([128, 8 * 128], BF16, "wslab")
    wstg = ar.alloc([128, 512], F32, "wstg")
    pstg = ar.alloc([128, 1024], BF16, "pstg")
    sqv = ar.alloc([128, 512], F32, "sqv")
    nsc = ar.alloc([4, 512], F32, "nsc")
    nv_t = ar.alloc([128, 16], F32, "nv_t")
    wv_b = ar.alloc([128, 8 * 1024], BF16, "wv_b")

    sqv_w = ar.alias(43072, [128, 2048], F32, "sqv_w")
    nsc_w = ar.alias(51264, [4, 2048], F32, "nsc_w")

    t_mult = [None] * 4

    def proj_padded(w_hbm, out_tile_fn):
        t_dma = [None, None]
        for ot in range(4):
            for d in range(NDT):
                tw = E.dma("sync", wstg[:, 0:128],
                           w_hbm[d * 128:(d + 1) * 128, ot * 128:(ot + 1) * 128],
                           waits=[E.tok("v")])
                E.do("vector", lambda e, d=d: e.tensor_copy(
                    out=wslab[:, d * 128:(d + 1) * 128], in_=wstg[:, 0:128]),
                    waits=[tw], inc="v")
            t_ws = E.tok("v")
            for ti, tc in enumerate(TC):
                for d in range(NDT):
                    E.do("tensor", lambda e, d=d, tc=tc, ti=ti: e.matmul(
                        pA[:, ti * 512:(ti + 1) * 512],
                        wslab[:, d * 128:(d + 1) * 128], xn1sl(d, tc),
                        start=(d == 0), stop=(d == NDT - 1)),
                        waits=([t_ws, t_xn1] +
                               ([t_mult[ti]] if t_mult[ti] else
                                [E.tok("act"), E.tok("v")]))
                        if d == 0 else (), inc="mm")
            tpp = E.tok("mm")
            tsq0 = E.do("scalar", lambda e: e.activation(
                out=sqv_w[:, 0:1024], in_=pA[:, 0:1024], func=AF.Square),
                waits=[tpp], inc="act")
            tsq = E.do("scalar", lambda e: e.activation(
                out=sqv_w[:, 1024:2048], in_=pA[:, 1024:2048], func=AF.Square),
                inc="act")
            for ti in range(4):
                E.do("tensor", lambda e, ti=ti: e.matmul(
                    pB[0:4, ti * 512:(ti + 1) * 512], Ekq[:, :],
                    sqv_w[:, ti * 512:(ti + 1) * 512], start=True, stop=True),
                    waits=[tsq0, tsq, E.tok("v")] if ti == 0 else (), inc="mm")
            tnm = E.tok("mm")
            E.do("scalar", lambda e: e.activation(
                out=nsc_w[:, :], in_=pB[0:4, 0:2048], func=AF.Sqrt),
                waits=[tnm], inc="act")
            E.do("vector", lambda e: e.engine_nop(), waits=[E.tok("act")],
                 inc="v")
            vch(lambda e: e.tensor_scalar_max(
                out=nsc_w[:, :], in0=nsc_w[:, :], scalar1=1e-12))
            vch(lambda e: e.reciprocal(out=nsc_w[:, :], in_=nsc_w[:, :]))
            trc = E.tok("v")
            for ti in range(4):
                tex = E.do("tensor", lambda e, ti=ti: e.matmul(
                    pB[:, ti * 512:(ti + 1) * 512], Eexp[:, :],
                    nsc_w[:, ti * 512:(ti + 1) * 512], start=True, stop=True),
                    waits=([trc, E.tok("act")] if ti == 0 else []), inc="mm")
                tev = E.do("scalar", lambda e, ti=ti: e.activation(
                    out=sqv_w[:, ti * 512:(ti + 1) * 512],
                    in_=pB[:, ti * 512:(ti + 1) * 512], func=AF.Copy),
                    waits=[tex], inc="act")
                t_mult[ti] = tno = E.do(
                    "vector", lambda e, ti=ti: e.tensor_tensor(
                        out=pstg[:, (ti % 2) * 512:(ti % 2) * 512 + 512],
                        in0=pA[:, ti * 512:(ti + 1) * 512],
                        in1=sqv_w[:, ti * 512:(ti + 1) * 512], op=ALU.mult),
                    waits=[tev] + ([t_dma[ti % 2]] if t_dma[ti % 2] else []),
                    inc="v")
                t_dma[ti % 2] = out_tile_fn(ot, ti, tno)

    def q_out(ot, ti, tok):
        return E.dma(
            "gpsimd", qn_hbm[ot * 128:(ot + 1) * 128, ti * 512:(ti + 1) * 512],
            pstg[:, (ti % 2) * 512:(ti % 2) * 512 + 512], waits=[tok])

    def k_out(ot, ti, tok):
        return E.dma(
            "gpsimd", ag_kn_in[ot * 128:(ot + 1) * 128, ti * 256:(ti + 1) * 256],
            pstg[:, (ti % 2) * 512:(ti % 2) * 512 + 512].bitcast(I32),
            waits=[tok])

    proj_padded(l1_wqp, q_out)
    proj_padded(l1_wkp, k_out)

    for d in range(NDT):
        tw = E.dma("sync", wstg[:, 0:512],
                   l1_wv[d * 128:(d + 1) * 128, 0:512], waits=[E.tok("v")])
        E.do("vector", lambda e, d=d: e.tensor_copy(
            out=wv_b[:, d * 1024:d * 1024 + 512], in_=wstg[:, 0:512]),
            waits=[tw], inc="v")
        tw = E.dma("sync", wstg[:, 0:512],
                   l1_wv[d * 128:(d + 1) * 128, 512:1024], waits=[E.tok("v")])
        E.do("vector", lambda e, d=d: e.tensor_copy(
            out=wv_b[:, d * 1024 + 512:(d + 1) * 1024], in_=wstg[:, 0:512]),
            waits=[tw], inc="v")
    t_wv = E.tok("v")
    sqv_v = ar.alias(59456, [128, 1024], F32, "sqv_v")
    for vt in range(16):
        for dvc in range(2):
            reg = pA[:, 1024 + dvc * 512:1536 + dvc * 512]
            for d in range(NDT):
                E.do("tensor", lambda e, d=d, vt=vt, dvc=dvc, reg=reg: e.matmul(
                    reg, xn1_b[:, d * T_LOC + vt * 128:d * T_LOC + (vt + 1) * 128],
                    wv_b[:, d * 1024 + dvc * 512:d * 1024 + (dvc + 1) * 512],
                    start=(d == 0), stop=(d == NDT - 1)),
                    waits=[t_wv, t_xn1, E.tok("act"), E.tok("v")]
                    if (dvc == 0 and d == 0) else (), inc="mm")
        tpp = E.tok("mm")
        tsq = E.do("scalar", lambda e: e.activation(
            out=sqv_v[:, :], in_=pA[:, 1024:2048], func=AF.Square),
            waits=[tpp], inc="act")
        E.do("vector", lambda e: e.tensor_reduce(
            out=nv_t[:, 0:16],
            in_=sqv_v[:, :].rearrange("p (a b) -> p a b", b=64),
            axis=AX.X, op=ALU.add),
            waits=[tsq, E.tok("v")], inc="v")
        E.do("scalar", lambda e: e.activation(
            out=pstg[:, 0:1024], in_=pA[:, 1024:2048], func=AF.Copy),
            waits=[E.tok("v")], inc="act")
        E.do("vector", lambda e: e.engine_nop(), waits=[E.tok("act")], inc="v")
        vch(lambda e: e.tensor_copy(out=sqv[:, 0:16], in_=nv_t[:, :]))
        E.do("scalar", lambda e: e.activation(
            out=sqv[:, 0:16], in_=sqv[:, 0:16], func=AF.Sqrt),
            waits=[E.tok("v")], inc="act")
        E.do("vector", lambda e: e.engine_nop(), waits=[E.tok("act")], inc="v")
        vch(lambda e: e.tensor_scalar_max(
            out=sqv[:, 0:16], in0=sqv[:, 0:16], scalar1=1e-12))
        vch(lambda e: e.reciprocal(out=sqv[:, 0:16], in_=sqv[:, 0:16]))
        tno = vch(lambda e: e.tensor_tensor(
            out=pstg[:, :].rearrange("p (a b) -> p a b", b=64),
            in0=pstg[:, :].rearrange("p (a b) -> p a b", b=64),
            in1=sqv[:, 0:16].unsqueeze(2).broadcast_to([128, 16, 64]),
            op=ALU.mult))
        E.dma("gpsimd", ag_vn_in[vt * 128:(vt + 1) * 128, :],
              pstg[:, :].bitcast(I32), waits=[tno])
    t_vdone = E.tok("q_gpsimd")
    tag_k = E.do("gpsimd", lambda e: e.collective_compute(
        "AllGather", ALU.bypass,
        replica_groups=[[0, 1], [2, 3], [4, 5], [6, 7]],
        ins=[ag_kn_in.ap().opt()], outs=[ag_kn_out.ap().opt()]),
        waits=[t_vdone], inc="cc")
    tag_v = E.do("gpsimd", lambda e: e.collective_compute(
        "AllGather", ALU.bypass,
        replica_groups=[[0, 1], [2, 3], [4, 5], [6, 7]],
        ins=[ag_vn_in.ap().opt()], outs=[ag_vn_out.ap().opt()]),
        inc="cc")
    ar.reset(mark_aproj)
    E.flush()

    def sc(i):
        return scal[0:1, i:i + 1]

    def bcast_col(src, col):
        reg = pB[:, 1536:1544]
        tm = E.do("tensor", lambda e, src=src, reg=reg: e.matmul(
            reg[:, 0:1], ones_f[0:1, :], src, start=True, stop=True),
            waits=(E.tok("v"), E.tok("act")), inc="mm")
        return E.do("scalar", lambda e, col=col, reg=reg: e.activation(
            out=scol[:, col:col + 1], in_=reg[:, 0:1], func=AF.Copy),
            waits=[tm], inc="act")

    r1_b = ar.alloc([128, NDT * T_LOC], BF16, "r1_b")
    m_stg6 = ar.mark()
    stg_w = ar.alloc([128, 2 * 512], F32, "stg_w")
    stg_q = ar.alloc([128, 512], I32, "stg_q")
    w_big = ar.alloc([128, 8 * 512], BF16, "w_big")
    ub_s = ar.alias(m_stg6, [128, 2048], BF16, "ub_s")

    E.do("gpsimd", lambda e: e.nop(), waits=[tar1])
    tld = E.dma("gpsimd", sc(0), ar1_out[:, 0:1])
    E.do("vector", lambda e: e.engine_nop(), waits=[tld], inc="v")
    bcast_col(sc(0), 16)
    E.do("vector", lambda e: e.engine_nop(), waits=[E.tok("act")], inc="v")
    vch(lambda e: e.tensor_scalar(
        out=scol[:, 17:18], in0=scol[:, 16:17], scalar1=1.0 / 32767.0,
        scalar2=1e-12, op0=ALU.mult, op1=ALU.max))
    vch(lambda e: e.reciprocal(out=scol[:, 0:1], in_=scol[:, 17:18]))
    t_c1 = E.tok("v")
    stg_r1 = ar.alias(m_stg6, [128, 2048], I32, "stg_r1")
    for d in range(NDT):
        E.do("vector", lambda e, d=d: e.tensor_scalar(
            out=stg_r1[:, :], in0=xn1sl(d), scalar1=scol[:, 0:1],
            scalar2=None, op0=ALU.mult),
            waits=[t_c1] if d == 0 else (), inc="v")
        E.do("vector", lambda e, d=d: e.tensor_copy(
            out=r1_b[:, d * T_LOC:(d + 1) * T_LOC], in_=stg_r1[:, :]), inc="v")
    t_r1 = E.tok("v")
    if dump == "l1sc":
        E.do("gpsimd", lambda e: e.nop(), waits=[t_r1, E.tok("act")])
        E.dma("gpsimd", dbg_out[4:5, 0:32], scal[0:1, 0:32])
        E.do("gpsimd", lambda e: e.nop(), waits=[E.tok("q_gpsimd")])
    sTu = ar.alias(0, [32, 3072], F32, "sTu")
    rTu = ar.alias(12 * 1024 + 32, [32, 3072], F32, "rTu")

    def lpbq_scales(w_rm, sT, rT, nrt, first_wait=()):
        t_red = [None, None]
        t_tt = [None, None]
        for rt in range(nrt):
            pp = rt % 2
            hs = slice(pp * 1024, (pp + 1) * 1024)
            tw = E.dma("sync", stg_f2[:, hs], w_rm[rt * 128:(rt + 1) * 128, :],
                       waits=((E.tok("v"), E.tok("mm")) + tuple(first_wait))
                       if rt < 2 else [t_red[pp]])
            t_red[pp] = E.do("vector", lambda e, pp=pp, hs=hs: e.tensor_reduce(
                out=sq_a[:, pp * 32:(pp + 1) * 32],
                in_=stg_f2[:, hs].rearrange("p (a b) -> p a b", b=32),
                axis=AX.X, op=ALU.max, apply_absolute_value=True),
                waits=[tw], inc="v")
            E.do("vector", lambda e, pp=pp: e.tensor_scalar(
                out=sq_a[:, 64 + pp * 32:64 + (pp + 1) * 32],
                in0=sq_a[:, pp * 32:(pp + 1) * 32], scalar1=1.0 / 7.0,
                scalar2=1e-12, op0=ALU.mult, op1=ALU.max),
                waits=[E.tok("v")] + ([t_tt[pp]] if t_tt[pp] else []), inc="v")
            tv = E.tok("v")
            t_tt[pp] = tt = E.do("tensor", lambda e, pp=pp: e.transpose(
                pB[0:32, 0:128], sq_a[:, 64 + pp * 32:64 + (pp + 1) * 32],
                ident[:, :]), waits=[tv, E.tok("act")], inc="mm")
            E.do("scalar", lambda e, rt=rt, sT=sT: e.activation(
                out=sT[0:32, rt * 128:(rt + 1) * 128], in_=pB[0:32, 0:128],
                func=AF.Copy), waits=[tt], inc="act")
        tse = E.tok("act")
        return E.do("vector", lambda e, sT=sT, rT=rT: e.reciprocal(
            out=rT[:, :], in_=sT[:, :]), waits=[tse], inc="v")

    E.do("vector", lambda e: e.memset(scol[:, 1:3], 0.0), inc="v")

    def l1_mm_pass(wT_hbm, sT, rT, hbm_dst, maxcol, t_sc):
        for oc in range(6):
            for it in range(NDT):
                tw = E.dma("sync", stg_w[:, 0:512],
                           wT_hbm[it * 128:(it + 1) * 128,
                                  oc * 512:(oc + 1) * 512],
                           waits=[E.tok("v"), E.tok("act"), E.tok("q_gpsimd")]
                           if it == 0 else [E.tok("v")])
                rreg = pB[:, 1024:1536]
                sreg2 = pB[:, 1536:2048]
                tm1 = E.do("tensor", lambda e, it=it, oc=oc, rT=rT, rreg=rreg:
                           e.matmul(rreg, Esel[:, it * 128:(it + 1) * 128],
                                    rT[:, oc * 512:(oc + 1) * 512],
                                    start=True, stop=True),
                           waits=[t_sc, E.tok("v"), E.tok("act")], inc="mm")
                tm2 = E.do("tensor", lambda e, it=it, oc=oc, sT=sT, sreg2=sreg2:
                           e.matmul(sreg2, Esel[:, it * 128:(it + 1) * 128],
                                    sT[:, oc * 512:(oc + 1) * 512],
                                    start=True, stop=True), inc="mm")
                E.do("vector", lambda e, rreg=rreg: e.tensor_tensor(
                    out=stg_w[:, 512:1024], in0=stg_w[:, 0:512], in1=rreg,
                    op=ALU.mult), waits=[tm1, tw], inc="v")
                E.do("vector", lambda e: e.tensor_copy(
                    out=stg_q[:, :], in_=stg_w[:, 512:1024]), inc="v")
                tse2 = E.do("scalar", lambda e, sreg2=sreg2: e.activation(
                    out=stg_w[:, 512:1024], in_=sreg2, func=AF.Copy),
                    waits=[tm2, E.tok("v")], inc="act")
                E.do("vector", lambda e, it=it: e.tensor_tensor(
                    out=w_big[:, it * 512:(it + 1) * 512], in0=stg_q[:, :],
                    in1=stg_w[:, 512:1024], op=ALU.mult),
                    waits=[tse2], inc="v")
            t_wb = E.tok("v")
            for otl in range(4):
                ot = oc * 4 + otl
                for ci, tc in enumerate(TC):
                    reg = pA[:, (ci % 2) * 512:(ci % 2) * 512 + 512]
                    for it in range(NDT):
                        E.do("tensor", lambda e, it=it, otl=otl, tc=tc, reg=reg:
                             e.matmul(reg,
                                      w_big[:, it * 512 + otl * 128:
                                            it * 512 + (otl + 1) * 128],
                                      r1_b[:, it * T_LOC + tc.start:
                                           it * T_LOC + tc.stop],
                                      start=(it == 0), stop=(it == NDT - 1)),
                             waits=([t_wb, t_r1, E.tok("act")]
                                    if it == 0 else ()), inc="mm")
                    E.do("scalar", lambda e, tc=tc, reg=reg: e.activation(
                        out=ub_s[:, tc], in_=reg, func=AF.Copy),
                        waits=[E.tok("mm")], inc="act")
                tev = E.tok("act")
                E.do("vector", lambda e: e.tensor_reduce(
                    out=sq_a[:, 0:1], in_=ub_s[:, :], axis=AX.X, op=ALU.max,
                    apply_absolute_value=True), waits=[tev, E.tok("v")], inc="v")
                vch(lambda e, maxcol=maxcol: e.tensor_max(
                    out=scol[:, maxcol:maxcol + 1],
                    in0=scol[:, maxcol:maxcol + 1], in1=sq_a[:, 0:1]))
                E.dma("gpsimd", hbm_dst[ot * 128:(ot + 1) * 128, :], ub_s[:, :],
                      waits=[tev])
                E.do("scalar", lambda e: e.nop(), waits=[E.tok("q_gpsimd")])
        return E.tok("v")

    t_sc_u = lpbq_scales(l1_up_rm, sTu, rTu, 24, first_wait=[t_r1])
    t_up = l1_mm_pass(l1_upT, sTu, rTu, up_hbm, 1, t_sc_u)
    if dump == "l1sc":
        E.do("gpsimd", lambda e: e.nop(), waits=[t_up, E.tok("act"), E.tok("mm")])
        E.dma("gpsimd", dbg_out[5:6, 0:32], scal[0:1, 0:32])
        E.do("gpsimd", lambda e: e.nop(), waits=[E.tok("q_gpsimd")])
    t_sc_g = lpbq_scales(l1_gate_rm, sTu, rTu, 24, first_wait=[t_up])
    t_gate = l1_mm_pass(l1_gateT, sTu, rTu, gate_hbm, 2, t_sc_g)

    def partition_max_to(srccol, dst, wait):
        tt1 = E.do("tensor", lambda e, srccol=srccol: e.transpose(
            pB[0:1, 0:128], srccol, ident[:, :]),
            waits=[wait, E.tok("act")], inc="mm")
        tt2 = E.do("scalar", lambda e: e.activation(
            out=brow_s[0:1, 0:128], in_=pB[0:1, 0:128], func=AF.Copy),
            waits=[tt1], inc="act")
        return E.do("vector", lambda e, dst=dst: e.tensor_reduce(
            out=dst, in_=brow_s[0:1, 0:128], axis=AX.X, op=ALU.max),
            waits=[tt2], inc="v")

    tv4 = partition_max_to(scol[:, 1:2], sc(4), t_gate)
    tq3 = E.dma("gpsimd", ar2_in[:, :], sc(4), waits=[tv4])
    tar2a = E.do("gpsimd", lambda e: e.collective_compute(
        "AllReduce", ALU.max, replica_groups=[list(range(8))],
        ins=[ar2_in.ap().opt()], outs=[ar2_out.ap().opt()]),
        waits=[tq3], inc="cc")
    tv5 = partition_max_to(scol[:, 2:3], sc(5), t_gate)
    tq3b = E.dma("gpsimd", ar2b_in[:, :], sc(5), waits=[tv5])
    tar2 = E.do("gpsimd", lambda e: e.collective_compute(
        "AllReduce", ALU.max, replica_groups=[list(range(8))],
        ins=[ar2b_in.ap().opt()], outs=[ar2b_out.ap().opt()]),
        waits=[tq3b], inc="cc")
    E.flush()
    if dump == "l1ug":
        E.do("gpsimd", lambda e: e.nop(), waits=[E.tok("cc")])
        E.dma("gpsimd", dbg_out[0:1, 0:1], ar2_out[:, :])
        E.dma("gpsimd", dbg_out[0:1, 1:2], ar2b_out[:, :])
        E.dma("gpsimd", dbg_out[:, 4096:5120].bitcast(BF16), up_hbm[0:128, :])
        E.dma("gpsimd", dbg_out[:, 5120:6144].bitcast(BF16),
              gate_hbm[20 * 128:21 * 128, :])
        E.do("gpsimd", lambda e: e.nop(), waits=[E.tok("q_gpsimd")])

    ta4 = E.dma("gpsimd", sc(4), ar2_out[:, :], waits=[tar2a])
    tld = E.dma("gpsimd", sc(5), ar2b_out[:, :], waits=[tar2])
    E.do("vector", lambda e: e.engine_nop(), waits=[ta4, tld], inc="v")
    bcast_col(sc(4), 18)
    bcast_col(sc(5), 19)
    E.do("vector", lambda e: e.engine_nop(), waits=[E.tok("act")], inc="v")
    vch(lambda e: e.tensor_tensor(
        out=scol[:, 18:19], in0=scol[:, 18:19], in1=scol[:, 17:18], op=ALU.mult))
    vch(lambda e: e.tensor_scalar(
        out=scol[:, 20:21], in0=scol[:, 18:19], scalar1=1.0 / 32767.0,
        scalar2=1e-12, op0=ALU.mult, op1=ALU.max))
    vch(lambda e: e.reciprocal(out=scol[:, 3:4], in_=scol[:, 20:21]))
    vch(lambda e: e.tensor_tensor(
        out=scol[:, 3:4], in0=scol[:, 3:4], in1=scol[:, 17:18], op=ALU.mult))
    vch(lambda e: e.tensor_tensor(
        out=scol[:, 19:20], in0=scol[:, 19:20], in1=scol[:, 17:18], op=ALU.mult))
    vch(lambda e: e.tensor_scalar(
        out=scol[:, 5:6], in0=scol[:, 19:20], scalar1=1.0 / 32767.0,
        scalar2=1e-12, op0=ALU.mult, op1=ALU.max))
    vch(lambda e: e.reciprocal(out=scol[:, 4:5], in_=scol[:, 5:6]))
    vch(lambda e: e.tensor_tensor(
        out=scol[:, 4:5], in0=scol[:, 4:5], in1=scol[:, 17:18], op=ALU.mult))
    t_sc2 = E.tok("v")

    if dump == "l1sc":
        E.do("gpsimd", lambda e: e.nop(), waits=[E.tok("v"), E.tok("act")])
        E.dma("gpsimd", dbg_out[6:7, 0:32], scal[0:1, 0:32])
        E.do("gpsimd", lambda e: e.nop(), waits=[E.tok("q_gpsimd")])
    pa_g2 = [ar.alias(0, [128, 2048], BF16, "pa_g2a"),
             ar.alias(4096, [128, 2048], BF16, "pa_g2b")]
    pb_u2 = [ar.alias(8192, [128, 2048], BF16, "pb_u2a"),
             ar.alias(12288, [128, 2048], BF16, "pb_u2b")]
    pa_sg2 = [ar.alias(16384, [128, 2048], F32, "pa_sg2a"),
              ar.alias(24576, [128, 2048], F32, "pa_sg2b")]
    pa_rg2 = [ar.alias(32768, [128, 2048], I32, "pa_rg2a"),
              ar.alias(40960, [128, 2048], I32, "pa_rg2b")]
    pa_m32 = [ar.alias(49152, [128, 2048], F32, "pa_m32a"),
              ar.alias(57344, [128, 2048], F32, "pa_m32b")]
    pa_q2 = ar.alias(65536, [128, 2048], I32, "pa_q2")
    pa_m3 = ar.alias(24704, [128, 2048], F32, "pa_m3")
    stg_q2 = ar.alias(41 * 1024 + 128, [128, 2048], I32, "stg_q2")

    t_g_rd = [None, None]
    t_m3st = [None, None]
    for rt in range(24):
        pp = rt % 2
        twg = E.dma("sync", pa_g2[pp][:, :], gate_hbm[rt * 128:(rt + 1) * 128, :],
                    waits=([E.tok("mm"), E.tok("v"), E.tok("act"),
                            E.tok("q_gpsimd")] if rt == 0 else
                           ([t_g_rd[pp]] if t_g_rd[pp] else [])))
        t_g_rd[pp] = E.do("vector", lambda e, pp=pp: e.tensor_scalar(
            out=pa_rg2[pp][:, :], in0=pa_g2[pp][:, :], scalar1=scol[:, 4:5],
            scalar2=None, op0=ALU.mult), waits=[twg, t_sc2], inc="v")
        tsg = E.do("scalar", lambda e, pp=pp: e.activation(
            out=pa_sg2[pp][:, :], in_=pa_rg2[pp][:, :], func=AF.Sigmoid,
            scale=scol[:, 5:6]), waits=[t_g_rd[pp]], inc="act")
        E.do("vector", lambda e, pp=pp: e.tensor_scalar(
            out=pa_q2[:, :], in0=pa_sg2[pp][:, :], scalar1=65536.0,
            scalar2=65535.0, op0=ALU.mult, op1=ALU.min),
            waits=[tsg] + ([E.tok("q_gpsimd")] if rt == 0 else []), inc="v")
        E.do("vector", lambda e, pp=pp: e.tensor_tensor(
            out=pa_m32[pp][:, :], in0=pa_rg2[pp][:, :], in1=pa_q2[:, :],
            op=ALU.mult), waits=[t_m3st[pp]] if t_m3st[pp] else (), inc="v")
        tva = E.tok("v")
        E.do("vector", lambda e, rt=rt, pp=pp: e.tensor_reduce(
            out=sq_a[:, rt:rt + 1], in_=pa_m32[pp][:, :], axis=AX.X, op=ALU.max,
            apply_absolute_value=True), inc="v")
        t_m3st[pp] = E.dma("gpsimd", m3_hbm[rt * 128:(rt + 1) * 128, :],
                           pa_m32[pp][:, :], waits=[tva])
    E.do("vector", lambda e: e.tensor_reduce(
        out=sq_b[:, 0:1], in_=sq_a[:, 0:24], axis=AX.X, op=ALU.max), inc="v")
    tt1 = E.do("tensor", lambda e: e.transpose(
        pB[0:1, 0:128], sq_b[:, 0:1], ident[:, :]),
        waits=[E.tok("v"), E.tok("act")], inc="mm")
    tt2 = E.do("scalar", lambda e: e.activation(
        out=brow_s[0:1, 0:128], in_=pB[0:1, 0:128], func=AF.Copy),
        waits=[tt1], inc="act")
    E.do("vector", lambda e: e.tensor_reduce(
        out=sc(10), in_=brow_s[0:1, 0:128], axis=AX.X, op=ALU.max),
        waits=[tt2], inc="v")
    tq4 = E.dma("gpsimd", ar3_in[:, :], sc(10), waits=[E.tok("v")])
    tar3 = E.do("gpsimd", lambda e: e.collective_compute(
        "AllReduce", ALU.max, replica_groups=[list(range(8))],
        ins=[ar3_in.ap().opt()], outs=[ar3_out.ap().opt()]),
        waits=[tq4], inc="cc")

    tld = E.dma("gpsimd", sc(10), ar3_out[:, :], waits=[tar3])
    E.do("vector", lambda e: e.engine_nop(), waits=[tld], inc="v")
    bcast_col(sc(10), 21)
    E.do("vector", lambda e: e.engine_nop(), waits=[E.tok("act")], inc="v")
    vch(lambda e: e.tensor_tensor(
        out=scol[:, 22:23], in0=scol[:, 21:22], in1=scol[:, 5:6], op=ALU.mult))
    vch(lambda e: e.tensor_scalar(
        out=scol[:, 22:23], in0=scol[:, 22:23], scalar1=1.0 / 65536.0,
        scalar2=None, op0=ALU.mult))
    vch(lambda e: e.tensor_scalar(
        out=scol[:, 23:24], in0=scol[:, 22:23], scalar1=1.0 / 32767.0,
        scalar2=1e-12, op0=ALU.mult, op1=ALU.max))
    vch(lambda e: e.reciprocal(out=scol[:, 24:25], in_=scol[:, 23:24]))
    vch(lambda e: e.tensor_scalar(
        out=scol[:, 7:8], in0=scol[:, 5:6], scalar1=1.0 / 65536.0,
        scalar2=None, op0=ALU.mult))
    vch(lambda e: e.tensor_tensor(
        out=scol[:, 7:8], in0=scol[:, 7:8], in1=scol[:, 24:25], op=ALU.mult))
    t_c3 = E.tok("v")

    t_m4st = [t_m3st[0], t_m3st[1]]
    t_b_rd = [None, None]
    t_q_rd = [None, None]
    for rt in range(24):
        pp = rt % 2
        tw3 = E.dma("sync", pa_sg2[pp][:, :], m3_hbm[rt * 128:(rt + 1) * 128, :],
                    waits=([E.tok("v"), E.tok("q_gpsimd")] if rt < 2 else
                           [t_q_rd[pp]]))
        twu = E.dma("sync", pb_u2[pp][:, :], up_hbm[rt * 128:(rt + 1) * 128, :],
                    waits=[t_b_rd[pp]] if t_b_rd[pp] else ())
        t_b_rd[pp] = E.do("vector", lambda e, pp=pp: e.tensor_scalar(
            out=pa_rg2[pp][:, :], in0=pb_u2[pp][:, :], scalar1=scol[:, 3:4],
            scalar2=None, op0=ALU.mult), waits=[twu], inc="v")
        t_q_rd[pp] = E.do("vector", lambda e, pp=pp: e.tensor_scalar(
            out=pa_q2[:, :], in0=pa_sg2[pp][:, :], scalar1=scol[:, 7:8],
            scalar2=None, op0=ALU.mult), waits=[tw3, t_c3], inc="v")
        E.do("vector", lambda e, pp=pp: e.tensor_tensor(
            out=pa_m32[pp][:, :], in0=pa_q2[:, :], in1=pa_rg2[pp][:, :],
            op=ALU.mult), waits=[t_m4st[pp]], inc="v")
        E.do("vector", lambda e, rt=rt, pp=pp: e.tensor_reduce(
            out=sq_a[:, rt:rt + 1], in_=pa_m32[pp][:, :], axis=AX.X, op=ALU.max,
            apply_absolute_value=True), inc="v")
        t_m4st[pp] = E.dma("gpsimd", m4_hbm[rt * 128:(rt + 1) * 128, :],
                           pa_m32[pp][:, :], waits=[E.tok("v")])
    E.do("vector", lambda e: e.tensor_reduce(
        out=sq_b[:, 0:1], in_=sq_a[:, 0:24], axis=AX.X, op=ALU.max), inc="v")
    tt1 = E.do("tensor", lambda e: e.transpose(
        pB[0:1, 0:128], sq_b[:, 0:1], ident[:, :]),
        waits=[E.tok("v"), E.tok("act")], inc="mm")
    tt2 = E.do("scalar", lambda e: e.activation(
        out=brow_s[0:1, 0:128], in_=pB[0:1, 0:128], func=AF.Copy),
        waits=[tt1], inc="act")
    E.do("vector", lambda e: e.tensor_reduce(
        out=sc(14), in_=brow_s[0:1, 0:128], axis=AX.X, op=ALU.max),
        waits=[tt2], inc="v")
    tq5 = E.dma("gpsimd", ar4_in[:, :], sc(14), waits=[E.tok("v")])
    tar4 = E.do("gpsimd", lambda e: e.collective_compute(
        "AllReduce", ALU.max, replica_groups=[list(range(8))],
        ins=[ar4_in.ap().opt()], outs=[ar4_out.ap().opt()]),
        waits=[tq5], inc="cc")

    sTd = ar.alias(0, [32, 3 * 1024], F32, "sTd")
    rTd_c = [ar.alias(19072, [32, 1024], F32, "rTd0"),
             ar.alias(32896, [32, 1024], F32, "rTd1"),
             ar.alias(36992, [32, 1024], F32, "rTd2")]
    stg_d2a = ar.alias(24576, [128, 2048], F32, "stg_d2a")
    stg_d2b = ar.alias(40960, [128, 1024], F32, "stg_d2b")
    stg_d = [(stg_f2[:, 0:2048], stg_w[:, 0:1024]),
             (stg_d2a[:, :], stg_d2b[:, :])]
    t_redd = [None, None]
    t_ttd = [None, None]
    for rt in range(NDT):
        pp = rt % 2
        sA, sB = stg_d[pp]
        tw = E.dma("sync", sA, l1_down_rm[rt * 128:(rt + 1) * 128, 0:2048],
                   waits=[E.tok("v"), E.tok("mm")] if rt < 2 else [t_redd[pp]])
        tw2b = E.dma("sync", sB, l1_down_rm[rt * 128:(rt + 1) * 128, 2048:3072])
        E.do("vector", lambda e, pp=pp, sA=sA: e.tensor_reduce(
            out=sq_a[:, pp * 96:pp * 96 + 64],
            in_=sA.rearrange("p (a b) -> p a b", b=32),
            axis=AX.X, op=ALU.max, apply_absolute_value=True),
            waits=[tw], inc="v")
        t_redd[pp] = E.do("vector", lambda e, pp=pp, sB=sB: e.tensor_reduce(
            out=sq_a[:, pp * 96 + 64:pp * 96 + 96],
            in_=sB.rearrange("p (a b) -> p a b", b=32),
            axis=AX.X, op=ALU.max, apply_absolute_value=True),
            waits=[tw2b], inc="v")
        E.do("vector", lambda e, pp=pp: e.tensor_scalar(
            out=sq_a[:, 192 + pp * 96:192 + (pp + 1) * 96],
            in0=sq_a[:, pp * 96:(pp + 1) * 96], scalar1=1.0 / 7.0,
            scalar2=1e-12, op0=ALU.mult, op1=ALU.max),
            waits=[E.tok("v")] + ([t_ttd[pp]] if t_ttd[pp] else []), inc="v")
        tv = E.tok("v")
        t_ttd[pp] = tt = E.do("tensor", lambda e, pp=pp: e.transpose(
            pB[0:96, 0:128], sq_a[:, 192 + pp * 96:192 + (pp + 1) * 96],
            ident[:, :]), waits=[tv, E.tok("act")], inc="mm")
        for g3 in range(3):
            E.do("scalar", lambda e, rt=rt, g3=g3: e.activation(
                out=sTd[0:32, g3 * 1024 + rt * 128: g3 * 1024 + (rt + 1) * 128],
                in_=pB[g3 * 32:(g3 + 1) * 32, 0:128], func=AF.Copy),
                waits=[tt], inc="act")
    tse = E.tok("act")
    for g3 in range(3):
        E.do("vector", lambda e, g3=g3: e.reciprocal(
            out=rTd_c[g3][:, :], in_=sTd[:, g3 * 1024:(g3 + 1) * 1024]),
            waits=[tse] if g3 == 0 else (), inc="v")
    t_sc_d = E.tok("v")

    tld = E.dma("gpsimd", sc(14), ar4_out[:, :], waits=[tar4])
    E.do("vector", lambda e: e.engine_nop(), waits=[tld], inc="v")
    bcast_col(sc(14), 25)
    E.do("vector", lambda e: e.engine_nop(), waits=[E.tok("act")], inc="v")
    vch(lambda e: e.tensor_tensor(
        out=scol[:, 26:27], in0=scol[:, 23:24], in1=scol[:, 20:21], op=ALU.mult))
    vch(lambda e: e.tensor_tensor(
        out=scol[:, 27:28], in0=scol[:, 26:27], in1=scol[:, 25:26], op=ALU.mult))
    vch(lambda e: e.tensor_scalar(
        out=scol[:, 10:11], in0=scol[:, 27:28], scalar1=1.0 / 32767.0,
        scalar2=1e-12, op0=ALU.mult, op1=ALU.max))
    vch(lambda e: e.reciprocal(out=scol[:, 9:10], in_=scol[:, 10:11]))
    vch(lambda e: e.tensor_tensor(
        out=scol[:, 9:10], in0=scol[:, 9:10], in1=scol[:, 26:27], op=ALU.mult))
    t_c4 = E.tok("v")
    t_s4 = t_c4

    if dump == "l1sc":
        E.do("gpsimd", lambda e: e.nop(), waits=[E.tok("v"), E.tok("act")])
        E.dma("gpsimd", dbg_out[0:1, 0:32], scol[0:1, 0:32])
        E.dma("gpsimd", dbg_out[0:1, 4096:6144], m3_hbm[0:1, :])
        E.dma("gpsimd", dbg_out[0:1, 6144:8192].bitcast(I32) if False else dbg_out[1:2, 4096:6144], r2_hbm[0:1, :])
        E.dma("gpsimd", dbg_out[2:3, 4096:6144], m4_hbm[0:1, :])
        E.do("gpsimd", lambda e: e.nop(), waits=[E.tok("q_gpsimd")])

    o_bc4 = ar.alias(59 * 1024, [128, 4 * 2048], BF16, "o_bc4")
    wd_b = ar.alias(50 * 1024, [128, 4 * 1024], BF16, "wd_b")
    stg_wC = ar.alias(12 * 1024 + 512, [128, 1024], F32, "stg_wC")
    stg_qC = ar.alias(16 * 1024 + 640, [128, 512], I32, "stg_qC")
    t_mmC = None
    for oc in range(6):
        for itl in range(4):
            rt = oc * 4 + itl
            tw = E.dma("sync", pa_m3[:, :], m4_hbm[rt * 128:(rt + 1) * 128, :],
                       waits=[E.tok("v"), E.tok("mm"), E.tok("q_gpsimd")]
                       if oc == 0 and itl == 0 else [E.tok("v")])
            E.do("vector", lambda e: e.tensor_scalar(
                out=stg_q2[:, :], in0=pa_m3[:, :], scalar1=scol[:, 9:10],
                scalar2=None, op0=ALU.mult), waits=[tw, t_c4, t_s4], inc="v")
            E.do("vector", lambda e, itl=itl: e.tensor_copy(
                out=o_bc4[:, itl * 2048:(itl + 1) * 2048], in_=stg_q2[:, :]),
                waits=[t_mmC] if (t_mmC and itl == 0) else (), inc="v")
        for itl in range(4):
            it = oc * 4 + itl
            for hc in range(2):
                tw = E.dma("sync", stg_wC[:, 0:512],
                           l1_downT[it * 128:(it + 1) * 128,
                                    hc * 512:(hc + 1) * 512],
                           waits=[E.tok("v")])
                rreg = pB[:, 1024:1536]
                sreg2 = pB[:, 1536:2048]
                tm1 = E.do("tensor", lambda e, it=it, hc=hc, rreg=rreg:
                           e.matmul(rreg, Esel[:, (it % 8) * 128:(it % 8 + 1) * 128],
                                    rTd_c[it // 8][:, hc * 512:(hc + 1) * 512],
                                    start=True, stop=True),
                           waits=[t_sc_d, E.tok("v"), E.tok("act")], inc="mm")
                tm2 = E.do("tensor", lambda e, it=it, hc=hc, sreg2=sreg2:
                           e.matmul(sreg2, Esel[:, (it % 8) * 128:(it % 8 + 1) * 128],
                                    sTd[:, (it // 8) * 1024 + hc * 512:
                                        (it // 8) * 1024 + (hc + 1) * 512],
                                    start=True, stop=True), inc="mm")
                E.do("vector", lambda e, rreg=rreg: e.tensor_tensor(
                    out=stg_wC[:, 512:1024], in0=stg_wC[:, 0:512], in1=rreg,
                    op=ALU.mult), waits=[tm1, tw], inc="v")
                E.do("vector", lambda e: e.tensor_copy(
                    out=stg_qC[:, :], in_=stg_wC[:, 512:1024]), inc="v")
                tse2 = E.do("scalar", lambda e, sreg2=sreg2: e.activation(
                    out=stg_wC[:, 512:1024], in_=sreg2, func=AF.Copy),
                    waits=[tm2, E.tok("v")], inc="act")
                E.do("vector", lambda e, itl=itl, hc=hc: e.tensor_tensor(
                    out=wd_b[:, itl * 1024 + hc * 512:itl * 1024 + (hc + 1) * 512],
                    in0=stg_qC[:, :], in1=stg_wC[:, 512:1024], op=ALU.mult),
                    waits=[tse2], inc="v")
        t_wd = E.tok("v")
        for ci, tc in enumerate(TC):
            for d in range(NDT):
                zreg = pA[:, 512 + (d % 2) * 512:1024 + (d % 2) * 512] if False \
                    else pA[:, (d % 2) * 512:(d % 2) * 512 + 512]
                for itl in range(4):
                    E.do("tensor", lambda e, d=d, itl=itl, tc=tc, zreg=zreg:
                         e.matmul(zreg,
                                  wd_b[:, itl * 1024 + d * 128:
                                       itl * 1024 + (d + 1) * 128],
                                  o_bc4[:, itl * 2048 + tc.start:
                                        itl * 2048 + tc.stop],
                                  start=(itl == 0), stop=(itl == 3)),
                         waits=[t_wd, E.tok("v")] if itl == 0 else (), inc="mm")
                E.do("vector", lambda e, d=d, tc=tc, zreg=zreg:
                     e.scalar_tensor_tensor(
                         out=xsl(d, tc), in0=zreg, scalar=scol[:, 10:11],
                         in1=xsl(d, tc), op0=ALU.mult, op1=ALU.add),
                     waits=[E.tok("mm")], inc="v")
        t_mmC = E.tok("mm")
    E.flush()
    if dump == "l1ffn":
        for d in range(NDT):
            E.dma("gpsimd", dbg_out[:, d * 2048:(d + 1) * 2048], xsl(d))
        E.do("gpsimd", lambda e: e.nop(), waits=[E.tok("q_gpsimd")])
    if upto == "l1ffn":
        _finish(E, nc, y_out, xsl)
        return nc

    for d in range(NDT):
        E.dma("gpsimd", x_spill[d * 128:(d + 1) * 128, :], xsl(d),
              waits=[E.tok("v"), E.tok("mm")] if d == 0 else ())
    t_spill = E.tok("q_gpsimd")

    ar.reset(0)
    qn_s = ar.alloc([128, T_LOC], BF16, "qn_s")
    kn_s = ar.alloc([128, T_FULL], BF16, "kn_s")
    P2q = ar.alloc([128, 4 * 1024], BF16, "P2q")
    rstg2 = ar.alloc([128, 2 * 1024], BF16, "rstg2")
    sqv8 = ar.alloc([128, 512], F32, "sqv8")
    fsc = ar.alloc([2, 512], F32, "fsc")
    fsc2 = ar.alloc([2, 512], F32, "fsc2")
    fex = ar.alloc([128, 512], F32, "fex")
    on_st = ar.alloc([128, 512], BF16, "on_st")
    vn_b = nc.alloc_sbuf_tensor_at("vn_b", [128, 32 * 1024], BF16,
                                   offset=nc.lookup_mloc(xs).addr)

    for r in range(2):
        for vt in range(16):
            E.dma("sync",
                  vn_b[:, (r * 16 + vt) * 1024:(r * 16 + vt + 1) * 1024]
                  .bitcast(I32),
                  ag_vn_out[r * 2048 + vt * 128: r * 2048 + (vt + 1) * 128, :],
                  waits=[tag_v, t_spill] if (r == 0 and vt == 0) else ())
    t_kvld = E.tok("q_sync")
    E.do("vector", lambda e: e.engine_nop(), waits=[t_kvld], inc="v")

    wout_b = ar.alloc([128, 8 * 1024], BF16, "wout_b")
    onld = ar.alloc([128, 8 * 512], BF16, "onld")
    wstg8b = ar.alloc([128, 1024], F32, "wstg8b")
    onld2 = [onld, ar.alias(4096, [128, 8 * 512], BF16, "onld_b")]
    for kt in range(NDT):
        tw = E.dma("gpsimd", wstg8b[:, :], l1_wout[kt * 128:(kt + 1) * 128, :],
                   waits=[E.tok("v")])
        E.do("vector", lambda e, kt=kt: e.tensor_copy(
            out=wout_b[:, kt * 1024:(kt + 1) * 1024], in_=wstg8b[:, :]),
            waits=[tw], inc="v")
    t_wo8 = E.tok("v")

    QC4 = [slice(i * 512, (i + 1) * 512) for i in range(4)]
    relu_toks8 = [None, None]
    rstg_toks = [None, None]
    pv_toks = [None, None, None, None]
    for pr in range(8):
        slab = pr // 2
        ha = slab * 4 + (pr % 2) * 2
        sa = 32 * (ha % 4)
        sb = sa + 32
        if pr % 2 == 0:
            E.dma("sync", qn_s[:, :], qn_hbm[slab * 128:(slab + 1) * 128, :],
                  waits=[E.tok("mm"), E.tok("v")])
            for r in range(2):
                E.dma("sync",
                      kn_s[:, r * T_LOC:(r + 1) * T_LOC].bitcast(I32),
                      ag_kn_out[r * 512 + slab * 128:
                                r * 512 + (slab + 1) * 128, :],
                      waits=[tag_k] if (pr == 0 and r == 0) else ())
            t_ld8v = E.tok("q_sync")
            E.do("vector", lambda e: e.engine_nop(), waits=[t_ld8v], inc="v")
        t_ld8 = E.tok("v")
        for ci, qc in enumerate(QC4):
            ov = pB[:, 512:1024]
            sq_toks = [None] * 4

            def emit_pv(j, ha=ha, ov=ov, sq_toks=sq_toks):
                sl = j % 4
                for hj in range(2):
                    pv_toks[sl] = E.do(
                        "tensor", lambda e, j=j, hj=hj, sl=sl, ha=ha, ov=ov:
                        e.matmul(
                            ov[hj * 64:(hj + 1) * 64, :],
                            vn_b[:, j * 1024 + (ha + hj) * 64:
                                 j * 1024 + (ha + hj + 1) * 64],
                            P2q[:, sl * 1024 + hj * 512:
                                sl * 1024 + (hj + 1) * 512],
                            start=(j == 0), stop=(j == 31),
                            tile_position=(0, hj * 64)),
                        waits=[sq_toks[sl]] if hj == 0 else (), inc="mm")

            for kt in range(32):
                par = kt % 2
                sl = kt % 4
                for hj, sp in ((0, sa), (1, sb)):
                    reg = pA[:, par * 1024 + hj * 512:
                             par * 1024 + (hj + 1) * 512]
                    E.do("tensor", lambda e, kt=kt, sp=sp, qc=qc, reg=reg:
                         e.matmul(
                             reg,
                             kn_s[sp:sp + 32, kt * 128:(kt + 1) * 128],
                             qn_s[sp:sp + 32, qc],
                             start=True, stop=True, tile_position=(sp, 0)),
                         waits=(([t_ld8] if kt == 0 else []) +
                                ([relu_toks8[par]] if relu_toks8[par] else []))
                         if hj == 0 else (), inc="mm")
                tmm = E.tok("mm")
                relu_toks8[par] = tr = E.do(
                    "scalar", lambda e, par=par: e.activation(
                        out=rstg2[:, par * 1024:(par + 1) * 1024],
                        in_=pA[:, par * 1024:(par + 1) * 1024], func=AF.Relu),
                    waits=[tmm] + ([rstg_toks[par]] if rstg_toks[par] else []),
                    inc="act")
                E.do("vector", lambda e, par=par, sl=sl: e.tensor_tensor(
                    out=P2q[:, sl * 1024:sl * 1024 + 512],
                    in0=rstg2[:, par * 1024:par * 1024 + 512],
                    in1=rstg2[:, par * 1024:par * 1024 + 512], op=ALU.mult),
                    waits=[tr] + ([pv_toks[sl]] if pv_toks[sl] else []),
                    inc="v")
                rstg_toks[par] = sq_toks[sl] = E.do(
                    "vector", lambda e, par=par, sl=sl: e.tensor_tensor(
                        out=P2q[:, sl * 1024 + 512:(sl + 1) * 1024],
                        in0=rstg2[:, par * 1024 + 512:(par + 1) * 1024],
                        in1=rstg2[:, par * 1024 + 512:(par + 1) * 1024],
                        op=ALU.mult), inc="v")
                if kt >= 2:
                    emit_pv(kt - 2)
            emit_pv(30)
            emit_pv(31)
            t_pv = E.tok("mm")
            tsq = E.do("scalar", lambda e, ov=ov: e.activation(
                out=sqv8[:, :], in_=ov, func=AF.Square), waits=[t_pv], inc="act")
            tnm = E.do("tensor", lambda e: e.matmul(
                pB[0:2, 0:512], E64[:, :], sqv8[:, :], start=True, stop=True),
                waits=[tsq, E.tok("v")], inc="mm")
            E.do("scalar", lambda e: e.activation(
                out=fsc[:, :], in_=pB[0:2, 0:512], func=AF.Sqrt),
                waits=[tnm], inc="act")
            E.do("vector", lambda e: e.engine_nop(), waits=[E.tok("act")], inc="v")
            vch(lambda e: e.tensor_scalar_max(
                out=fsc2[:, :], in0=fsc[:, :], scalar1=1e-12))
            vch(lambda e: e.reciprocal(out=fsc2[:, :], in_=fsc2[:, :]))
            tth = E.do("scalar", lambda e: e.activation(
                out=fsc[:, :], in_=fsc[:, :], func=AF.Tanh),
                waits=[E.tok("v")], inc="act")
            E.do("vector", lambda e: e.engine_nop(), waits=[tth], inc="v")
            vch(lambda e: e.tensor_tensor(
                out=fsc[:, :], in0=fsc[:, :], in1=fsc2[:, :], op=ALU.mult))
            tf = E.tok("v")
            tfe = E.do("tensor", lambda e: e.matmul(
                pB[:, 1024:1536], E2[:, :], fsc[:, :], start=True, stop=True),
                waits=[tf, E.tok("act")], inc="mm")
            tfc = E.do("scalar", lambda e: e.activation(
                out=fex[:, :], in_=pB[:, 1024:1536], func=AF.Copy),
                waits=[tfe], inc="act")
            ton = E.do("vector", lambda e, ov=ov: e.tensor_tensor(
                out=on_st[:, :], in0=ov, in1=fex[:, :], op=ALU.mult),
                waits=[tfc], inc="v")
            E.dma("gpsimd",
                  outn_hbm[pr * 128:(pr + 1) * 128, ci * 512:(ci + 1) * 512],
                  on_st[:, :], waits=[ton])
            E.do("vector", lambda e: e.engine_nop(),
                 waits=[E.tok("q_gpsimd")], inc="v")
    E.flush()

    for d in range(NDT):
        E.dma("sync", xsl(d), x_spill[d * 128:(d + 1) * 128, :],
              waits=[E.tok("q_gpsimd"), E.tok("mm"), E.tok("v")]
              if d == 0 else ())
    t_xre = E.tok("q_sync")
    t_mmw = [None, None]
    for ci, tc in enumerate(TC):
        ob = onld2[ci % 2]
        for kt in range(NDT):
            E.dma("sync", ob[:, kt * 512:(kt + 1) * 512],
                  outn_hbm[kt * 128:(kt + 1) * 128, tc],
                  waits=([E.tok("v")] +
                         ([t_mmw[ci % 2]] if t_mmw[ci % 2] else [E.tok("mm")]))
                  if kt == 0 else ())
        t_on8 = E.tok("q_sync")
        for d in range(NDT):
            reg = pB[:, 1024 + (d % 2) * 512:1536 + (d % 2) * 512]
            for kt in range(NDT):
                E.do("tensor", lambda e, d=d, kt=kt, reg=reg, ob=ob: e.matmul(
                    reg, wout_b[:, kt * 1024 + d * 128: kt * 1024 + (d + 1) * 128],
                    ob[:, kt * 512:(kt + 1) * 512],
                    start=(kt == 0), stop=(kt == NDT - 1)),
                    waits=[t_on8, t_wo8, t_xre, E.tok("v")]
                    if kt == 0 else (), inc="mm")
            E.do("vector", lambda e, d=d, tc=tc, reg=reg: e.tensor_add(
                out=xsl(d, tc), in0=xsl(d, tc), in1=reg),
                waits=[E.tok("mm")], inc="v")
        t_mmw[ci % 2] = E.tok("mm")
    E.flush()

    _finish(E, nc, y_out, xsl)
    return nc

def _finish(E, nc, y_out, xsl):
    for d in range(NDT):
        E.dma("gpsimd", y_out[d * 128:(d + 1) * 128, :], xsl(d),
              waits=[E.tok("v"), E.tok("act"), E.tok("mm")] if d == 0 else ())
    tst = E.tok("q_gpsimd")
    E.do("gpsimd", lambda e: e.nop(), waits=[tst])
    E.flush()


_CACHE = {}


def _consts():
    ident = np.eye(128, dtype=np.float32)
    Ekq = np.zeros((128, 4), np.float32)
    for p in range(128):
        if p % 32 < 16:
            Ekq[p, p // 32] = 1.0
    Eexp = np.zeros((4, 128), np.float32)
    for p in range(128):
        Eexp[p // 32, p] = 1.0
    E64 = np.zeros((128, 2), np.float32)
    E64[np.arange(128), np.arange(128) // 64] = 1.0
    E2 = np.zeros((2, 128), np.float32)
    E2[np.arange(128) // 64, np.arange(128)] = 1.0
    Esel = np.zeros((32, 1024), np.float32)
    for it in range(8):
        for p in range(128):
            Esel[4 * it + p // 32, it * 128 + p] = 1.0
    return ident, Ekq, Eexp, E64, E2, Esel


def _colmajor(v):
    return np.ascontiguousarray(v.reshape(-1, 128).T)


def kernel(L0_attn_x, L0_attn_Wq, L0_attn_Wkv_down, L0_attn_Wk_up, L0_attn_Wv_up,
           L0_attn_Wo, L0_ffn_ln_gamma, L0_ffn_ln_beta, L0_ffn_fc1_q,
           L0_ffn_fc1_scales, L0_ffn_fc1_bias, L0_ffn_fc2_q, L0_ffn_fc2_scales,
           L0_ffn_fc2_bias, L1_attn_Wq, L1_attn_Wkv, L1_attn_Wout,
           L1_ffn_w_gate, L1_ffn_w_up, L1_ffn_w_down, _dump=None, _upto=None):
    f32 = np.float32
    asc = np.ascontiguousarray

    ident, Ekq, Eexp, E64, E2, Esel = _consts()
    shared = {
        "l0_wq": asc(np.asarray(L0_attn_Wq, f32)),
        "l0_wkvdT": asc(np.asarray(L0_attn_Wkv_down, f32).T),
        "l0_wku": asc(np.asarray(L0_attn_Wk_up, f32)),
        "l0_wvuT": asc(np.asarray(L0_attn_Wv_up, f32).T),
        "l0_woT": asc(np.asarray(L0_attn_Wo, f32).T),
        "l0_fc1q": asc(np.asarray(L0_ffn_fc1_q, np.int32)),
        "l0_fc2q": asc(np.asarray(L0_ffn_fc2_q, np.int32)),
        "l0_fc1s": asc(np.asarray(L0_ffn_fc1_scales, f32)),
        "l0_fc2s": asc(np.asarray(L0_ffn_fc2_scales, f32)),
        "ln0_g": _colmajor(np.asarray(L0_ffn_ln_gamma, f32)),
        "ln0_b": _colmajor(np.asarray(L0_ffn_ln_beta, f32)),
        "fc1b_c": _colmajor(np.asarray(L0_ffn_fc1_bias, f32)),
        "fc2b_c": _colmajor(np.asarray(L0_ffn_fc2_bias, f32)),
        "l1_wout": asc(np.asarray(L1_attn_Wout, f32)),
        "l1_upT": asc(np.asarray(L1_ffn_w_up, f32).T),
        "l1_gateT": asc(np.asarray(L1_ffn_w_gate, f32).T),
        "l1_downT": asc(np.asarray(L1_ffn_w_down, f32).T),
        "l1_up_rm": asc(np.asarray(L1_ffn_w_up, f32)),
        "l1_gate_rm": asc(np.asarray(L1_ffn_w_gate, f32)),
        "l1_down_rm": asc(np.asarray(L1_ffn_w_down, f32)),
        "c_ident": ident, "c_Ekq": Ekq, "c_Eexp": Eexp, "c_E64": E64,
        "c_E2": E2, "c_Esel": Esel,
    }
    Wq1 = np.asarray(L1_attn_Wq, f32)
    Wkv1 = np.asarray(L1_attn_Wkv, f32)
    wqp = np.zeros((D, 512), f32)
    wkp = np.zeros((D, 512), f32)
    wv = np.zeros((D, 1024), f32)
    for h in range(16):
        wqp[:, h * 32:h * 32 + 16] = Wq1[:, h * 16:(h + 1) * 16]
        wkp[:, h * 32:h * 32 + 16] = Wkv1[:, h * 80:h * 80 + 16]
        wv[:, h * 64:(h + 1) * 64] = Wkv1[:, h * 80 + 16:(h + 1) * 80]
    shared["l1_wqp"] = wqp
    shared["l1_wkp"] = wkp
    shared["l1_wv"] = wv

    x = np.asarray(L0_attn_x, f32)
    in_maps = []
    for c in range(8):
        b, half = c // 2, c % 2
        m = dict(shared)
        m["x_in"] = asc(x[b, half * T_LOC:(half + 1) * T_LOC, :].T)
        in_maps.append(m)

    key = (_dump, _upto)
    if key not in _CACHE:
        _CACHE[key] = build_program(dump=_dump, upto=_upto)
    res = run_bass_kernel_spmd(_CACHE[key], in_maps, core_ids=list(range(8)))

    out = np.empty((4, T_FULL, D), f32)
    for c in range(8):
        b, half = c // 2, c % 2
        out[b, half * T_LOC:(half + 1) * T_LOC, :] = res.results[c]["y_out"].T
    if _dump:
        return out, [res.results[c].get("dbg_out") for c in range(8)]
    return out


# version 16 (final)
# speedup vs baseline: 1.4535x; 1.0026x over previous
import numpy as np
import concourse.bass as bass
import concourse.mybir as mybir
from concourse.bass_utils import run_bass_kernel_spmd

F32 = mybir.dt.float32
BF16 = mybir.dt.bfloat16
I32 = mybir.dt.int32
AF = mybir.ActivationFunctionType
ALU = mybir.AluOpType
AX = mybir.AxisListType

T_LOC = 2048
T_FULL = 4096
D = 1024
NDT = D // 128
EPS_LN = 1e-5
ENGINES = ("sync", "tensor", "vector", "scalar", "gpsimd")
TC = [slice(i * 512, (i + 1) * 512) for i in range(4)]
KC8 = [slice(i * 512, (i + 1) * 512) for i in range(8)]


class Emitter:

    def __init__(self, nc):
        self.nc = nc
        self.q = {e: [] for e in ENGINES}
        self.sems = {}
        self.cnt = {}
        self._ctx = []

    def sem(self, name):
        if name not in self.sems:
            cm = self.nc.semaphore(name)
            self.sems[name] = cm.__enter__()
            self._ctx.append(cm)
            self.cnt[name] = 0
        return self.sems[name]

    def do(self, engine, fn, waits=(), inc=None, inck=1):
        tok = None
        if inc is not None:
            self.sem(inc)
            self.cnt[inc] += inck
            tok = (inc, self.cnt[inc])
        self.q[engine].append((fn, tuple(w for w in waits if w), inc, inck))
        return tok

    def dma(self, engine, out, in_, waits=()):
        qn = "q_" + engine
        return self.do(engine, lambda e, o=out, i=in_: e.dma_start(out=o, in_=i),
                       waits=waits, inc=qn, inck=16)

    def tok(self, semname):
        return (semname, self.cnt.get(semname, 0))

    def guard(self):
        return (self.tok("v"), self.tok("act"), self.tok("gv"))

    def flush(self):
        nc, q = self.nc, self.q
        self.q = {e: [] for e in ENGINES}
        sems = self.sems
        used = [e for e in ENGINES if q[e]]
        if not used:
            return

        def runner(ename):
            def run(eng):
                seen = {}
                for fn, waits, inc, inck in q[ename]:
                    for sname, val in waits:
                        if val > seen.get(sname, 0):
                            eng.wait_ge(sems[sname], val)
                            seen[sname] = val
                    ins = fn(eng)
                    if inc is not None:
                        ins.then_inc(sems[inc], inck)
            return run

        with nc.Block() as block:
            for e in used:
                getattr(block, e)(runner(e))


class Arena:
    def __init__(self, nc, name, nbytes):
        self.nc = nc
        self.backing = nc.alloc_sbuf_tensor(name, [128, nbytes], mybir.dt.uint8)
        self.base = nc.lookup_mloc(self.backing).addr
        self.off = 0
        self.nbytes = nbytes
        self.n = 0

    def alloc(self, shape, dtype, name=None):
        per_part = int(np.prod(shape[1:])) * mybir.dt.size(dtype)
        off = (self.off + 31) & ~31
        assert off + per_part <= self.nbytes, \
            f"arena overflow: {off}+{per_part} > {self.nbytes} ({name})"
        self.off = off + per_part
        self.n += 1
        return self.nc.alloc_sbuf_tensor_at(name or f"ar{self.n}", shape, dtype,
                                            offset=self.base + off)

    def mark(self):
        return self.off

    def reset(self, mark=0):
        self.off = mark

    def alias(self, at, shape, dtype, name):
        per_part = int(np.prod(shape[1:])) * mybir.dt.size(dtype)
        at = (at + 31) & ~31
        assert at + per_part <= self.nbytes
        self.n += 1
        return self.nc.alloc_sbuf_tensor_at(name, shape, dtype,
                                            offset=self.base + at)


def build_program(dump=None, upto=None):
    nc = bass.Bass("TRN2", target_bir_lowering=False)
    E = Emitter(nc)

    DT = nc.dram_tensor
    x_in = DT("x_in", [D, T_LOC], F32, kind="ExternalInput")
    l0_wq = DT("l0_wq", [2048, D], F32, kind="ExternalInput")
    l0_wkvdT = DT("l0_wkvdT", [D, 64], F32, kind="ExternalInput")
    l0_wku = DT("l0_wku", [2048, 64], F32, kind="ExternalInput")
    l0_wvuT = DT("l0_wvuT", [64, 2048], F32, kind="ExternalInput")
    l0_woT = DT("l0_woT", [2048, D], F32, kind="ExternalInput")
    l0_fc1q = DT("l0_fc1q", [D, 4096], I32, kind="ExternalInput")
    l0_fc2q = DT("l0_fc2q", [4096, D], I32, kind="ExternalInput")
    l0_fc1s = DT("l0_fc1s", [8, 4096], F32, kind="ExternalInput")
    l0_fc2s = DT("l0_fc2s", [32, D], F32, kind="ExternalInput")
    ln0_g = DT("ln0_g", [128, 8], F32, kind="ExternalInput")
    ln0_b = DT("ln0_b", [128, 8], F32, kind="ExternalInput")
    fc1b_c = DT("fc1b_c", [128, 32], F32, kind="ExternalInput")
    fc2b_c = DT("fc2b_c", [128, 8], F32, kind="ExternalInput")
    l1_wqp = DT("l1_wqp", [D, 512], F32, kind="ExternalInput")
    l1_wkp = DT("l1_wkp", [D, 512], F32, kind="ExternalInput")
    l1_wv = DT("l1_wv", [D, 1024], F32, kind="ExternalInput")
    l1_wout = DT("l1_wout", [1024, D], F32, kind="ExternalInput")
    l1_upT = DT("l1_upT", [D, 3072], F32, kind="ExternalInput")
    l1_gateT = DT("l1_gateT", [D, 3072], F32, kind="ExternalInput")
    l1_downT = DT("l1_downT", [3072, D], F32, kind="ExternalInput")
    l1_up_rm = DT("l1_up_rm", [3072, D], F32, kind="ExternalInput")
    l1_gate_rm = DT("l1_gate_rm", [3072, D], F32, kind="ExternalInput")
    l1_down_rm = DT("l1_down_rm", [D, 3072], F32, kind="ExternalInput")
    c_ident = DT("c_ident", [128, 128], F32, kind="ExternalInput")
    c_Ekq = DT("c_Ekq", [128, 4], F32, kind="ExternalInput")
    c_Eexp = DT("c_Eexp", [4, 128], F32, kind="ExternalInput")
    c_E64 = DT("c_E64", [128, 2], F32, kind="ExternalInput")
    c_E2 = DT("c_E2", [2, 128], F32, kind="ExternalInput")
    c_Esel = DT("c_Esel", [32, 1024], F32, kind="ExternalInput")

    y_out = DT("y_out", [D, T_LOC], F32, kind="ExternalOutput")
    dbg_out = DT("dbg_out", [128, 16384], F32, kind="ExternalOutput") if dump else None

    ag0_in = DT("ag0_in", [64, T_LOC // 2], I32)
    ag0_out = DT("ag0_out", [128, T_LOC // 2], I32)
    qn_hbm = DT("qn_hbm", [512, T_LOC], BF16)
    outn_hbm = DT("outn_hbm", [1024, T_LOC], BF16)
    ag_kn_in = DT("ag_kn_in", [512, T_LOC // 2], I32)
    ag_kn_out = DT("ag_kn_out", [1024, T_LOC // 2], I32)
    ag_vn_in = DT("ag_vn_in", [T_LOC, 512], I32)
    ag_vn_out = DT("ag_vn_out", [2 * T_LOC, 512], I32)
    ar2_in = DT("ar2_in", [1, 1], F32)
    ar2_out = DT("ar2_out", [1, 1], F32)
    ar2b_in = DT("ar2b_in", [1, 1], F32)
    ar2b_out = DT("ar2b_out", [1, 1], F32)
    ar1_in = DT("ar1_in", [1, 1], F32)
    ar1_out = DT("ar1_out", [1, 1], F32)
    ar3_in = DT("ar3_in", [1, 1], F32)
    ar3_out = DT("ar3_out", [1, 1], F32)
    ar4_in = DT("ar4_in", [1, 1], F32)
    ar4_out = DT("ar4_out", [1, 1], F32)
    up_hbm = DT("up_hbm", [3072, T_LOC], BF16)
    gate_hbm = DT("gate_hbm", [3072, T_LOC], BF16)
    r2_hbm = DT("r2_hbm", [3072, T_LOC], I32)
    m3_hbm = DT("m3_hbm", [3072, T_LOC], F32)
    m4_hbm = DT("m4_hbm", [3072, T_LOC], F32)
    x_spill = DT("x_spill", [D, T_LOC], F32)

    A = nc.alloc_sbuf_tensor
    xs = A("xs", [128, NDT * T_LOC], F32)
    stat_r = A("stat_r", [1, 2 * T_LOC], F32)
    srw = A("srw", [1, 1536], F32)
    ident = A("ident", [128, 128], F32)
    ident_b = A("ident_b", [128, 128], BF16)
    ones_f = A("ones_f", [128, 128], F32)
    ones_b = A("ones_b", [128, 1], BF16)
    Ekq = A("Ekq", [128, 4], F32)
    Eexp = A("Eexp", [4, 128], F32)
    E64 = A("E64", [128, 2], F32)
    E2 = A("E2", [2, 128], F32)
    Esel = A("Esel", [32, 1024], F32)
    g0col = A("g0col", [128, 8], F32)
    b0col = A("b0col", [128, 8], F32)
    fc1bcol = A("fc1bcol", [128, 32], F32)
    fc2bcol = A("fc2bcol", [128, 8], F32)
    scal = A("scal", [1, 32], F32)
    brow_s = A("brow_s", [1, 1024], F32)
    half_f = A("half_f", [1, 128], F32)
    scol = A("scol", [128, 32], F32)
    kvfull = A("kvfull", [64, T_FULL], BF16)
    qtil_all = A("qtil_all", [128, 4 * T_LOC], BF16)
    sq_a = A("sq_a", [128, 512], F32)
    sq_b = A("sq_b", [128, 512], F32)
    bc_a = A("bc_a", [128, 512], F32)
    bc_b = A("bc_b", [128, 512], F32)

    pA = nc.alloc_psum_tensor("pA", [128, 2048], F32)
    pB = nc.alloc_psum_tensor("pB", [128, 2048], F32)

    rem = (nc.sbuf_bytes_remaining - 512) & ~31
    ar = Arena(nc, "arena", rem)

    def xsl(d, t=None):
        if t is None:
            return xs[:, d * T_LOC:(d + 1) * T_LOC]
        return xs[:, d * T_LOC + t.start: d * T_LOC + t.stop]

    def MU(t=None):
        return stat_r[0:1, t.start:t.stop] if t else stat_r[0:1, 0:T_LOC]

    def M2(t=None):
        return stat_r[0:1, T_LOC + t.start:T_LOC + t.stop] if t else stat_r[0:1, T_LOC:2 * T_LOC]

    def dtap(label, emitfns):
        if dump == label:
            for fn in emitfns:
                fn()
            E.do("gpsimd", lambda e: e.nop(), waits=[E.tok("q_gpsimd")])

    for d in range(NDT):
        E.dma("sync", xsl(d), x_in[d * 128:(d + 1) * 128, :])
    for dst, src in ((ident, c_ident), (Ekq, c_Ekq), (Eexp, c_Eexp),
                     (E64, c_E64), (E2, c_E2), (Esel, c_Esel), (g0col, ln0_g), (b0col, ln0_b),
                     (fc1bcol, fc1b_c), (fc2bcol, fc2b_c)):
        E.dma("sync", dst[:, :], src[:, :])
    t_loads = E.tok("q_sync")
    E.do("vector", lambda e: e.memset(ones_f[:, :], 1.0))
    E.do("vector", lambda e: e.memset(ones_b[:, :], 1.0))
    E.do("vector", lambda e: e.memset(half_f[:, :], 0.5), inc="v")
    E.do("vector", lambda e: e.tensor_scalar_mul(
        out=fc2bcol[:, :], in0=fc2bcol[:, :], scalar1=0.5),
        waits=[t_loads], inc="v")
    E.do("vector", lambda e: e.tensor_copy(out=ident_b[:, :], in_=ident[:, :]),
         inc="v")
    E.do("sync", lambda e: e.nop(), waits=[t_loads])
    E.flush()

    def emit_stats():
        sqs = [sq_a, sq_b]
        for ci, tc in enumerate(TC):
            s1 = pA[0:1, 0:512]
            s2 = pA[0:1, 512:1024]
            for d in range(NDT):
                tsq = E.do("scalar",
                           lambda e, d=d, tc=tc, s=sqs[d % 2]:
                           e.activation(out=s[:, :], in_=xsl(d, tc), func=AF.Square),
                           waits=[E.tok("mm")] if d >= 2 else (), inc="act")
                E.do("tensor",
                     lambda e, d=d, tc=tc, s1=s1: e.matmul(
                         s1, ones_f[:, 0:1], xsl(d, tc),
                         start=(d == 0), stop=(d == NDT - 1)),
                     waits=[E.tok("v")] if d == 0 else (), inc="mm")
                E.do("tensor",
                     lambda e, d=d, s2=s2, s=sqs[d % 2]: e.matmul(
                         s2, ones_f[:, 0:1], s[:, :],
                         start=(d == 0), stop=(d == NDT - 1)),
                     waits=[tsq], inc="mm")
            tmm = E.tok("mm")
            E.do("vector", lambda e, tc=tc, s1=s1: e.tensor_scalar(
                out=MU(tc), in0=s1, scalar1=1.0 / D,
                scalar2=None, op0=ALU.mult), waits=[tmm], inc="v")
            E.do("vector", lambda e, tc=tc, s2=s2: e.tensor_scalar(
                out=M2(tc), in0=s2, scalar1=1.0 / D,
                scalar2=None, op0=ALU.mult), inc="v")
        return E.tok("v")

    def emit_ln_bcast(tc, kind):
        g = E.guard()
        s0 = srw[0:1, 0:512]
        s1 = srw[0:1, 512:1024]
        s2 = srw[0:1, 1024:1536]
        E.do("vector", lambda e: e.tensor_tensor(
            out=s0, in0=MU(tc), in1=MU(tc), op=ALU.mult), waits=g)
        E.do("vector", lambda e: e.tensor_sub(out=s2, in0=M2(tc), in1=s0))
        E.do("vector", lambda e: e.tensor_scalar_add(out=s0, in0=s2, scalar1=EPS_LN),
             inc="v")
        E.do("scalar", lambda e: e.activation(out=s0, in_=s0, func=AF.Sqrt),
             waits=[E.tok("v")], inc="act")
        E.do("vector", lambda e: e.reciprocal(out=s0, in_=s0),
             waits=[E.tok("act")], inc="v")
        E.do("vector", lambda e: e.scalar_tensor_tensor(
            out=s1, in0=MU(tc), scalar=-1.0, in1=s0, op0=ALU.mult, op1=ALU.mult),
            inc="v")
        if kind == "acbc":
            E.do("vector", lambda e: e.tensor_tensor(out=s2, in0=s2, in1=s0, op=ALU.mult))
            E.do("vector", lambda e: e.tensor_tensor(out=s2, in0=s2, in1=s0, op=ALU.mult))
            E.do("vector", lambda e: e.tensor_scalar_add(out=s2, in0=s2, scalar1=EPS_LN),
                 inc="v")
            E.do("scalar", lambda e: e.activation(out=s2, in_=s2, func=AF.Sqrt),
                 waits=[E.tok("v")], inc="act")
            E.do("vector", lambda e: e.reciprocal(out=s2, in_=s2),
                 waits=[E.tok("act")], inc="v")
            E.do("vector", lambda e: e.tensor_tensor(out=s0, in0=s0, in1=s2, op=ALU.mult))
            E.do("vector", lambda e: e.tensor_tensor(out=s1, in0=s1, in1=s2, op=ALU.mult),
                 inc="v")
        tlast = E.tok("v")
        out = None
        for row, dest, reg in ((s0, bc_a, pB[:, 1024:1536]),
                               (s1, bc_b, pB[:, 1536:2048])):
            tm = E.do("tensor", lambda e, row=row, reg=reg: e.matmul(
                reg, ones_f[0:1, :], row, start=True, stop=True),
                waits=g + (tlast,), inc="mm")
            out = E.do("scalar", lambda e, reg=reg, dest=dest: e.activation(
                out=dest[:, :], in_=reg, func=AF.Copy), waits=[tm], inc="act")
        return out

    def vch(fn):
        return E.do("vector", fn, waits=[E.tok("v")], inc="v")

    emit_stats()
    E.flush()
    dtap("stats0", [lambda: E.dma("gpsimd", dbg_out[0:1, 0:4096], stat_r[:, :])])

    base_mark = ar.mark()
    h_b = ar.alloc([128, NDT * T_LOC], BF16, "h_b")
    mark3 = ar.mark()
    xn_b = ar.alloc([128, NDT * T_LOC], BF16, "xn_b")
    wkvd_f = ar.alloc([128, 64], F32, "wkvd_f")
    wkvd_b = ar.alloc([128, NDT * 64], BF16, "wkvd_b")
    m_kv = ar.mark()
    kv_own = ar.alloc([64, T_LOC], BF16, "kv_own")
    m_stg = ar.mark()
    stg_f2 = ar.alloc([128, 2048], F32, "stg_f2")
    mh_b = ar.alias(m_stg, [128, NDT * 64], BF16, "mh_b")
    wqo_b = ar.alias(m_kv, [128, 2 * 1024], BF16, "wqo_b")
    wku_b = ar.alias(m_kv + 4 * 1024 + 32, [128, 2 * 64], BF16, "wku_b")

    def hsl(d, t):
        return h_b[:, d * T_LOC + t.start: d * T_LOC + t.stop]

    def xnsl(d, t):
        return xn_b[:, d * T_LOC + t.start: d * T_LOC + t.stop]

    def qtsl(hh, t):
        p = (hh % 2) * 64
        c = (hh // 2) * T_LOC
        return qtil_all[p:p + 64, c + t.start:c + t.stop]

    for d in range(NDT):
        tl = E.dma("sync", wkvd_f[:, :], l0_wkvdT[d * 128:(d + 1) * 128, :],
                   waits=[E.tok("v")])
        E.do("vector", lambda e, d=d: e.tensor_copy(
            out=wkvd_b[:, d * 64:(d + 1) * 64], in_=wkvd_f[:, :]),
            waits=[tl], inc="v")
    t_wkv = E.tok("v")

    for ci, tc in enumerate(TC):
        tb = emit_ln_bcast(tc, "ab")
        kvp = pA[0:64, 0:512]
        for d in range(NDT):
            E.do("vector", lambda e, d=d, tc=tc: e.tensor_tensor(
                out=sq_a[:, :], in0=xsl(d, tc), in1=bc_a[:, :], op=ALU.mult),
                waits=[tb], inc="v")
            tx = E.do("vector", lambda e, d=d, tc=tc: e.tensor_add(
                out=xnsl(d, tc), in0=sq_a[:, :], in1=bc_b[:, :]), inc="v")
            E.do("tensor", lambda e, d=d, tc=tc, kvp=kvp: e.matmul(
                kvp, wkvd_b[:, d * 64:(d + 1) * 64], xnsl(d, tc),
                start=(d == 0), stop=(d == NDT - 1)),
                waits=[tx, t_wkv] + ([E.tok("act")] if d == 0 else []), inc="mm")
        E.do("scalar", lambda e, tc=tc, kvp=kvp: e.activation(
            out=kv_own[:, tc], in_=kvp, func=AF.Copy),
            waits=[E.tok("mm")], inc="act")
    t_kv = E.tok("act")
    t_xnb = E.tok("v")
    tq = E.dma("gpsimd", ag0_in[:, :], kv_own[:, :].bitcast(I32), waits=[t_kv])
    tcc0 = E.do("gpsimd", lambda e: e.collective_compute(
        "AllGather", ALU.bypass,
        replica_groups=[[0, 1], [2, 3], [4, 5], [6, 7]],
        ins=[ag0_in.ap().opt()], outs=[ag0_out.ap().opt()]),
        waits=[tq], inc="cc")

    for ci, tc in enumerate(TC):
        tb = emit_ln_bcast(tc, "acbc")
        for d in range(NDT):
            E.do("vector", lambda e, d=d, tc=tc: e.tensor_tensor(
                out=sq_a[:, :], in0=xsl(d, tc), in1=bc_a[:, :], op=ALU.mult),
                waits=[tb], inc="v")
            E.do("vector", lambda e: e.tensor_add(
                out=sq_a[:, :], in0=sq_a[:, :], in1=bc_b[:, :]))
            E.do("vector", lambda e, d=d, tc=tc: e.tensor_scalar(
                out=hsl(d, tc), in0=sq_a[:, :], scalar1=g0col[:, d:d + 1],
                scalar2=b0col[:, d:d + 1], op0=ALU.mult, op1=ALU.add), inc="v")
    for ci, tc in enumerate(TC):
        for d in range(NDT):
            E.do("vector", lambda e, d=d, tc=tc: e.tensor_add(
                out=xsl(d, tc), in0=xsl(d, tc), in1=xnsl(d, tc)))
            E.do("vector", lambda e, d=d, tc=tc: e.tensor_scalar_add(
                out=xsl(d, tc), in0=xsl(d, tc), scalar1=fc2bcol[:, d:d + 1]),
                inc="v")
    t_h = E.tok("v")

    for hh in range(8):
        for dh in range(2):
            E.dma("sync", stg_f2[:, dh * 1024:(dh + 1) * 1024],
                  l0_wq[hh * 256 + dh * 128: hh * 256 + (dh + 1) * 128, :],
                  waits=[E.tok("v"), E.tok("mm"), tq] if dh == 0 else ())
        t_ldq = E.tok("q_sync")
        t_wq = E.do("vector", lambda e: e.tensor_copy(
            out=wqo_b[:, :], in_=stg_f2[:, :]), waits=[t_ldq], inc="v")
        for dh in range(2):
            E.dma("sync", stg_f2[:, dh * 64:(dh + 1) * 64],
                  l0_wku[hh * 256 + dh * 128: hh * 256 + (dh + 1) * 128, :],
                  waits=[t_wq] if dh == 0 else ())
        t_ldk = E.tok("q_sync")
        t_wk = E.do("vector", lambda e: e.tensor_copy(
            out=wku_b[:, :], in_=stg_f2[:, 0:128]), waits=[t_ldk], inc="v")
        for d in range(NDT):
            reg = pB[:, (d % 2) * 512:(d % 2) * 512 + 64]
            for dh in range(2):
                E.do("tensor", lambda e, d=d, dh=dh, reg=reg: e.matmul(
                    reg, wqo_b[:, dh * 1024 + d * 128: dh * 1024 + (d + 1) * 128],
                    wku_b[:, dh * 64:(dh + 1) * 64],
                    start=(dh == 0), stop=(dh == 1)),
                    waits=[t_wq, t_wk, E.tok("act")] if dh == 0 else (), inc="mm")
            E.do("scalar", lambda e, d=d, reg=reg: e.activation(
                out=mh_b[:, d * 64:(d + 1) * 64], in_=reg, func=AF.Copy),
                waits=[E.tok("mm")], inc="act")
        t_mh = E.tok("act")
        for ci, tc in enumerate(TC):
            reg = pA[0:64, 1024 + (ci % 2) * 512: 1536 + (ci % 2) * 512]
            for d in range(NDT):
                E.do("tensor", lambda e, d=d, tc=tc, reg=reg: e.matmul(
                    reg, mh_b[:, d * 64:(d + 1) * 64], xnsl(d, tc),
                    start=(d == 0), stop=(d == NDT - 1)),
                    waits=[t_mh, t_xnb, E.tok("act")] if d == 0 else (), inc="mm")
            E.do("scalar", lambda e, hh=hh, tc=tc, reg=reg: e.activation(
                out=qtsl(hh, tc), in_=reg, func=AF.Copy),
                waits=[E.tok("mm")], inc="act")
    E.flush()
    dtap("kv", [lambda: E.dma("gpsimd", dbg_out[0:64, 0:2048],
                              kv_own[:, :].bitcast(I32))])

    ar.reset(mark3)
    w1q = ar.alloc([128, NDT * 1024], BF16, "w1q")
    w2q = ar.alloc([128, 8 * 1024], BF16, "w2q")
    m_stg3 = ar.mark()
    stg_i = ar.alloc([128, 2 * 1024], I32, "stg_i")
    ybuf = ar.alias(m_stg3, [128, 8 * 512], BF16, "ybuf")
    sc_bc = ar.alloc([128, 512], F32, "sc_bc")

    def dequant_tile(dst, src_i, s_hbm, lhs_row, wload):
        tsd = E.dma("sync", brow_s[0:1, :], s_hbm, waits=[E.tok("mm")])
        for half in range(2):
            reg = pB[:, half * 512:(half + 1) * 512]
            hs = slice(half * 512, (half + 1) * 512)
            tm = E.do("tensor", lambda e, reg=reg, lhs_row=lhs_row, hs=hs:
                      e.matmul(reg, lhs_row[0:1, :], brow_s[0:1, hs],
                               start=True, stop=True),
                      waits=[tsd, E.tok("act"), E.tok("v")], inc="mm")
            E.do("scalar", lambda e, reg=reg: e.activation(
                out=sc_bc[:, :], in_=reg, func=AF.Copy),
                waits=[tm], inc="act")
            tqb = E.do("scalar", lambda e, dst=dst, src_i=src_i, hs=hs: e.activation(
                out=dst[:, hs], in_=src_i[:, hs], func=AF.Copy, bias=-128.0),
                waits=[wload], inc="act")
            E.do("vector", lambda e, dst=dst, hs=hs: e.tensor_tensor(
                out=dst[:, hs], in0=dst[:, hs], in1=sc_bc[:, :], op=ALU.mult),
                waits=[E.tok("act")], inc="v")
        return E.tok("v")

    t_w2 = None
    silu_toks = [None, None]
    xadd_toks = [None, None]
    for Q in range(4):
        fofs = Q * 1024
        for w in range(4):
            for dd in range(2):
                d = w * 2 + dd
                E.dma("sync", stg_i[:, dd * 1024:(dd + 1) * 1024],
                      l0_fc1q[d * 128:(d + 1) * 128, fofs:fofs + 1024],
                      waits=[E.tok("v")] if dd == 0 and (w > 0 or t_w2) else ())
            t_ld1 = E.tok("q_sync")
            for dd in range(2):
                d = w * 2 + dd
                dequant_tile(w1q[:, d * 1024:(d + 1) * 1024],
                             stg_i[:, dd * 1024:(dd + 1) * 1024],
                             l0_fc1s[d:d + 1, fofs:fofs + 1024],
                             ones_f, t_ld1)
        t_w1 = E.tok("v")
        for w in range(4):
            for jj in range(2):
                j = w * 2 + jj
                E.dma("gpsimd", stg_i[:, jj * 1024:(jj + 1) * 1024],
                      l0_fc2q[fofs + j * 128:fofs + (j + 1) * 128, :],
                      waits=[E.tok("v")] if jj == 0 else ())
            t_ld2 = E.tok("q_gpsimd")
            for jj in range(2):
                j = w * 2 + jj
                dequant_tile(w2q[:, j * 1024:(j + 1) * 1024],
                             stg_i[:, jj * 1024:(jj + 1) * 1024],
                             l0_fc2s[Q * 8 + j:Q * 8 + j + 1, :],
                             half_f, t_ld2)
        t_w2 = E.tok("v")

        for ci, tc in enumerate(TC):
            for j in range(8):
                ftile = Q * 8 + j
                yreg = pA[:, (j % 2) * 512:(j % 2) * 512 + 512]
                for d in range(NDT):
                    E.do("tensor", lambda e, d=d, j=j, tc=tc, yreg=yreg: e.matmul(
                        yreg, w1q[:, d * 1024 + j * 128: d * 1024 + (j + 1) * 128],
                        hsl(d, tc), start=(d == 0), stop=(d == NDT - 1)),
                        waits=([t_w2, t_h, silu_toks[j % 2]]
                               if d == 0 else ()), inc="mm")
                tym = E.tok("mm")
                silu_toks[j % 2] = E.do(
                    "scalar", lambda e, j=j, yreg=yreg, ftile=ftile: e.activation(
                        out=ybuf[:, j * 512:(j + 1) * 512], in_=yreg,
                        func=AF.Silu, bias=fc1bcol[:, ftile:ftile + 1]),
                    waits=[tym], inc="act")
            t_y = E.tok("act")
            for d in range(NDT):
                zreg = pB[:, 512 + (d % 2) * 512: 1024 + (d % 2) * 512]
                for j in range(8):
                    E.do("tensor", lambda e, d=d, j=j, zreg=zreg: e.matmul(
                        zreg, w2q[:, j * 1024 + d * 128: j * 1024 + (d + 1) * 128],
                        ybuf[:, j * 512:(j + 1) * 512],
                        start=(j == 0), stop=(j == 7)),
                        waits=[t_y, xadd_toks[d % 2]] if j == 0 else (), inc="mm")
                tzm = E.tok("mm")
                xadd_toks[d % 2] = E.do(
                    "vector", lambda e, d=d, tc=tc, zreg=zreg: e.tensor_add(
                        out=xsl(d, tc), in0=xsl(d, tc), in1=zreg),
                    waits=[tzm], inc="v")
    E.flush()
    dtap("ffn", [lambda d=d: E.dma("gpsimd", dbg_out[:, d * 2048:(d + 1) * 2048], xsl(d))
                 for d in range(NDT)])
    if upto == "ffn":
        _finish(E, nc, y_out, xsl)
        return nc

    ar.reset(base_mark)
    stg_f = ar.alloc([128, 2048], F32, "stg_f")
    kvT = ar.alloc([128, 32 * 66], BF16, "kvT")
    wvu_b = ar.alloc([64, 2048], BF16, "wvu_b")
    wo_b = ar.alloc([128, 2 * 1024], BF16, "wo_b")
    P_b = ar.alloc([128, 32 * 512], BF16, "P_b")
    on_b = ar.alloc([128, 2 * 512], BF16, "on_b")
    un_b = ar.alloc([64, 1024], BF16, "un_b")
    rz_bc = ar.alloc([64, 512], F32, "rz_bc")
    rzrow = ar.alloc([1, 512], F32, "rzrow")
    qt_l = ar.alloc([64, T_LOC], BF16, "qt_l")

    tw2 = E.dma("sync", stg_f[0:64, :], l0_wvuT[:, :], waits=[E.tok("v")])
    E.do("vector", lambda e: e.tensor_copy(out=wvu_b[:, :], in_=stg_f[0:64, :]),
         waits=[tw2], inc="v")
    t_wvu = E.tok("v")

    E.do("vector", lambda e: e.memset(kvT[:, :], 1.0), inc="v")
    tg1 = E.dma("gpsimd", kvfull[:, 0:T_LOC].bitcast(I32), ag0_out[0:64, :],
                waits=[tcc0])
    tg2 = E.dma("gpsimd", kvfull[:, T_LOC:T_FULL].bitcast(I32), ag0_out[64:128, :])
    t_kvf = E.do("vector", lambda e: e.engine_nop(), waits=[tg1, tg2], inc="v")
    for kt in range(32):
        reg = pA[:, (kt % 2) * 512:(kt % 2) * 512 + 32].bitcast(BF16)
        E.do("tensor", lambda e, kt=kt, reg=reg: e.transpose(
            reg, kvfull[0:64, kt * 128:(kt + 1) * 128], ident_b[0:64, 0:64]),
            waits=[t_kvf, E.tok("act")], inc="mm")
        E.do("scalar", lambda e, kt=kt, reg=reg: e.activation(
            out=kvT[:, kt * 66:kt * 66 + 64], in_=reg, func=AF.Copy),
            waits=[E.tok("mm")], inc="act")
    t_kvT = E.tok("act")
    E.flush()

    exp_toks = [None, None]
    for hh in range(8):
        for o in range(2):
            E.dma("sync", stg_f[:, o * 1024:(o + 1) * 1024],
                  l0_woT[hh * 256 + o * 128: hh * 256 + (o + 1) * 128, :],
                  waits=[E.tok("v"), E.tok("mm")] if o == 0 else ())
        t_ldo = E.tok("q_sync")
        t_wo = E.do("vector", lambda e: e.tensor_copy(
            out=wo_b[:, :], in_=stg_f[:, :]), waits=[t_ldo], inc="v")
        t_qtl = E.do("vector", lambda e, hh=hh: e.tensor_copy(
            out=qt_l[:, :], in_=qtil_all[(hh % 2) * 64:(hh % 2) * 64 + 64,
                                         (hh // 2) * T_LOC:(hh // 2 + 1) * T_LOC]),
            waits=[E.tok("mm")], inc="v")

        pend = None
        for ci, tc in enumerate(TC):
            uz = pA[0:65, 1024 + (ci % 2) * 512:1536 + (ci % 2) * 512]
            unb = un_b[:, (ci % 2) * 512:(ci % 2) * 512 + 512]
            exp_k = [None] * 32

            def emit_uz(kt, uz=uz, exp_k=exp_k, ew=()):
                E.do("tensor", lambda e, kt=kt, uz=uz: e.matmul(
                    uz, kvT[:, kt * 66:kt * 66 + 65],
                    P_b[:, kt * 512:(kt + 1) * 512],
                    start=(kt == 0), stop=(kt == 31)),
                    waits=[exp_k[kt]] + list(ew), inc="mm")

            for kt in range(32):
                sreg = pA[:, (kt % 2) * 512:(kt % 2) * 512 + 512]
                tm = E.do("tensor", lambda e, kt=kt, tc=tc, sreg=sreg: e.matmul(
                    sreg, kvfull[:, kt * 128:(kt + 1) * 128], qt_l[:, tc],
                    start=True, stop=True),
                    waits=[t_kvf, t_kvT, t_qtl, exp_toks[kt % 2]] +
                          ([E.tok("v")] if kt == 0 else []), inc="mm")
                exp_toks[kt % 2] = exp_k[kt] = E.do(
                    "scalar", lambda e, kt=kt, sreg=sreg: e.activation(
                        out=P_b[:, kt * 512:(kt + 1) * 512], in_=sreg,
                        func=AF.Exp, scale=1.0 / 16.0), waits=[tm], inc="act")
                if kt >= 2:
                    emit_uz(kt - 2, ew=[t_kvT, E.tok("v")] if kt == 2 else ())
            emit_uz(30)
            emit_uz(31)
            t_exp = E.tok("act")
            if dump == "attn1" and hh == 0 and ci == 0:
                E.do("gpsimd", lambda e: e.nop(), waits=[t_exp])
                E.dma("gpsimd", dbg_out[:, 4096:12288].bitcast(BF16), P_b[:, :])
                E.do("gpsimd", lambda e: e.nop(), waits=[E.tok("q_gpsimd")])
            t_uz = E.tok("mm")
            if pend is not None:
                pend()
            trz = E.do("vector", lambda e, uz=uz: e.reciprocal(
                out=rzrow[:, :], in_=uz[64:65, :]), waits=[t_uz], inc="v")
            tbm = E.do("tensor", lambda e: e.matmul(
                pB[0:64, 0:512], ones_f[0:1, 0:64], rzrow[:, :],
                start=True, stop=True), waits=[trz, E.tok("act")], inc="mm")
            trzb = E.do("scalar", lambda e: e.activation(
                out=rz_bc[:, :], in_=pB[0:64, 0:512], func=AF.Copy),
                waits=[tbm, E.tok("v")], inc="act")
            t_un = E.do("vector", lambda e, uz=uz, unb=unb: e.tensor_tensor(
                out=unb, in0=uz[0:64, :], in1=rz_bc[:, :], op=ALU.mult),
                waits=[trzb], inc="v")

            def pend(ci=ci, tc=tc, unb=unb, t_un=t_un, trzb=trzb):
                for dv in range(2):
                    oreg = pB[:, dv * 512:(dv + 1) * 512]
                    E.do("tensor", lambda e, dv=dv, oreg=oreg, unb=unb:
                         e.matmul(
                             oreg,
                             wvu_b[:, hh * 256 + dv * 128: hh * 256 + (dv + 1) * 128],
                             unb, start=True, stop=True),
                         waits=[t_un, t_wvu, trzb] if dv == 0 else (), inc="mm")
                    E.do("scalar", lambda e, dv=dv, oreg=oreg: e.activation(
                        out=on_b[:, dv * 512:(dv + 1) * 512], in_=oreg,
                        func=AF.Copy), waits=[E.tok("mm")], inc="act")
                t_on = E.tok("act")
                for d in range(NDT):
                    wreg = pB[:, 1024 + (d % 2) * 512: 1536 + (d % 2) * 512]
                    for o in range(2):
                        E.do("tensor", lambda e, d=d, o=o, wreg=wreg: e.matmul(
                            wreg,
                            wo_b[:, o * 1024 + d * 128: o * 1024 + (d + 1) * 128],
                            on_b[:, o * 512:(o + 1) * 512],
                            start=(o == 0), stop=(o == 1)),
                            waits=[t_on, t_wo, E.tok("v")] if o == 0 else (),
                            inc="mm")
                    E.do("vector", lambda e, d=d, tc=tc, wreg=wreg: e.tensor_add(
                        out=xsl(d, tc), in0=xsl(d, tc), in1=wreg),
                        waits=[E.tok("mm")], inc="v")
        pend()
    E.flush()

    dtap("l0", [lambda d=d: E.dma("gpsimd", dbg_out[:, d * 2048:(d + 1) * 2048], xsl(d))
                for d in range(NDT)])
    if upto == "l0":
        _finish(E, nc, y_out, xsl)
        return nc

    ar.reset(0)
    xn1_b = ar.alloc([128, NDT * T_LOC], BF16, "xn1_b")

    def xn1sl(d, t=None):
        if t is None:
            return xn1_b[:, d * T_LOC:(d + 1) * T_LOC]
        return xn1_b[:, d * T_LOC + t.start: d * T_LOC + t.stop]

    emit_stats()
    for ci, tc in enumerate(TC):
        tb = emit_ln_bcast(tc, "ab")
        for d in range(NDT):
            E.do("vector", lambda e, d=d, tc=tc: e.tensor_tensor(
                out=sq_a[:, :], in0=xsl(d, tc), in1=bc_a[:, :], op=ALU.mult),
                waits=[tb], inc="v")
            E.do("vector", lambda e, d=d, tc=tc: e.tensor_add(
                out=xn1sl(d, tc), in0=sq_a[:, :], in1=bc_b[:, :]), inc="v")
    t_xn1 = E.tok("v")
    for d in range(NDT):
        E.do("vector", lambda e, d=d: e.tensor_reduce(
            out=sq_a[:, 0:1] if d == 0 else sq_a[:, 1:2], in_=xn1sl(d),
            axis=AX.X, op=ALU.max, apply_absolute_value=True),
            waits=[t_xn1, E.tok("v")] if d == 0 else [E.tok("v")], inc="v")
        if d > 0:
            vch(lambda e: e.tensor_max(
                out=sq_a[:, 0:1], in0=sq_a[:, 0:1], in1=sq_a[:, 1:2]))
    tmx = E.do("vector", lambda e: e.engine_nop(), inc="v")
    tt1 = E.do("tensor", lambda e: e.transpose(
        pB[0:1, 0:128], sq_a[:, 0:1], ident[:, :]),
        waits=[tmx, E.tok("act")], inc="mm")
    tt2 = E.do("scalar", lambda e: e.activation(
        out=brow_s[0:1, 0:128], in_=pB[0:1, 0:128], func=AF.Copy),
        waits=[tt1], inc="act")
    E.do("vector", lambda e: e.tensor_reduce(
        out=scal[0:1, 0:1], in_=brow_s[0:1, 0:128], axis=AX.X, op=ALU.max),
        waits=[tt2], inc="v")
    tg1v = E.tok("v")
    tq1 = E.dma("gpsimd", ar1_in[:, :], scal[0:1, 0:1], waits=[tg1v])
    tar1 = E.do("gpsimd", lambda e: e.collective_compute(
        "AllReduce", ALU.max, replica_groups=[list(range(8))],
        ins=[ar1_in.ap().opt()], outs=[ar1_out.ap().opt()]),
        waits=[tq1], inc="cc")
    mark_aproj = ar.mark()
    wslab = ar.alloc([128, 8 * 128], BF16, "wslab")
    wstg = ar.alloc([128, 512], F32, "wstg")
    pstg = ar.alloc([128, 1024], BF16, "pstg")
    sqv = ar.alloc([128, 512], F32, "sqv")
    nsc = ar.alloc([4, 512], F32, "nsc")
    nv_t = ar.alloc([128, 16], F32, "nv_t")
    wv_b = ar.alloc([128, 8 * 1024], BF16, "wv_b")

    sqv_w = ar.alias(43072, [128, 2048], F32, "sqv_w")
    nsc_w = ar.alias(51264, [4, 2048], F32, "nsc_w")

    t_mult = [None] * 4

    def proj_padded(w_hbm, out_tile_fn):
        t_dma = [None, None]
        for ot in range(4):
            for d in range(NDT):
                tw = E.dma("sync", wstg[:, 0:128],
                           w_hbm[d * 128:(d + 1) * 128, ot * 128:(ot + 1) * 128],
                           waits=[E.tok("v")])
                E.do("vector", lambda e, d=d: e.tensor_copy(
                    out=wslab[:, d * 128:(d + 1) * 128], in_=wstg[:, 0:128]),
                    waits=[tw], inc="v")
            t_ws = E.tok("v")
            for ti, tc in enumerate(TC):
                for d in range(NDT):
                    E.do("tensor", lambda e, d=d, tc=tc, ti=ti: e.matmul(
                        pA[:, ti * 512:(ti + 1) * 512],
                        wslab[:, d * 128:(d + 1) * 128], xn1sl(d, tc),
                        start=(d == 0), stop=(d == NDT - 1)),
                        waits=([t_ws, t_xn1] +
                               ([t_mult[ti]] if t_mult[ti] else
                                [E.tok("act"), E.tok("v")]))
                        if d == 0 else (), inc="mm")
            tpp = E.tok("mm")
            tsq0 = E.do("scalar", lambda e: e.activation(
                out=sqv_w[:, 0:1024], in_=pA[:, 0:1024], func=AF.Square),
                waits=[tpp], inc="act")
            tsq = E.do("scalar", lambda e: e.activation(
                out=sqv_w[:, 1024:2048], in_=pA[:, 1024:2048], func=AF.Square),
                inc="act")
            for ti in range(4):
                E.do("tensor", lambda e, ti=ti: e.matmul(
                    pB[0:4, ti * 512:(ti + 1) * 512], Ekq[:, :],
                    sqv_w[:, ti * 512:(ti + 1) * 512], start=True, stop=True),
                    waits=[tsq0, tsq, E.tok("v")] if ti == 0 else (), inc="mm")
            tnm = E.tok("mm")
            E.do("scalar", lambda e: e.activation(
                out=nsc_w[:, :], in_=pB[0:4, 0:2048], func=AF.Sqrt),
                waits=[tnm], inc="act")
            E.do("vector", lambda e: e.engine_nop(), waits=[E.tok("act")],
                 inc="v")
            vch(lambda e: e.tensor_scalar_max(
                out=nsc_w[:, :], in0=nsc_w[:, :], scalar1=1e-12))
            vch(lambda e: e.reciprocal(out=nsc_w[:, :], in_=nsc_w[:, :]))
            trc = E.tok("v")
            for ti in range(4):
                tex = E.do("tensor", lambda e, ti=ti: e.matmul(
                    pB[:, ti * 512:(ti + 1) * 512], Eexp[:, :],
                    nsc_w[:, ti * 512:(ti + 1) * 512], start=True, stop=True),
                    waits=([trc, E.tok("act")] if ti == 0 else []), inc="mm")
                tev = E.do("scalar", lambda e, ti=ti: e.activation(
                    out=sqv_w[:, ti * 512:(ti + 1) * 512],
                    in_=pB[:, ti * 512:(ti + 1) * 512], func=AF.Copy),
                    waits=[tex], inc="act")
                t_mult[ti] = tno = E.do(
                    "vector", lambda e, ti=ti: e.tensor_tensor(
                        out=pstg[:, (ti % 2) * 512:(ti % 2) * 512 + 512],
                        in0=pA[:, ti * 512:(ti + 1) * 512],
                        in1=sqv_w[:, ti * 512:(ti + 1) * 512], op=ALU.mult),
                    waits=[tev] + ([t_dma[ti % 2]] if t_dma[ti % 2] else []),
                    inc="v")
                t_dma[ti % 2] = out_tile_fn(ot, ti, tno)

    def q_out(ot, ti, tok):
        return E.dma(
            "gpsimd", qn_hbm[ot * 128:(ot + 1) * 128, ti * 512:(ti + 1) * 512],
            pstg[:, (ti % 2) * 512:(ti % 2) * 512 + 512], waits=[tok])

    def k_out(ot, ti, tok):
        return E.dma(
            "gpsimd", ag_kn_in[ot * 128:(ot + 1) * 128, ti * 256:(ti + 1) * 256],
            pstg[:, (ti % 2) * 512:(ti % 2) * 512 + 512].bitcast(I32),
            waits=[tok])

    proj_padded(l1_wqp, q_out)
    proj_padded(l1_wkp, k_out)

    for d in range(NDT):
        tw = E.dma("sync", wstg[:, 0:512],
                   l1_wv[d * 128:(d + 1) * 128, 0:512], waits=[E.tok("v")])
        E.do("vector", lambda e, d=d: e.tensor_copy(
            out=wv_b[:, d * 1024:d * 1024 + 512], in_=wstg[:, 0:512]),
            waits=[tw], inc="v")
        tw = E.dma("sync", wstg[:, 0:512],
                   l1_wv[d * 128:(d + 1) * 128, 512:1024], waits=[E.tok("v")])
        E.do("vector", lambda e, d=d: e.tensor_copy(
            out=wv_b[:, d * 1024 + 512:(d + 1) * 1024], in_=wstg[:, 0:512]),
            waits=[tw], inc="v")
    t_wv = E.tok("v")
    sqv_v = ar.alias(59456, [128, 1024], F32, "sqv_v")
    for vt in range(16):
        for dvc in range(2):
            reg = pA[:, 1024 + dvc * 512:1536 + dvc * 512]
            for d in range(NDT):
                E.do("tensor", lambda e, d=d, vt=vt, dvc=dvc, reg=reg: e.matmul(
                    reg, xn1_b[:, d * T_LOC + vt * 128:d * T_LOC + (vt + 1) * 128],
                    wv_b[:, d * 1024 + dvc * 512:d * 1024 + (dvc + 1) * 512],
                    start=(d == 0), stop=(d == NDT - 1)),
                    waits=[t_wv, t_xn1, E.tok("act"), E.tok("v")]
                    if (dvc == 0 and d == 0) else (), inc="mm")
        tpp = E.tok("mm")
        tsq = E.do("scalar", lambda e: e.activation(
            out=sqv_v[:, :], in_=pA[:, 1024:2048], func=AF.Square),
            waits=[tpp], inc="act")
        E.do("vector", lambda e: e.tensor_reduce(
            out=nv_t[:, 0:16],
            in_=sqv_v[:, :].rearrange("p (a b) -> p a b", b=64),
            axis=AX.X, op=ALU.add),
            waits=[tsq, E.tok("v")], inc="v")
        E.do("scalar", lambda e: e.activation(
            out=pstg[:, 0:1024], in_=pA[:, 1024:2048], func=AF.Copy),
            waits=[E.tok("v")], inc="act")
        E.do("vector", lambda e: e.engine_nop(), waits=[E.tok("act")], inc="v")
        vch(lambda e: e.tensor_copy(out=sqv[:, 0:16], in_=nv_t[:, :]))
        E.do("scalar", lambda e: e.activation(
            out=sqv[:, 0:16], in_=sqv[:, 0:16], func=AF.Sqrt),
            waits=[E.tok("v")], inc="act")
        E.do("vector", lambda e: e.engine_nop(), waits=[E.tok("act")], inc="v")
        vch(lambda e: e.tensor_scalar_max(
            out=sqv[:, 0:16], in0=sqv[:, 0:16], scalar1=1e-12))
        vch(lambda e: e.reciprocal(out=sqv[:, 0:16], in_=sqv[:, 0:16]))
        tno = vch(lambda e: e.tensor_tensor(
            out=pstg[:, :].rearrange("p (a b) -> p a b", b=64),
            in0=pstg[:, :].rearrange("p (a b) -> p a b", b=64),
            in1=sqv[:, 0:16].unsqueeze(2).broadcast_to([128, 16, 64]),
            op=ALU.mult))
        E.dma("gpsimd", ag_vn_in[vt * 128:(vt + 1) * 128, :],
              pstg[:, :].bitcast(I32), waits=[tno])
    t_vdone = E.tok("q_gpsimd")
    tag_k = E.do("gpsimd", lambda e: e.collective_compute(
        "AllGather", ALU.bypass,
        replica_groups=[[0, 1], [2, 3], [4, 5], [6, 7]],
        ins=[ag_kn_in.ap().opt()], outs=[ag_kn_out.ap().opt()]),
        waits=[t_vdone], inc="cc")
    tag_v = E.do("gpsimd", lambda e: e.collective_compute(
        "AllGather", ALU.bypass,
        replica_groups=[[0, 1], [2, 3], [4, 5], [6, 7]],
        ins=[ag_vn_in.ap().opt()], outs=[ag_vn_out.ap().opt()]),
        inc="cc")
    ar.reset(mark_aproj)
    E.flush()

    def sc(i):
        return scal[0:1, i:i + 1]

    def bcast_col(src, col):
        reg = pB[:, 1536:1544]
        tm = E.do("tensor", lambda e, src=src, reg=reg: e.matmul(
            reg[:, 0:1], ones_f[0:1, :], src, start=True, stop=True),
            waits=(E.tok("v"), E.tok("act")), inc="mm")
        return E.do("scalar", lambda e, col=col, reg=reg: e.activation(
            out=scol[:, col:col + 1], in_=reg[:, 0:1], func=AF.Copy),
            waits=[tm], inc="act")

    r1_b = ar.alloc([128, NDT * T_LOC], BF16, "r1_b")
    m_stg6 = ar.mark()
    stg_w = ar.alloc([128, 2 * 512], F32, "stg_w")
    stg_q = ar.alloc([128, 512], I32, "stg_q")
    w_big = ar.alloc([128, 8 * 512], BF16, "w_big")
    ub_s = ar.alias(m_stg6, [128, 2048], BF16, "ub_s")

    E.do("gpsimd", lambda e: e.nop(), waits=[tar1])
    tld = E.dma("gpsimd", sc(0), ar1_out[:, 0:1])
    E.do("vector", lambda e: e.engine_nop(), waits=[tld], inc="v")
    bcast_col(sc(0), 16)
    E.do("vector", lambda e: e.engine_nop(), waits=[E.tok("act")], inc="v")
    vch(lambda e: e.tensor_scalar(
        out=scol[:, 17:18], in0=scol[:, 16:17], scalar1=1.0 / 32767.0,
        scalar2=1e-12, op0=ALU.mult, op1=ALU.max))
    vch(lambda e: e.reciprocal(out=scol[:, 0:1], in_=scol[:, 17:18]))
    t_c1 = E.tok("v")
    stg_r1 = ar.alias(m_stg6, [128, 2048], I32, "stg_r1")
    for d in range(NDT):
        E.do("vector", lambda e, d=d: e.tensor_scalar(
            out=stg_r1[:, :], in0=xn1sl(d), scalar1=scol[:, 0:1],
            scalar2=None, op0=ALU.mult),
            waits=[t_c1] if d == 0 else (), inc="v")
        E.do("vector", lambda e, d=d: e.tensor_copy(
            out=r1_b[:, d * T_LOC:(d + 1) * T_LOC], in_=stg_r1[:, :]), inc="v")
    t_r1 = E.tok("v")
    if dump == "l1sc":
        E.do("gpsimd", lambda e: e.nop(), waits=[t_r1, E.tok("act")])
        E.dma("gpsimd", dbg_out[4:5, 0:32], scal[0:1, 0:32])
        E.do("gpsimd", lambda e: e.nop(), waits=[E.tok("q_gpsimd")])
    sTu = ar.alias(0, [32, 3072], F32, "sTu")
    rTu = ar.alias(12 * 1024 + 32, [32, 3072], F32, "rTu")

    def lpbq_scales(w_rm, sT, rT, nrt, first_wait=()):
        t_red = [None, None]
        t_tt = [None, None]
        for rt in range(nrt):
            pp = rt % 2
            hs = slice(pp * 1024, (pp + 1) * 1024)
            tw = E.dma("sync", stg_f2[:, hs], w_rm[rt * 128:(rt + 1) * 128, :],
                       waits=((E.tok("v"), E.tok("mm")) + tuple(first_wait))
                       if rt < 2 else [t_red[pp]])
            t_red[pp] = E.do("vector", lambda e, pp=pp, hs=hs: e.tensor_reduce(
                out=sq_a[:, pp * 32:(pp + 1) * 32],
                in_=stg_f2[:, hs].rearrange("p (a b) -> p a b", b=32),
                axis=AX.X, op=ALU.max, apply_absolute_value=True),
                waits=[tw], inc="v")
            E.do("vector", lambda e, pp=pp: e.tensor_scalar(
                out=sq_a[:, 64 + pp * 32:64 + (pp + 1) * 32],
                in0=sq_a[:, pp * 32:(pp + 1) * 32], scalar1=1.0 / 7.0,
                scalar2=1e-12, op0=ALU.mult, op1=ALU.max),
                waits=[E.tok("v")] + ([t_tt[pp]] if t_tt[pp] else []), inc="v")
            tv = E.tok("v")
            t_tt[pp] = tt = E.do("tensor", lambda e, pp=pp: e.transpose(
                pB[0:32, 0:128], sq_a[:, 64 + pp * 32:64 + (pp + 1) * 32],
                ident[:, :]), waits=[tv, E.tok("act")], inc="mm")
            E.do("scalar", lambda e, rt=rt, sT=sT: e.activation(
                out=sT[0:32, rt * 128:(rt + 1) * 128], in_=pB[0:32, 0:128],
                func=AF.Copy), waits=[tt], inc="act")
        tse = E.tok("act")
        return E.do("vector", lambda e, sT=sT, rT=rT: e.reciprocal(
            out=rT[:, :], in_=sT[:, :]), waits=[tse], inc="v")

    E.do("vector", lambda e: e.memset(scol[:, 1:3], 0.0), inc="v")

    def l1_mm_pass(wT_hbm, sT, rT, hbm_dst, maxcol, t_sc):
        for oc in range(6):
            for it in range(NDT):
                tw = E.dma("sync", stg_w[:, 0:512],
                           wT_hbm[it * 128:(it + 1) * 128,
                                  oc * 512:(oc + 1) * 512],
                           waits=[E.tok("v"), E.tok("act"), E.tok("q_gpsimd")]
                           if it == 0 else [E.tok("v")])
                rreg = pB[:, 1024:1536]
                sreg2 = pB[:, 1536:2048]
                tm1 = E.do("tensor", lambda e, it=it, oc=oc, rT=rT, rreg=rreg:
                           e.matmul(rreg, Esel[:, it * 128:(it + 1) * 128],
                                    rT[:, oc * 512:(oc + 1) * 512],
                                    start=True, stop=True),
                           waits=[t_sc, E.tok("v"), E.tok("act")], inc="mm")
                tm2 = E.do("tensor", lambda e, it=it, oc=oc, sT=sT, sreg2=sreg2:
                           e.matmul(sreg2, Esel[:, it * 128:(it + 1) * 128],
                                    sT[:, oc * 512:(oc + 1) * 512],
                                    start=True, stop=True), inc="mm")
                E.do("vector", lambda e, rreg=rreg: e.tensor_tensor(
                    out=stg_w[:, 512:1024], in0=stg_w[:, 0:512], in1=rreg,
                    op=ALU.mult), waits=[tm1, tw], inc="v")
                E.do("vector", lambda e: e.tensor_copy(
                    out=stg_q[:, :], in_=stg_w[:, 512:1024]), inc="v")
                tse2 = E.do("scalar", lambda e, sreg2=sreg2: e.activation(
                    out=stg_w[:, 512:1024], in_=sreg2, func=AF.Copy),
                    waits=[tm2, E.tok("v")], inc="act")
                E.do("vector", lambda e, it=it: e.tensor_tensor(
                    out=w_big[:, it * 512:(it + 1) * 512], in0=stg_q[:, :],
                    in1=stg_w[:, 512:1024], op=ALU.mult),
                    waits=[tse2], inc="v")
            t_wb = E.tok("v")
            for otl in range(4):
                ot = oc * 4 + otl
                for ci, tc in enumerate(TC):
                    reg = pA[:, (ci % 2) * 512:(ci % 2) * 512 + 512]
                    for it in range(NDT):
                        E.do("tensor", lambda e, it=it, otl=otl, tc=tc, reg=reg:
                             e.matmul(reg,
                                      w_big[:, it * 512 + otl * 128:
                                            it * 512 + (otl + 1) * 128],
                                      r1_b[:, it * T_LOC + tc.start:
                                           it * T_LOC + tc.stop],
                                      start=(it == 0), stop=(it == NDT - 1)),
                             waits=([t_wb, t_r1, E.tok("act")]
                                    if it == 0 else ()), inc="mm")
                    E.do("scalar", lambda e, tc=tc, reg=reg: e.activation(
                        out=ub_s[:, tc], in_=reg, func=AF.Copy),
                        waits=[E.tok("mm")], inc="act")
                tev = E.tok("act")
                E.do("vector", lambda e: e.tensor_reduce(
                    out=sq_a[:, 0:1], in_=ub_s[:, :], axis=AX.X, op=ALU.max,
                    apply_absolute_value=True), waits=[tev, E.tok("v")], inc="v")
                vch(lambda e, maxcol=maxcol: e.tensor_max(
                    out=scol[:, maxcol:maxcol + 1],
                    in0=scol[:, maxcol:maxcol + 1], in1=sq_a[:, 0:1]))
                E.dma("gpsimd", hbm_dst[ot * 128:(ot + 1) * 128, :], ub_s[:, :],
                      waits=[tev])
                E.do("scalar", lambda e: e.nop(), waits=[E.tok("q_gpsimd")])
        return E.tok("v")

    t_sc_u = lpbq_scales(l1_up_rm, sTu, rTu, 24, first_wait=[t_r1])
    t_up = l1_mm_pass(l1_upT, sTu, rTu, up_hbm, 1, t_sc_u)
    if dump == "l1sc":
        E.do("gpsimd", lambda e: e.nop(), waits=[t_up, E.tok("act"), E.tok("mm")])
        E.dma("gpsimd", dbg_out[5:6, 0:32], scal[0:1, 0:32])
        E.do("gpsimd", lambda e: e.nop(), waits=[E.tok("q_gpsimd")])
    t_sc_g = lpbq_scales(l1_gate_rm, sTu, rTu, 24, first_wait=[t_up])
    t_gate = l1_mm_pass(l1_gateT, sTu, rTu, gate_hbm, 2, t_sc_g)

    def partition_max_to(srccol, dst, wait):
        tt1 = E.do("tensor", lambda e, srccol=srccol: e.transpose(
            pB[0:1, 0:128], srccol, ident[:, :]),
            waits=[wait, E.tok("act")], inc="mm")
        tt2 = E.do("scalar", lambda e: e.activation(
            out=brow_s[0:1, 0:128], in_=pB[0:1, 0:128], func=AF.Copy),
            waits=[tt1], inc="act")
        return E.do("vector", lambda e, dst=dst: e.tensor_reduce(
            out=dst, in_=brow_s[0:1, 0:128], axis=AX.X, op=ALU.max),
            waits=[tt2], inc="v")

    tv4 = partition_max_to(scol[:, 1:2], sc(4), t_gate)
    tq3 = E.dma("gpsimd", ar2_in[:, :], sc(4), waits=[tv4])
    tar2a = E.do("gpsimd", lambda e: e.collective_compute(
        "AllReduce", ALU.max, replica_groups=[list(range(8))],
        ins=[ar2_in.ap().opt()], outs=[ar2_out.ap().opt()]),
        waits=[tq3], inc="cc")
    tv5 = partition_max_to(scol[:, 2:3], sc(5), t_gate)
    tq3b = E.dma("gpsimd", ar2b_in[:, :], sc(5), waits=[tv5])
    tar2 = E.do("gpsimd", lambda e: e.collective_compute(
        "AllReduce", ALU.max, replica_groups=[list(range(8))],
        ins=[ar2b_in.ap().opt()], outs=[ar2b_out.ap().opt()]),
        waits=[tq3b], inc="cc")
    E.flush()
    if dump == "l1ug":
        E.do("gpsimd", lambda e: e.nop(), waits=[E.tok("cc")])
        E.dma("gpsimd", dbg_out[0:1, 0:1], ar2_out[:, :])
        E.dma("gpsimd", dbg_out[0:1, 1:2], ar2b_out[:, :])
        E.dma("gpsimd", dbg_out[:, 4096:5120].bitcast(BF16), up_hbm[0:128, :])
        E.dma("gpsimd", dbg_out[:, 5120:6144].bitcast(BF16),
              gate_hbm[20 * 128:21 * 128, :])
        E.do("gpsimd", lambda e: e.nop(), waits=[E.tok("q_gpsimd")])

    ta4 = E.dma("gpsimd", sc(4), ar2_out[:, :], waits=[tar2a])
    tld = E.dma("gpsimd", sc(5), ar2b_out[:, :], waits=[tar2])
    E.do("vector", lambda e: e.engine_nop(), waits=[ta4, tld], inc="v")
    bcast_col(sc(4), 18)
    bcast_col(sc(5), 19)
    E.do("vector", lambda e: e.engine_nop(), waits=[E.tok("act")], inc="v")
    vch(lambda e: e.tensor_tensor(
        out=scol[:, 18:19], in0=scol[:, 18:19], in1=scol[:, 17:18], op=ALU.mult))
    vch(lambda e: e.tensor_scalar(
        out=scol[:, 20:21], in0=scol[:, 18:19], scalar1=1.0 / 32767.0,
        scalar2=1e-12, op0=ALU.mult, op1=ALU.max))
    vch(lambda e: e.reciprocal(out=scol[:, 3:4], in_=scol[:, 20:21]))
    vch(lambda e: e.tensor_tensor(
        out=scol[:, 3:4], in0=scol[:, 3:4], in1=scol[:, 17:18], op=ALU.mult))
    vch(lambda e: e.tensor_tensor(
        out=scol[:, 19:20], in0=scol[:, 19:20], in1=scol[:, 17:18], op=ALU.mult))
    vch(lambda e: e.tensor_scalar(
        out=scol[:, 5:6], in0=scol[:, 19:20], scalar1=1.0 / 32767.0,
        scalar2=1e-12, op0=ALU.mult, op1=ALU.max))
    vch(lambda e: e.reciprocal(out=scol[:, 4:5], in_=scol[:, 5:6]))
    vch(lambda e: e.tensor_tensor(
        out=scol[:, 4:5], in0=scol[:, 4:5], in1=scol[:, 17:18], op=ALU.mult))
    t_sc2 = E.tok("v")

    if dump == "l1sc":
        E.do("gpsimd", lambda e: e.nop(), waits=[E.tok("v"), E.tok("act")])
        E.dma("gpsimd", dbg_out[6:7, 0:32], scal[0:1, 0:32])
        E.do("gpsimd", lambda e: e.nop(), waits=[E.tok("q_gpsimd")])
    pa_g2 = [ar.alias(0, [128, 2048], BF16, "pa_g2a"),
             ar.alias(4096, [128, 2048], BF16, "pa_g2b")]
    pb_u2 = [ar.alias(8192, [128, 2048], BF16, "pb_u2a"),
             ar.alias(12288, [128, 2048], BF16, "pb_u2b")]
    pa_sg2 = [ar.alias(16384, [128, 2048], F32, "pa_sg2a"),
              ar.alias(24576, [128, 2048], F32, "pa_sg2b")]
    pa_rg2 = [ar.alias(32768, [128, 2048], I32, "pa_rg2a"),
              ar.alias(40960, [128, 2048], I32, "pa_rg2b")]
    pa_m32 = [ar.alias(49152, [128, 2048], F32, "pa_m32a"),
              ar.alias(57344, [128, 2048], F32, "pa_m32b")]
    pa_q2 = ar.alias(65536, [128, 2048], I32, "pa_q2")
    pa_m3 = ar.alias(24704, [128, 2048], F32, "pa_m3")
    stg_q2 = ar.alias(41 * 1024 + 128, [128, 2048], I32, "stg_q2")

    t_g_rd = [None, None]
    t_m3st = [None, None]
    for rt in range(24):
        pp = rt % 2
        twg = E.dma("sync", pa_g2[pp][:, :], gate_hbm[rt * 128:(rt + 1) * 128, :],
                    waits=([E.tok("mm"), E.tok("v"), E.tok("act"),
                            E.tok("q_gpsimd")] if rt == 0 else
                           ([t_g_rd[pp]] if t_g_rd[pp] else [])))
        t_g_rd[pp] = E.do("vector", lambda e, pp=pp: e.tensor_scalar(
            out=pa_rg2[pp][:, :], in0=pa_g2[pp][:, :], scalar1=scol[:, 4:5],
            scalar2=None, op0=ALU.mult), waits=[twg, t_sc2], inc="v")
        tsg = E.do("scalar", lambda e, pp=pp: e.activation(
            out=pa_sg2[pp][:, :], in_=pa_rg2[pp][:, :], func=AF.Sigmoid,
            scale=scol[:, 5:6]), waits=[t_g_rd[pp]], inc="act")
        E.do("vector", lambda e, pp=pp: e.tensor_scalar(
            out=pa_q2[:, :], in0=pa_sg2[pp][:, :], scalar1=65536.0,
            scalar2=65535.0, op0=ALU.mult, op1=ALU.min),
            waits=[tsg] + ([E.tok("q_gpsimd")] if rt == 0 else []), inc="v")
        E.do("vector", lambda e, pp=pp: e.tensor_tensor(
            out=pa_m32[pp][:, :], in0=pa_rg2[pp][:, :], in1=pa_q2[:, :],
            op=ALU.mult), waits=[t_m3st[pp]] if t_m3st[pp] else (), inc="v")
        tva = E.tok("v")
        E.do("vector", lambda e, rt=rt, pp=pp: e.tensor_reduce(
            out=sq_a[:, rt:rt + 1], in_=pa_m32[pp][:, :], axis=AX.X, op=ALU.max,
            apply_absolute_value=True), inc="v")
        t_m3st[pp] = E.dma("gpsimd", m3_hbm[rt * 128:(rt + 1) * 128, :],
                           pa_m32[pp][:, :], waits=[tva])
    E.do("vector", lambda e: e.tensor_reduce(
        out=sq_b[:, 0:1], in_=sq_a[:, 0:24], axis=AX.X, op=ALU.max), inc="v")
    tt1 = E.do("tensor", lambda e: e.transpose(
        pB[0:1, 0:128], sq_b[:, 0:1], ident[:, :]),
        waits=[E.tok("v"), E.tok("act")], inc="mm")
    tt2 = E.do("scalar", lambda e: e.activation(
        out=brow_s[0:1, 0:128], in_=pB[0:1, 0:128], func=AF.Copy),
        waits=[tt1], inc="act")
    E.do("vector", lambda e: e.tensor_reduce(
        out=sc(10), in_=brow_s[0:1, 0:128], axis=AX.X, op=ALU.max),
        waits=[tt2], inc="v")
    tq4 = E.dma("gpsimd", ar3_in[:, :], sc(10), waits=[E.tok("v")])
    tar3 = E.do("gpsimd", lambda e: e.collective_compute(
        "AllReduce", ALU.max, replica_groups=[list(range(8))],
        ins=[ar3_in.ap().opt()], outs=[ar3_out.ap().opt()]),
        waits=[tq4], inc="cc")

    tld = E.dma("gpsimd", sc(10), ar3_out[:, :], waits=[tar3])
    E.do("vector", lambda e: e.engine_nop(), waits=[tld], inc="v")
    bcast_col(sc(10), 21)
    E.do("vector", lambda e: e.engine_nop(), waits=[E.tok("act")], inc="v")
    vch(lambda e: e.tensor_tensor(
        out=scol[:, 22:23], in0=scol[:, 21:22], in1=scol[:, 5:6], op=ALU.mult))
    vch(lambda e: e.tensor_scalar(
        out=scol[:, 22:23], in0=scol[:, 22:23], scalar1=1.0 / 65536.0,
        scalar2=None, op0=ALU.mult))
    vch(lambda e: e.tensor_scalar(
        out=scol[:, 23:24], in0=scol[:, 22:23], scalar1=1.0 / 32767.0,
        scalar2=1e-12, op0=ALU.mult, op1=ALU.max))
    vch(lambda e: e.reciprocal(out=scol[:, 24:25], in_=scol[:, 23:24]))
    vch(lambda e: e.tensor_scalar(
        out=scol[:, 7:8], in0=scol[:, 5:6], scalar1=1.0 / 65536.0,
        scalar2=None, op0=ALU.mult))
    vch(lambda e: e.tensor_tensor(
        out=scol[:, 7:8], in0=scol[:, 7:8], in1=scol[:, 24:25], op=ALU.mult))
    t_c3 = E.tok("v")

    t_m4st = [t_m3st[0], t_m3st[1]]
    t_b_rd = [None, None]
    t_q_rd = [None, None]
    for rt in range(24):
        pp = rt % 2
        tw3 = E.dma("sync", pa_sg2[pp][:, :], m3_hbm[rt * 128:(rt + 1) * 128, :],
                    waits=([E.tok("v"), E.tok("q_gpsimd")] if rt < 2 else
                           [t_q_rd[pp]]))
        twu = E.dma("sync", pb_u2[pp][:, :], up_hbm[rt * 128:(rt + 1) * 128, :],
                    waits=[t_b_rd[pp]] if t_b_rd[pp] else ())
        t_b_rd[pp] = E.do("vector", lambda e, pp=pp: e.tensor_scalar(
            out=pa_rg2[pp][:, :], in0=pb_u2[pp][:, :], scalar1=scol[:, 3:4],
            scalar2=None, op0=ALU.mult), waits=[twu], inc="v")
        t_q_rd[pp] = E.do("vector", lambda e, pp=pp: e.tensor_scalar(
            out=pa_q2[:, :], in0=pa_sg2[pp][:, :], scalar1=scol[:, 7:8],
            scalar2=None, op0=ALU.mult), waits=[tw3, t_c3], inc="v")
        E.do("vector", lambda e, pp=pp: e.tensor_tensor(
            out=pa_m32[pp][:, :], in0=pa_q2[:, :], in1=pa_rg2[pp][:, :],
            op=ALU.mult), waits=[t_m4st[pp]], inc="v")
        E.do("vector", lambda e, rt=rt, pp=pp: e.tensor_reduce(
            out=sq_a[:, rt:rt + 1], in_=pa_m32[pp][:, :], axis=AX.X, op=ALU.max,
            apply_absolute_value=True), inc="v")
        t_m4st[pp] = E.dma("gpsimd", m4_hbm[rt * 128:(rt + 1) * 128, :],
                           pa_m32[pp][:, :], waits=[E.tok("v")])
    E.do("vector", lambda e: e.tensor_reduce(
        out=sq_b[:, 0:1], in_=sq_a[:, 0:24], axis=AX.X, op=ALU.max), inc="v")
    tt1 = E.do("tensor", lambda e: e.transpose(
        pB[0:1, 0:128], sq_b[:, 0:1], ident[:, :]),
        waits=[E.tok("v"), E.tok("act")], inc="mm")
    tt2 = E.do("scalar", lambda e: e.activation(
        out=brow_s[0:1, 0:128], in_=pB[0:1, 0:128], func=AF.Copy),
        waits=[tt1], inc="act")
    E.do("vector", lambda e: e.tensor_reduce(
        out=sc(14), in_=brow_s[0:1, 0:128], axis=AX.X, op=ALU.max),
        waits=[tt2], inc="v")
    tq5 = E.dma("gpsimd", ar4_in[:, :], sc(14), waits=[E.tok("v")])
    tar4 = E.do("gpsimd", lambda e: e.collective_compute(
        "AllReduce", ALU.max, replica_groups=[list(range(8))],
        ins=[ar4_in.ap().opt()], outs=[ar4_out.ap().opt()]),
        waits=[tq5], inc="cc")

    sTd = ar.alias(0, [32, 3 * 1024], F32, "sTd")
    rTd_c = [ar.alias(19072, [32, 1024], F32, "rTd0"),
             ar.alias(32896, [32, 1024], F32, "rTd1"),
             ar.alias(36992, [32, 1024], F32, "rTd2")]
    stg_d2a = ar.alias(24576, [128, 2048], F32, "stg_d2a")
    stg_d2b = ar.alias(40960, [128, 1024], F32, "stg_d2b")
    stg_d = [(stg_f2[:, 0:2048], stg_w[:, 0:1024]),
             (stg_d2a[:, :], stg_d2b[:, :])]
    t_redd = [None, None]
    t_ttd = [None, None]
    for rt in range(NDT):
        pp = rt % 2
        sA, sB = stg_d[pp]
        tw = E.dma("sync", sA, l1_down_rm[rt * 128:(rt + 1) * 128, 0:2048],
                   waits=[E.tok("v"), E.tok("mm")] if rt < 2 else [t_redd[pp]])
        tw2b = E.dma("sync", sB, l1_down_rm[rt * 128:(rt + 1) * 128, 2048:3072])
        E.do("vector", lambda e, pp=pp, sA=sA: e.tensor_reduce(
            out=sq_a[:, pp * 96:pp * 96 + 64],
            in_=sA.rearrange("p (a b) -> p a b", b=32),
            axis=AX.X, op=ALU.max, apply_absolute_value=True),
            waits=[tw], inc="v")
        t_redd[pp] = E.do("vector", lambda e, pp=pp, sB=sB: e.tensor_reduce(
            out=sq_a[:, pp * 96 + 64:pp * 96 + 96],
            in_=sB.rearrange("p (a b) -> p a b", b=32),
            axis=AX.X, op=ALU.max, apply_absolute_value=True),
            waits=[tw2b], inc="v")
        E.do("vector", lambda e, pp=pp: e.tensor_scalar(
            out=sq_a[:, 192 + pp * 96:192 + (pp + 1) * 96],
            in0=sq_a[:, pp * 96:(pp + 1) * 96], scalar1=1.0 / 7.0,
            scalar2=1e-12, op0=ALU.mult, op1=ALU.max),
            waits=[E.tok("v")] + ([t_ttd[pp]] if t_ttd[pp] else []), inc="v")
        tv = E.tok("v")
        t_ttd[pp] = tt = E.do("tensor", lambda e, pp=pp: e.transpose(
            pB[0:96, 0:128], sq_a[:, 192 + pp * 96:192 + (pp + 1) * 96],
            ident[:, :]), waits=[tv, E.tok("act")], inc="mm")
        for g3 in range(3):
            E.do("scalar", lambda e, rt=rt, g3=g3: e.activation(
                out=sTd[0:32, g3 * 1024 + rt * 128: g3 * 1024 + (rt + 1) * 128],
                in_=pB[g3 * 32:(g3 + 1) * 32, 0:128], func=AF.Copy),
                waits=[tt], inc="act")
    tse = E.tok("act")
    for g3 in range(3):
        E.do("vector", lambda e, g3=g3: e.reciprocal(
            out=rTd_c[g3][:, :], in_=sTd[:, g3 * 1024:(g3 + 1) * 1024]),
            waits=[tse] if g3 == 0 else (), inc="v")
    t_sc_d = E.tok("v")

    tld = E.dma("gpsimd", sc(14), ar4_out[:, :], waits=[tar4])
    E.do("vector", lambda e: e.engine_nop(), waits=[tld], inc="v")
    bcast_col(sc(14), 25)
    E.do("vector", lambda e: e.engine_nop(), waits=[E.tok("act")], inc="v")
    vch(lambda e: e.tensor_tensor(
        out=scol[:, 26:27], in0=scol[:, 23:24], in1=scol[:, 20:21], op=ALU.mult))
    vch(lambda e: e.tensor_tensor(
        out=scol[:, 27:28], in0=scol[:, 26:27], in1=scol[:, 25:26], op=ALU.mult))
    vch(lambda e: e.tensor_scalar(
        out=scol[:, 10:11], in0=scol[:, 27:28], scalar1=1.0 / 32767.0,
        scalar2=1e-12, op0=ALU.mult, op1=ALU.max))
    vch(lambda e: e.reciprocal(out=scol[:, 9:10], in_=scol[:, 10:11]))
    vch(lambda e: e.tensor_tensor(
        out=scol[:, 9:10], in0=scol[:, 9:10], in1=scol[:, 26:27], op=ALU.mult))
    t_c4 = E.tok("v")
    t_s4 = t_c4

    if dump == "l1sc":
        E.do("gpsimd", lambda e: e.nop(), waits=[E.tok("v"), E.tok("act")])
        E.dma("gpsimd", dbg_out[0:1, 0:32], scol[0:1, 0:32])
        E.dma("gpsimd", dbg_out[0:1, 4096:6144], m3_hbm[0:1, :])
        E.dma("gpsimd", dbg_out[0:1, 6144:8192].bitcast(I32) if False else dbg_out[1:2, 4096:6144], r2_hbm[0:1, :])
        E.dma("gpsimd", dbg_out[2:3, 4096:6144], m4_hbm[0:1, :])
        E.do("gpsimd", lambda e: e.nop(), waits=[E.tok("q_gpsimd")])

    o_bc4 = ar.alias(59 * 1024, [128, 4 * 2048], BF16, "o_bc4")
    wd_b = ar.alias(50 * 1024, [128, 4 * 1024], BF16, "wd_b")
    stg_wC = ar.alias(12 * 1024 + 512, [128, 1024], F32, "stg_wC")
    stg_qC = ar.alias(16 * 1024 + 640, [128, 512], I32, "stg_qC")
    t_mmC = None
    for oc in range(6):
        for itl in range(4):
            rt = oc * 4 + itl
            tw = E.dma("sync", pa_m3[:, :], m4_hbm[rt * 128:(rt + 1) * 128, :],
                       waits=[E.tok("v"), E.tok("mm"), E.tok("q_gpsimd")]
                       if oc == 0 and itl == 0 else [E.tok("v")])
            E.do("vector", lambda e: e.tensor_scalar(
                out=stg_q2[:, :], in0=pa_m3[:, :], scalar1=scol[:, 9:10],
                scalar2=None, op0=ALU.mult), waits=[tw, t_c4, t_s4], inc="v")
            E.do("vector", lambda e, itl=itl: e.tensor_copy(
                out=o_bc4[:, itl * 2048:(itl + 1) * 2048], in_=stg_q2[:, :]),
                waits=[t_mmC] if (t_mmC and itl == 0) else (), inc="v")
        for itl in range(4):
            it = oc * 4 + itl
            for hc in range(2):
                tw = E.dma("sync", stg_wC[:, 0:512],
                           l1_downT[it * 128:(it + 1) * 128,
                                    hc * 512:(hc + 1) * 512],
                           waits=[E.tok("v")])
                rreg = pB[:, 1024:1536]
                sreg2 = pB[:, 1536:2048]
                tm1 = E.do("tensor", lambda e, it=it, hc=hc, rreg=rreg:
                           e.matmul(rreg, Esel[:, (it % 8) * 128:(it % 8 + 1) * 128],
                                    rTd_c[it // 8][:, hc * 512:(hc + 1) * 512],
                                    start=True, stop=True),
                           waits=[t_sc_d, E.tok("v"), E.tok("act")], inc="mm")
                tm2 = E.do("tensor", lambda e, it=it, hc=hc, sreg2=sreg2:
                           e.matmul(sreg2, Esel[:, (it % 8) * 128:(it % 8 + 1) * 128],
                                    sTd[:, (it // 8) * 1024 + hc * 512:
                                        (it // 8) * 1024 + (hc + 1) * 512],
                                    start=True, stop=True), inc="mm")
                E.do("vector", lambda e, rreg=rreg: e.tensor_tensor(
                    out=stg_wC[:, 512:1024], in0=stg_wC[:, 0:512], in1=rreg,
                    op=ALU.mult), waits=[tm1, tw], inc="v")
                E.do("vector", lambda e: e.tensor_copy(
                    out=stg_qC[:, :], in_=stg_wC[:, 512:1024]), inc="v")
                tse2 = E.do("scalar", lambda e, sreg2=sreg2: e.activation(
                    out=stg_wC[:, 512:1024], in_=sreg2, func=AF.Copy),
                    waits=[tm2, E.tok("v")], inc="act")
                E.do("vector", lambda e, itl=itl, hc=hc: e.tensor_tensor(
                    out=wd_b[:, itl * 1024 + hc * 512:itl * 1024 + (hc + 1) * 512],
                    in0=stg_qC[:, :], in1=stg_wC[:, 512:1024], op=ALU.mult),
                    waits=[tse2], inc="v")
        t_wd = E.tok("v")
        for ci, tc in enumerate(TC):
            for d in range(NDT):
                zreg = pA[:, 512 + (d % 2) * 512:1024 + (d % 2) * 512] if False \
                    else pA[:, (d % 2) * 512:(d % 2) * 512 + 512]
                for itl in range(4):
                    E.do("tensor", lambda e, d=d, itl=itl, tc=tc, zreg=zreg:
                         e.matmul(zreg,
                                  wd_b[:, itl * 1024 + d * 128:
                                       itl * 1024 + (d + 1) * 128],
                                  o_bc4[:, itl * 2048 + tc.start:
                                        itl * 2048 + tc.stop],
                                  start=(itl == 0), stop=(itl == 3)),
                         waits=[t_wd, E.tok("v")] if itl == 0 else (), inc="mm")
                E.do("vector", lambda e, d=d, tc=tc, zreg=zreg:
                     e.scalar_tensor_tensor(
                         out=xsl(d, tc), in0=zreg, scalar=scol[:, 10:11],
                         in1=xsl(d, tc), op0=ALU.mult, op1=ALU.add),
                     waits=[E.tok("mm")], inc="v")
        t_mmC = E.tok("mm")
    E.flush()
    if dump == "l1ffn":
        for d in range(NDT):
            E.dma("gpsimd", dbg_out[:, d * 2048:(d + 1) * 2048], xsl(d))
        E.do("gpsimd", lambda e: e.nop(), waits=[E.tok("q_gpsimd")])
    if upto == "l1ffn":
        _finish(E, nc, y_out, xsl)
        return nc

    for d in range(NDT):
        E.dma("gpsimd", x_spill[d * 128:(d + 1) * 128, :], xsl(d),
              waits=[E.tok("v"), E.tok("mm")] if d == 0 else ())
    t_spill = E.tok("q_gpsimd")

    ar.reset(0)
    qn_s = ar.alloc([128, T_LOC], BF16, "qn_s")
    kn_s = ar.alloc([128, T_FULL], BF16, "kn_s")
    P2q = ar.alloc([128, 4 * 1024], BF16, "P2q")
    rstg2 = ar.alloc([128, 2 * 1024], BF16, "rstg2")
    sqv8 = ar.alloc([128, 512], F32, "sqv8")
    fsc = ar.alloc([2, 512], F32, "fsc")
    fsc2 = ar.alloc([2, 512], F32, "fsc2")
    fex = ar.alloc([128, 512], F32, "fex")
    on_st = ar.alloc([128, 512], BF16, "on_st")
    vn_b = nc.alloc_sbuf_tensor_at("vn_b", [128, 32 * 1024], BF16,
                                   offset=nc.lookup_mloc(xs).addr)

    for r in range(2):
        for vt in range(16):
            E.dma("sync",
                  vn_b[:, (r * 16 + vt) * 1024:(r * 16 + vt + 1) * 1024]
                  .bitcast(I32),
                  ag_vn_out[r * 2048 + vt * 128: r * 2048 + (vt + 1) * 128, :],
                  waits=[tag_v, t_spill] if (r == 0 and vt == 0) else ())
    t_kvld = E.tok("q_sync")
    E.do("vector", lambda e: e.engine_nop(), waits=[t_kvld], inc="v")

    wout_b = ar.alloc([128, 8 * 1024], BF16, "wout_b")
    onld = ar.alloc([128, 8 * 512], BF16, "onld")
    wstg8b = ar.alloc([128, 1024], F32, "wstg8b")
    onld2 = [onld, ar.alias(4096, [128, 8 * 512], BF16, "onld_b")]
    for kt in range(NDT):
        tw = E.dma("gpsimd", wstg8b[:, :], l1_wout[kt * 128:(kt + 1) * 128, :],
                   waits=[E.tok("v")])
        E.do("vector", lambda e, kt=kt: e.tensor_copy(
            out=wout_b[:, kt * 1024:(kt + 1) * 1024], in_=wstg8b[:, :]),
            waits=[tw], inc="v")
    t_wo8 = E.tok("v")

    QC4 = [slice(i * 512, (i + 1) * 512) for i in range(4)]
    t_onst = None
    relu_toks8 = [None, None]
    rstg_toks = [None, None]
    pv_toks = [None, None, None, None]
    for pr in range(8):
        slab = pr // 2
        ha = slab * 4 + (pr % 2) * 2
        sa = 32 * (ha % 4)
        sb = sa + 32
        if pr % 2 == 0:
            E.dma("sync", qn_s[:, :], qn_hbm[slab * 128:(slab + 1) * 128, :],
                  waits=[E.tok("mm"), E.tok("v")])
            for r in range(2):
                E.dma("sync",
                      kn_s[:, r * T_LOC:(r + 1) * T_LOC].bitcast(I32),
                      ag_kn_out[r * 512 + slab * 128:
                                r * 512 + (slab + 1) * 128, :],
                      waits=[tag_k] if (pr == 0 and r == 0) else ())
            t_ld8v = E.tok("q_sync")
            E.do("vector", lambda e: e.engine_nop(), waits=[t_ld8v], inc="v")
        t_ld8 = E.tok("v")
        for ci, qc in enumerate(QC4):
            ov = pB[:, 512:1024]
            sq_toks = [None] * 4

            def emit_pv(j, ha=ha, ov=ov, sq_toks=sq_toks):
                sl = j % 4
                for hj in range(2):
                    pv_toks[sl] = E.do(
                        "tensor", lambda e, j=j, hj=hj, sl=sl, ha=ha, ov=ov:
                        e.matmul(
                            ov[hj * 64:(hj + 1) * 64, :],
                            vn_b[:, j * 1024 + (ha + hj) * 64:
                                 j * 1024 + (ha + hj + 1) * 64],
                            P2q[:, sl * 1024 + hj * 512:
                                sl * 1024 + (hj + 1) * 512],
                            start=(j == 0), stop=(j == 31),
                            tile_position=(0, hj * 64)),
                        waits=[sq_toks[sl]] if hj == 0 else (), inc="mm")

            for kt in range(32):
                par = kt % 2
                sl = kt % 4
                for hj, sp in ((0, sa), (1, sb)):
                    reg = pA[:, par * 1024 + hj * 512:
                             par * 1024 + (hj + 1) * 512]
                    E.do("tensor", lambda e, kt=kt, sp=sp, qc=qc, reg=reg:
                         e.matmul(
                             reg,
                             kn_s[sp:sp + 32, kt * 128:(kt + 1) * 128],
                             qn_s[sp:sp + 32, qc],
                             start=True, stop=True, tile_position=(sp, 0)),
                         waits=(([t_ld8] if kt == 0 else []) +
                                ([relu_toks8[par]] if relu_toks8[par] else []))
                         if hj == 0 else (), inc="mm")
                tmm = E.tok("mm")
                relu_toks8[par] = tr = E.do(
                    "scalar", lambda e, par=par: e.activation(
                        out=rstg2[:, par * 1024:(par + 1) * 1024],
                        in_=pA[:, par * 1024:(par + 1) * 1024], func=AF.Relu),
                    waits=[tmm] + ([rstg_toks[par]] if rstg_toks[par] else []),
                    inc="act")
                E.do("vector", lambda e, par=par, sl=sl: e.tensor_tensor(
                    out=P2q[:, sl * 1024:sl * 1024 + 512],
                    in0=rstg2[:, par * 1024:par * 1024 + 512],
                    in1=rstg2[:, par * 1024:par * 1024 + 512], op=ALU.mult),
                    waits=[tr] + ([pv_toks[sl]] if pv_toks[sl] else []),
                    inc="v")
                rstg_toks[par] = sq_toks[sl] = E.do(
                    "vector", lambda e, par=par, sl=sl: e.tensor_tensor(
                        out=P2q[:, sl * 1024 + 512:(sl + 1) * 1024],
                        in0=rstg2[:, par * 1024 + 512:(par + 1) * 1024],
                        in1=rstg2[:, par * 1024 + 512:(par + 1) * 1024],
                        op=ALU.mult), inc="v")
                if kt >= 2:
                    emit_pv(kt - 2)
            emit_pv(30)
            emit_pv(31)
            t_pv = E.tok("mm")
            tsq = E.do("scalar", lambda e, ov=ov: e.activation(
                out=sqv8[:, :], in_=ov, func=AF.Square), waits=[t_pv], inc="act")
            tnm = E.do("tensor", lambda e: e.matmul(
                pB[0:2, 0:512], E64[:, :], sqv8[:, :], start=True, stop=True),
                waits=[tsq, E.tok("v")], inc="mm")
            E.do("scalar", lambda e: e.activation(
                out=fsc[:, :], in_=pB[0:2, 0:512], func=AF.Sqrt),
                waits=[tnm], inc="act")
            E.do("vector", lambda e: e.engine_nop(), waits=[E.tok("act")], inc="v")
            vch(lambda e: e.tensor_scalar_max(
                out=fsc2[:, :], in0=fsc[:, :], scalar1=1e-12))
            vch(lambda e: e.reciprocal(out=fsc2[:, :], in_=fsc2[:, :]))
            tth = E.do("scalar", lambda e: e.activation(
                out=fsc[:, :], in_=fsc[:, :], func=AF.Tanh),
                waits=[E.tok("v")], inc="act")
            E.do("vector", lambda e: e.engine_nop(), waits=[tth], inc="v")
            vch(lambda e: e.tensor_tensor(
                out=fsc[:, :], in0=fsc[:, :], in1=fsc2[:, :], op=ALU.mult))
            tf = E.tok("v")
            tfe = E.do("tensor", lambda e: e.matmul(
                pB[:, 1024:1536], E2[:, :], fsc[:, :], start=True, stop=True),
                waits=[tf, E.tok("act")], inc="mm")
            tfc = E.do("scalar", lambda e: e.activation(
                out=fex[:, :], in_=pB[:, 1024:1536], func=AF.Copy),
                waits=[tfe], inc="act")
            ton = E.do("vector", lambda e, ov=ov: e.tensor_tensor(
                out=on_st[:, :], in0=ov, in1=fex[:, :], op=ALU.mult),
                waits=[tfc] + ([t_onst] if t_onst else []), inc="v")
            t_onst = E.dma("gpsimd",
                           outn_hbm[pr * 128:(pr + 1) * 128,
                                    ci * 512:(ci + 1) * 512],
                           on_st[:, :], waits=[ton])
    E.flush()

    for d in range(NDT):
        E.dma("sync", xsl(d), x_spill[d * 128:(d + 1) * 128, :],
              waits=[E.tok("q_gpsimd"), E.tok("mm"), E.tok("v")]
              if d == 0 else ())
    t_xre = E.tok("q_sync")
    t_mmw = [None, None]
    for ci, tc in enumerate(TC):
        ob = onld2[ci % 2]
        for kt in range(NDT):
            E.dma("sync", ob[:, kt * 512:(kt + 1) * 512],
                  outn_hbm[kt * 128:(kt + 1) * 128, tc],
                  waits=([E.tok("v")] +
                         ([t_mmw[ci % 2]] if t_mmw[ci % 2] else [E.tok("mm")]))
                  if kt == 0 else ())
        t_on8 = E.tok("q_sync")
        for d in range(NDT):
            reg = pB[:, 1024 + (d % 2) * 512:1536 + (d % 2) * 512]
            for kt in range(NDT):
                E.do("tensor", lambda e, d=d, kt=kt, reg=reg, ob=ob: e.matmul(
                    reg, wout_b[:, kt * 1024 + d * 128: kt * 1024 + (d + 1) * 128],
                    ob[:, kt * 512:(kt + 1) * 512],
                    start=(kt == 0), stop=(kt == NDT - 1)),
                    waits=[t_on8, t_wo8, t_xre, E.tok("v")]
                    if kt == 0 else (), inc="mm")
            E.do("vector", lambda e, d=d, tc=tc, reg=reg: e.tensor_add(
                out=xsl(d, tc), in0=xsl(d, tc), in1=reg),
                waits=[E.tok("mm")], inc="v")
        t_mmw[ci % 2] = E.tok("mm")
    E.flush()

    _finish(E, nc, y_out, xsl)
    return nc

def _finish(E, nc, y_out, xsl):
    for d in range(NDT):
        E.dma("gpsimd", y_out[d * 128:(d + 1) * 128, :], xsl(d),
              waits=[E.tok("v"), E.tok("act"), E.tok("mm")] if d == 0 else ())
    tst = E.tok("q_gpsimd")
    E.do("gpsimd", lambda e: e.nop(), waits=[tst])
    E.flush()


_CACHE = {}


def _consts():
    ident = np.eye(128, dtype=np.float32)
    Ekq = np.zeros((128, 4), np.float32)
    for p in range(128):
        if p % 32 < 16:
            Ekq[p, p // 32] = 1.0
    Eexp = np.zeros((4, 128), np.float32)
    for p in range(128):
        Eexp[p // 32, p] = 1.0
    E64 = np.zeros((128, 2), np.float32)
    E64[np.arange(128), np.arange(128) // 64] = 1.0
    E2 = np.zeros((2, 128), np.float32)
    E2[np.arange(128) // 64, np.arange(128)] = 1.0
    Esel = np.zeros((32, 1024), np.float32)
    for it in range(8):
        for p in range(128):
            Esel[4 * it + p // 32, it * 128 + p] = 1.0
    return ident, Ekq, Eexp, E64, E2, Esel


def _colmajor(v):
    return np.ascontiguousarray(v.reshape(-1, 128).T)


def kernel(L0_attn_x, L0_attn_Wq, L0_attn_Wkv_down, L0_attn_Wk_up, L0_attn_Wv_up,
           L0_attn_Wo, L0_ffn_ln_gamma, L0_ffn_ln_beta, L0_ffn_fc1_q,
           L0_ffn_fc1_scales, L0_ffn_fc1_bias, L0_ffn_fc2_q, L0_ffn_fc2_scales,
           L0_ffn_fc2_bias, L1_attn_Wq, L1_attn_Wkv, L1_attn_Wout,
           L1_ffn_w_gate, L1_ffn_w_up, L1_ffn_w_down, _dump=None, _upto=None):
    f32 = np.float32
    asc = np.ascontiguousarray

    ident, Ekq, Eexp, E64, E2, Esel = _consts()
    shared = {
        "l0_wq": asc(np.asarray(L0_attn_Wq, f32)),
        "l0_wkvdT": asc(np.asarray(L0_attn_Wkv_down, f32).T),
        "l0_wku": asc(np.asarray(L0_attn_Wk_up, f32)),
        "l0_wvuT": asc(np.asarray(L0_attn_Wv_up, f32).T),
        "l0_woT": asc(np.asarray(L0_attn_Wo, f32).T),
        "l0_fc1q": asc(np.asarray(L0_ffn_fc1_q, np.int32)),
        "l0_fc2q": asc(np.asarray(L0_ffn_fc2_q, np.int32)),
        "l0_fc1s": asc(np.asarray(L0_ffn_fc1_scales, f32)),
        "l0_fc2s": asc(np.asarray(L0_ffn_fc2_scales, f32)),
        "ln0_g": _colmajor(np.asarray(L0_ffn_ln_gamma, f32)),
        "ln0_b": _colmajor(np.asarray(L0_ffn_ln_beta, f32)),
        "fc1b_c": _colmajor(np.asarray(L0_ffn_fc1_bias, f32)),
        "fc2b_c": _colmajor(np.asarray(L0_ffn_fc2_bias, f32)),
        "l1_wout": asc(np.asarray(L1_attn_Wout, f32)),
        "l1_upT": asc(np.asarray(L1_ffn_w_up, f32).T),
        "l1_gateT": asc(np.asarray(L1_ffn_w_gate, f32).T),
        "l1_downT": asc(np.asarray(L1_ffn_w_down, f32).T),
        "l1_up_rm": asc(np.asarray(L1_ffn_w_up, f32)),
        "l1_gate_rm": asc(np.asarray(L1_ffn_w_gate, f32)),
        "l1_down_rm": asc(np.asarray(L1_ffn_w_down, f32)),
        "c_ident": ident, "c_Ekq": Ekq, "c_Eexp": Eexp, "c_E64": E64,
        "c_E2": E2, "c_Esel": Esel,
    }
    Wq1 = np.asarray(L1_attn_Wq, f32)
    Wkv1 = np.asarray(L1_attn_Wkv, f32)
    wqp = np.zeros((D, 512), f32)
    wkp = np.zeros((D, 512), f32)
    wv = np.zeros((D, 1024), f32)
    for h in range(16):
        wqp[:, h * 32:h * 32 + 16] = Wq1[:, h * 16:(h + 1) * 16]
        wkp[:, h * 32:h * 32 + 16] = Wkv1[:, h * 80:h * 80 + 16]
        wv[:, h * 64:(h + 1) * 64] = Wkv1[:, h * 80 + 16:(h + 1) * 80]
    shared["l1_wqp"] = wqp
    shared["l1_wkp"] = wkp
    shared["l1_wv"] = wv

    x = np.asarray(L0_attn_x, f32)
    in_maps = []
    for c in range(8):
        b, half = c // 2, c % 2
        m = dict(shared)
        m["x_in"] = asc(x[b, half * T_LOC:(half + 1) * T_LOC, :].T)
        in_maps.append(m)

    key = (_dump, _upto)
    if key not in _CACHE:
        _CACHE[key] = build_program(dump=_dump, upto=_upto)
    res = run_bass_kernel_spmd(_CACHE[key], in_maps, core_ids=list(range(8)))

    out = np.empty((4, T_FULL, D), f32)
    for c in range(8):
        b, half = c // 2, c % 2
        out[b, half * T_LOC:(half + 1) * T_LOC, :] = res.results[c]["y_out"].T
    if _dump:
        return out, [res.results[c].get("dbg_out") for c in range(8)]
    return out


# version 17
# speedup vs baseline: 1.4566x; 1.0022x over previous
import numpy as np
import concourse.bass as bass
import concourse.mybir as mybir
from concourse.bass_utils import run_bass_kernel_spmd

F32 = mybir.dt.float32
BF16 = mybir.dt.bfloat16
I32 = mybir.dt.int32
AF = mybir.ActivationFunctionType
ALU = mybir.AluOpType
AX = mybir.AxisListType

T_LOC = 2048
T_FULL = 4096
D = 1024
NDT = D // 128
EPS_LN = 1e-5
ENGINES = ("sync", "tensor", "vector", "scalar", "gpsimd")
TC = [slice(i * 512, (i + 1) * 512) for i in range(4)]
KC8 = [slice(i * 512, (i + 1) * 512) for i in range(8)]


class Emitter:

    def __init__(self, nc):
        self.nc = nc
        self.q = {e: [] for e in ENGINES}
        self.sems = {}
        self.cnt = {}
        self._ctx = []

    def sem(self, name):
        if name not in self.sems:
            cm = self.nc.semaphore(name)
            self.sems[name] = cm.__enter__()
            self._ctx.append(cm)
            self.cnt[name] = 0
        return self.sems[name]

    def do(self, engine, fn, waits=(), inc=None, inck=1):
        tok = None
        if inc is not None:
            self.sem(inc)
            self.cnt[inc] += inck
            tok = (inc, self.cnt[inc])
        self.q[engine].append((fn, tuple(w for w in waits if w), inc, inck))
        return tok

    def dma(self, engine, out, in_, waits=()):
        qn = "q_" + engine
        return self.do(engine, lambda e, o=out, i=in_: e.dma_start(out=o, in_=i),
                       waits=waits, inc=qn, inck=16)

    def tok(self, semname):
        return (semname, self.cnt.get(semname, 0))

    def guard(self):
        return (self.tok("v"), self.tok("act"), self.tok("gv"))

    def flush(self):
        nc, q = self.nc, self.q
        self.q = {e: [] for e in ENGINES}
        sems = self.sems
        used = [e for e in ENGINES if q[e]]
        if not used:
            return

        def runner(ename):
            def run(eng):
                seen = {}
                for fn, waits, inc, inck in q[ename]:
                    for sname, val in waits:
                        if val > seen.get(sname, 0):
                            eng.wait_ge(sems[sname], val)
                            seen[sname] = val
                    ins = fn(eng)
                    if inc is not None:
                        ins.then_inc(sems[inc], inck)
            return run

        with nc.Block() as block:
            for e in used:
                getattr(block, e)(runner(e))


class Arena:
    def __init__(self, nc, name, nbytes):
        self.nc = nc
        self.backing = nc.alloc_sbuf_tensor(name, [128, nbytes], mybir.dt.uint8)
        self.base = nc.lookup_mloc(self.backing).addr
        self.off = 0
        self.nbytes = nbytes
        self.n = 0

    def alloc(self, shape, dtype, name=None):
        per_part = int(np.prod(shape[1:])) * mybir.dt.size(dtype)
        off = (self.off + 31) & ~31
        assert off + per_part <= self.nbytes, \
            f"arena overflow: {off}+{per_part} > {self.nbytes} ({name})"
        self.off = off + per_part
        self.n += 1
        return self.nc.alloc_sbuf_tensor_at(name or f"ar{self.n}", shape, dtype,
                                            offset=self.base + off)

    def mark(self):
        return self.off

    def reset(self, mark=0):
        self.off = mark

    def alias(self, at, shape, dtype, name):
        per_part = int(np.prod(shape[1:])) * mybir.dt.size(dtype)
        at = (at + 31) & ~31
        assert at + per_part <= self.nbytes
        self.n += 1
        return self.nc.alloc_sbuf_tensor_at(name, shape, dtype,
                                            offset=self.base + at)


def build_program(dump=None, upto=None):
    nc = bass.Bass("TRN2", target_bir_lowering=False)
    E = Emitter(nc)

    DT = nc.dram_tensor
    x_in = DT("x_in", [D, T_LOC], F32, kind="ExternalInput")
    l0_wq = DT("l0_wq", [2048, D], F32, kind="ExternalInput")
    l0_wkvdT = DT("l0_wkvdT", [D, 64], F32, kind="ExternalInput")
    l0_wku = DT("l0_wku", [2048, 64], F32, kind="ExternalInput")
    l0_wvuT = DT("l0_wvuT", [64, 2048], F32, kind="ExternalInput")
    l0_woT = DT("l0_woT", [2048, D], F32, kind="ExternalInput")
    l0_fc1q = DT("l0_fc1q", [D, 4096], I32, kind="ExternalInput")
    l0_fc2q = DT("l0_fc2q", [4096, D], I32, kind="ExternalInput")
    l0_fc1s = DT("l0_fc1s", [8, 4096], F32, kind="ExternalInput")
    l0_fc2s = DT("l0_fc2s", [32, D], F32, kind="ExternalInput")
    ln0_g = DT("ln0_g", [128, 8], F32, kind="ExternalInput")
    ln0_b = DT("ln0_b", [128, 8], F32, kind="ExternalInput")
    fc1b_c = DT("fc1b_c", [128, 32], F32, kind="ExternalInput")
    fc2b_c = DT("fc2b_c", [128, 8], F32, kind="ExternalInput")
    l1_wqp = DT("l1_wqp", [D, 512], F32, kind="ExternalInput")
    l1_wkp = DT("l1_wkp", [D, 512], F32, kind="ExternalInput")
    l1_wv = DT("l1_wv", [D, 1024], F32, kind="ExternalInput")
    l1_wout = DT("l1_wout", [1024, D], F32, kind="ExternalInput")
    l1_upT = DT("l1_upT", [D, 3072], F32, kind="ExternalInput")
    l1_gateT = DT("l1_gateT", [D, 3072], F32, kind="ExternalInput")
    l1_downT = DT("l1_downT", [3072, D], F32, kind="ExternalInput")
    l1_up_rm = DT("l1_up_rm", [3072, D], F32, kind="ExternalInput")
    l1_gate_rm = DT("l1_gate_rm", [3072, D], F32, kind="ExternalInput")
    l1_down_rm = DT("l1_down_rm", [D, 3072], F32, kind="ExternalInput")
    c_ident = DT("c_ident", [128, 128], F32, kind="ExternalInput")
    c_Ekq = DT("c_Ekq", [128, 4], F32, kind="ExternalInput")
    c_Eexp = DT("c_Eexp", [4, 128], F32, kind="ExternalInput")
    c_E64 = DT("c_E64", [128, 2], F32, kind="ExternalInput")
    c_E2 = DT("c_E2", [2, 128], F32, kind="ExternalInput")
    c_Esel = DT("c_Esel", [32, 1024], F32, kind="ExternalInput")

    y_out = DT("y_out", [D, T_LOC], F32, kind="ExternalOutput")
    dbg_out = DT("dbg_out", [128, 16384], F32, kind="ExternalOutput") if dump else None

    ag0_in = DT("ag0_in", [64, T_LOC // 2], I32)
    ag0_out = DT("ag0_out", [128, T_LOC // 2], I32)
    qn_hbm = DT("qn_hbm", [512, T_LOC], BF16)
    outn_hbm = DT("outn_hbm", [1024, T_LOC], BF16)
    ag_kn_in = DT("ag_kn_in", [512, T_LOC // 2], I32)
    ag_kn_out = DT("ag_kn_out", [1024, T_LOC // 2], I32)
    ag_vn_in = DT("ag_vn_in", [T_LOC, 512], I32)
    ag_vn_out = DT("ag_vn_out", [2 * T_LOC, 512], I32)
    ar2_in = DT("ar2_in", [1, 1], F32)
    ar2_out = DT("ar2_out", [1, 1], F32)
    ar2b_in = DT("ar2b_in", [1, 1], F32)
    ar2b_out = DT("ar2b_out", [1, 1], F32)
    ar1_in = DT("ar1_in", [1, 1], F32)
    ar1_out = DT("ar1_out", [1, 1], F32)
    ar3_in = DT("ar3_in", [1, 1], F32)
    ar3_out = DT("ar3_out", [1, 1], F32)
    ar4_in = DT("ar4_in", [1, 1], F32)
    ar4_out = DT("ar4_out", [1, 1], F32)
    up_hbm = DT("up_hbm", [3072, T_LOC], BF16)
    gate_hbm = DT("gate_hbm", [3072, T_LOC], BF16)
    r2_hbm = DT("r2_hbm", [3072, T_LOC], I32)
    m3_hbm = DT("m3_hbm", [3072, T_LOC], F32)
    m4_hbm = DT("m4_hbm", [3072, T_LOC], F32)
    x_spill = DT("x_spill", [D, T_LOC], F32)

    A = nc.alloc_sbuf_tensor
    xs = A("xs", [128, NDT * T_LOC], F32)
    stat_r = A("stat_r", [1, 2 * T_LOC], F32)
    srw = A("srw", [1, 1536], F32)
    ident = A("ident", [128, 128], F32)
    ident_b = A("ident_b", [128, 128], BF16)
    ones_f = A("ones_f", [128, 128], F32)
    ones_b = A("ones_b", [128, 1], BF16)
    Ekq = A("Ekq", [128, 4], F32)
    Eexp = A("Eexp", [4, 128], F32)
    E64 = A("E64", [128, 2], F32)
    E2 = A("E2", [2, 128], F32)
    Esel = A("Esel", [32, 1024], F32)
    g0col = A("g0col", [128, 8], F32)
    b0col = A("b0col", [128, 8], F32)
    fc1bcol = A("fc1bcol", [128, 32], F32)
    fc2bcol = A("fc2bcol", [128, 8], F32)
    scal = A("scal", [1, 32], F32)
    brow_s = A("brow_s", [1, 1024], F32)
    half_f = A("half_f", [1, 128], F32)
    scol = A("scol", [128, 32], F32)
    kvfull = A("kvfull", [64, T_FULL], BF16)
    qtil_all = A("qtil_all", [128, 4 * T_LOC], BF16)
    sq_a = A("sq_a", [128, 512], F32)
    sq_b = A("sq_b", [128, 512], F32)
    bc_a = A("bc_a", [128, 512], F32)
    bc_b = A("bc_b", [128, 512], F32)

    pA = nc.alloc_psum_tensor("pA", [128, 2048], F32)
    pB = nc.alloc_psum_tensor("pB", [128, 2048], F32)

    rem = (nc.sbuf_bytes_remaining - 512) & ~31
    ar = Arena(nc, "arena", rem)

    def xsl(d, t=None):
        if t is None:
            return xs[:, d * T_LOC:(d + 1) * T_LOC]
        return xs[:, d * T_LOC + t.start: d * T_LOC + t.stop]

    def MU(t=None):
        return stat_r[0:1, t.start:t.stop] if t else stat_r[0:1, 0:T_LOC]

    def M2(t=None):
        return stat_r[0:1, T_LOC + t.start:T_LOC + t.stop] if t else stat_r[0:1, T_LOC:2 * T_LOC]

    def dtap(label, emitfns):
        if dump == label:
            for fn in emitfns:
                fn()
            E.do("gpsimd", lambda e: e.nop(), waits=[E.tok("q_gpsimd")])

    for d in range(NDT):
        E.dma("sync", xsl(d), x_in[d * 128:(d + 1) * 128, :])
    for dst, src in ((ident, c_ident), (Ekq, c_Ekq), (Eexp, c_Eexp),
                     (E64, c_E64), (E2, c_E2), (Esel, c_Esel), (g0col, ln0_g), (b0col, ln0_b),
                     (fc1bcol, fc1b_c), (fc2bcol, fc2b_c)):
        E.dma("sync", dst[:, :], src[:, :])
    t_loads = E.tok("q_sync")
    E.do("vector", lambda e: e.memset(ones_f[:, :], 1.0))
    E.do("vector", lambda e: e.memset(ones_b[:, :], 1.0))
    E.do("vector", lambda e: e.memset(half_f[:, :], 0.5), inc="v")
    E.do("vector", lambda e: e.tensor_scalar_mul(
        out=fc2bcol[:, :], in0=fc2bcol[:, :], scalar1=0.5),
        waits=[t_loads], inc="v")
    E.do("vector", lambda e: e.tensor_copy(out=ident_b[:, :], in_=ident[:, :]),
         inc="v")
    E.do("sync", lambda e: e.nop(), waits=[t_loads])
    E.flush()

    def emit_stats():
        sqs = [sq_a, sq_b]
        for ci, tc in enumerate(TC):
            s1 = pA[0:1, 0:512]
            s2 = pA[0:1, 512:1024]
            for d in range(NDT):
                tsq = E.do("scalar",
                           lambda e, d=d, tc=tc, s=sqs[d % 2]:
                           e.activation(out=s[:, :], in_=xsl(d, tc), func=AF.Square),
                           waits=[E.tok("mm")] if d >= 2 else (), inc="act")
                E.do("tensor",
                     lambda e, d=d, tc=tc, s1=s1: e.matmul(
                         s1, ones_f[:, 0:1], xsl(d, tc),
                         start=(d == 0), stop=(d == NDT - 1)),
                     waits=[E.tok("v")] if d == 0 else (), inc="mm")
                E.do("tensor",
                     lambda e, d=d, s2=s2, s=sqs[d % 2]: e.matmul(
                         s2, ones_f[:, 0:1], s[:, :],
                         start=(d == 0), stop=(d == NDT - 1)),
                     waits=[tsq], inc="mm")
            tmm = E.tok("mm")
            E.do("vector", lambda e, tc=tc, s1=s1: e.tensor_scalar(
                out=MU(tc), in0=s1, scalar1=1.0 / D,
                scalar2=None, op0=ALU.mult), waits=[tmm], inc="v")
            E.do("vector", lambda e, tc=tc, s2=s2: e.tensor_scalar(
                out=M2(tc), in0=s2, scalar1=1.0 / D,
                scalar2=None, op0=ALU.mult), inc="v")
        return E.tok("v")

    def emit_ln_bcast(tc, kind):
        g = E.guard()
        s0 = srw[0:1, 0:512]
        s1 = srw[0:1, 512:1024]
        s2 = srw[0:1, 1024:1536]
        E.do("vector", lambda e: e.tensor_tensor(
            out=s0, in0=MU(tc), in1=MU(tc), op=ALU.mult), waits=g)
        E.do("vector", lambda e: e.tensor_sub(out=s2, in0=M2(tc), in1=s0))
        E.do("vector", lambda e: e.tensor_scalar_add(out=s0, in0=s2, scalar1=EPS_LN),
             inc="v")
        E.do("scalar", lambda e: e.activation(out=s0, in_=s0, func=AF.Sqrt),
             waits=[E.tok("v")], inc="act")
        E.do("vector", lambda e: e.reciprocal(out=s0, in_=s0),
             waits=[E.tok("act")], inc="v")
        E.do("vector", lambda e: e.scalar_tensor_tensor(
            out=s1, in0=MU(tc), scalar=-1.0, in1=s0, op0=ALU.mult, op1=ALU.mult),
            inc="v")
        if kind == "acbc":
            E.do("vector", lambda e: e.tensor_tensor(out=s2, in0=s2, in1=s0, op=ALU.mult))
            E.do("vector", lambda e: e.tensor_tensor(out=s2, in0=s2, in1=s0, op=ALU.mult))
            E.do("vector", lambda e: e.tensor_scalar_add(out=s2, in0=s2, scalar1=EPS_LN),
                 inc="v")
            E.do("scalar", lambda e: e.activation(out=s2, in_=s2, func=AF.Sqrt),
                 waits=[E.tok("v")], inc="act")
            E.do("vector", lambda e: e.reciprocal(out=s2, in_=s2),
                 waits=[E.tok("act")], inc="v")
            E.do("vector", lambda e: e.tensor_tensor(out=s0, in0=s0, in1=s2, op=ALU.mult))
            E.do("vector", lambda e: e.tensor_tensor(out=s1, in0=s1, in1=s2, op=ALU.mult),
                 inc="v")
        tlast = E.tok("v")
        out = None
        for row, dest, reg in ((s0, bc_a, pB[:, 1024:1536]),
                               (s1, bc_b, pB[:, 1536:2048])):
            tm = E.do("tensor", lambda e, row=row, reg=reg: e.matmul(
                reg, ones_f[0:1, :], row, start=True, stop=True),
                waits=g + (tlast,), inc="mm")
            out = E.do("scalar", lambda e, reg=reg, dest=dest: e.activation(
                out=dest[:, :], in_=reg, func=AF.Copy), waits=[tm], inc="act")
        return out

    def vch(fn):
        return E.do("vector", fn, waits=[E.tok("v")], inc="v")

    emit_stats()
    E.flush()
    dtap("stats0", [lambda: E.dma("gpsimd", dbg_out[0:1, 0:4096], stat_r[:, :])])

    base_mark = ar.mark()
    h_b = ar.alloc([128, NDT * T_LOC], BF16, "h_b")
    mark3 = ar.mark()
    xn_b = ar.alloc([128, NDT * T_LOC], BF16, "xn_b")
    wkvd_f = ar.alloc([128, 64], F32, "wkvd_f")
    wkvd_b = ar.alloc([128, NDT * 64], BF16, "wkvd_b")
    m_kv = ar.mark()
    kv_own = ar.alloc([64, T_LOC], BF16, "kv_own")
    m_stg = ar.mark()
    stg_f2 = ar.alloc([128, 2048], F32, "stg_f2")
    mh_b = ar.alias(m_stg, [128, NDT * 64], BF16, "mh_b")
    wqo_b = ar.alias(m_kv, [128, 2 * 1024], BF16, "wqo_b")
    wku_b = ar.alias(m_kv + 4 * 1024 + 32, [128, 2 * 64], BF16, "wku_b")

    def hsl(d, t):
        return h_b[:, d * T_LOC + t.start: d * T_LOC + t.stop]

    def xnsl(d, t):
        return xn_b[:, d * T_LOC + t.start: d * T_LOC + t.stop]

    def qtsl(hh, t):
        p = (hh % 2) * 64
        c = (hh // 2) * T_LOC
        return qtil_all[p:p + 64, c + t.start:c + t.stop]

    for d in range(NDT):
        tl = E.dma("sync", wkvd_f[:, :], l0_wkvdT[d * 128:(d + 1) * 128, :],
                   waits=[E.tok("v")])
        E.do("vector", lambda e, d=d: e.tensor_copy(
            out=wkvd_b[:, d * 64:(d + 1) * 64], in_=wkvd_f[:, :]),
            waits=[tl], inc="v")
    t_wkv = E.tok("v")

    for ci, tc in enumerate(TC):
        tb = emit_ln_bcast(tc, "ab")
        kvp = pA[0:64, 0:512]
        for d in range(NDT):
            E.do("vector", lambda e, d=d, tc=tc: e.tensor_tensor(
                out=sq_a[:, :], in0=xsl(d, tc), in1=bc_a[:, :], op=ALU.mult),
                waits=[tb], inc="v")
            tx = E.do("vector", lambda e, d=d, tc=tc: e.tensor_add(
                out=xnsl(d, tc), in0=sq_a[:, :], in1=bc_b[:, :]), inc="v")
            E.do("tensor", lambda e, d=d, tc=tc, kvp=kvp: e.matmul(
                kvp, wkvd_b[:, d * 64:(d + 1) * 64], xnsl(d, tc),
                start=(d == 0), stop=(d == NDT - 1)),
                waits=[tx, t_wkv] + ([E.tok("act")] if d == 0 else []), inc="mm")
        E.do("scalar", lambda e, tc=tc, kvp=kvp: e.activation(
            out=kv_own[:, tc], in_=kvp, func=AF.Copy),
            waits=[E.tok("mm")], inc="act")
    t_kv = E.tok("act")
    t_xnb = E.tok("v")
    tq = E.dma("gpsimd", ag0_in[:, :], kv_own[:, :].bitcast(I32), waits=[t_kv])
    tcc0 = E.do("gpsimd", lambda e: e.collective_compute(
        "AllGather", ALU.bypass,
        replica_groups=[[0, 1], [2, 3], [4, 5], [6, 7]],
        ins=[ag0_in.ap().opt()], outs=[ag0_out.ap().opt()]),
        waits=[tq], inc="cc")

    for ci, tc in enumerate(TC):
        tb = emit_ln_bcast(tc, "acbc")
        for d in range(NDT):
            E.do("vector", lambda e, d=d, tc=tc: e.tensor_tensor(
                out=sq_a[:, :], in0=xsl(d, tc), in1=bc_a[:, :], op=ALU.mult),
                waits=[tb], inc="v")
            E.do("vector", lambda e: e.tensor_add(
                out=sq_a[:, :], in0=sq_a[:, :], in1=bc_b[:, :]))
            E.do("vector", lambda e, d=d, tc=tc: e.tensor_scalar(
                out=hsl(d, tc), in0=sq_a[:, :], scalar1=g0col[:, d:d + 1],
                scalar2=b0col[:, d:d + 1], op0=ALU.mult, op1=ALU.add), inc="v")
    for ci, tc in enumerate(TC):
        for d in range(NDT):
            E.do("vector", lambda e, d=d, tc=tc: e.tensor_add(
                out=xsl(d, tc), in0=xsl(d, tc), in1=xnsl(d, tc)))
            E.do("vector", lambda e, d=d, tc=tc: e.tensor_scalar_add(
                out=xsl(d, tc), in0=xsl(d, tc), scalar1=fc2bcol[:, d:d + 1]),
                inc="v")
    t_h = E.tok("v")

    for hh in range(8):
        for dh in range(2):
            E.dma("sync", stg_f2[:, dh * 1024:(dh + 1) * 1024],
                  l0_wq[hh * 256 + dh * 128: hh * 256 + (dh + 1) * 128, :],
                  waits=[E.tok("v"), E.tok("mm"), tq] if dh == 0 else ())
        t_ldq = E.tok("q_sync")
        t_wq = E.do("vector", lambda e: e.tensor_copy(
            out=wqo_b[:, :], in_=stg_f2[:, :]), waits=[t_ldq], inc="v")
        for dh in range(2):
            E.dma("sync", stg_f2[:, dh * 64:(dh + 1) * 64],
                  l0_wku[hh * 256 + dh * 128: hh * 256 + (dh + 1) * 128, :],
                  waits=[t_wq] if dh == 0 else ())
        t_ldk = E.tok("q_sync")
        t_wk = E.do("vector", lambda e: e.tensor_copy(
            out=wku_b[:, :], in_=stg_f2[:, 0:128]), waits=[t_ldk], inc="v")
        for d in range(NDT):
            reg = pB[:, (d % 2) * 512:(d % 2) * 512 + 64]
            for dh in range(2):
                E.do("tensor", lambda e, d=d, dh=dh, reg=reg: e.matmul(
                    reg, wqo_b[:, dh * 1024 + d * 128: dh * 1024 + (d + 1) * 128],
                    wku_b[:, dh * 64:(dh + 1) * 64],
                    start=(dh == 0), stop=(dh == 1)),
                    waits=[t_wq, t_wk, E.tok("act")] if dh == 0 else (), inc="mm")
            E.do("scalar", lambda e, d=d, reg=reg: e.activation(
                out=mh_b[:, d * 64:(d + 1) * 64], in_=reg, func=AF.Copy),
                waits=[E.tok("mm")], inc="act")
        t_mh = E.tok("act")
        for ci, tc in enumerate(TC):
            reg = pA[0:64, 1024 + (ci % 2) * 512: 1536 + (ci % 2) * 512]
            for d in range(NDT):
                E.do("tensor", lambda e, d=d, tc=tc, reg=reg: e.matmul(
                    reg, mh_b[:, d * 64:(d + 1) * 64], xnsl(d, tc),
                    start=(d == 0), stop=(d == NDT - 1)),
                    waits=[t_mh, t_xnb, E.tok("act")] if d == 0 else (), inc="mm")
            E.do("scalar", lambda e, hh=hh, tc=tc, reg=reg: e.activation(
                out=qtsl(hh, tc), in_=reg, func=AF.Copy),
                waits=[E.tok("mm")], inc="act")
    E.flush()
    dtap("kv", [lambda: E.dma("gpsimd", dbg_out[0:64, 0:2048],
                              kv_own[:, :].bitcast(I32))])

    ar.reset(mark3)
    w1q = ar.alloc([128, NDT * 1024], BF16, "w1q")
    w2q = ar.alloc([128, 8 * 1024], BF16, "w2q")
    m_stg3 = ar.mark()
    stg_i = ar.alloc([128, 2 * 1024], I32, "stg_i")
    ybuf = ar.alias(m_stg3, [128, 8 * 512], BF16, "ybuf")
    sc_bc = ar.alloc([128, 512], F32, "sc_bc")

    def dequant_tile(dst, src_i, s_hbm, lhs_row, wload):
        tsd = E.dma("sync", brow_s[0:1, :], s_hbm, waits=[E.tok("mm")])
        for half in range(2):
            reg = pB[:, half * 512:(half + 1) * 512]
            hs = slice(half * 512, (half + 1) * 512)
            tm = E.do("tensor", lambda e, reg=reg, lhs_row=lhs_row, hs=hs:
                      e.matmul(reg, lhs_row[0:1, :], brow_s[0:1, hs],
                               start=True, stop=True),
                      waits=[tsd, E.tok("act"), E.tok("v")], inc="mm")
            E.do("scalar", lambda e, reg=reg: e.activation(
                out=sc_bc[:, :], in_=reg, func=AF.Copy),
                waits=[tm], inc="act")
            tqb = E.do("scalar", lambda e, dst=dst, src_i=src_i, hs=hs: e.activation(
                out=dst[:, hs], in_=src_i[:, hs], func=AF.Copy, bias=-128.0),
                waits=[wload], inc="act")
            E.do("vector", lambda e, dst=dst, hs=hs: e.tensor_tensor(
                out=dst[:, hs], in0=dst[:, hs], in1=sc_bc[:, :], op=ALU.mult),
                waits=[E.tok("act")], inc="v")
        return E.tok("v")

    t_w2 = None
    silu_toks = [None, None]
    xadd_toks = [None, None]
    for Q in range(4):
        fofs = Q * 1024
        for w in range(4):
            for dd in range(2):
                d = w * 2 + dd
                E.dma("sync", stg_i[:, dd * 1024:(dd + 1) * 1024],
                      l0_fc1q[d * 128:(d + 1) * 128, fofs:fofs + 1024],
                      waits=[E.tok("v")] if dd == 0 and (w > 0 or t_w2) else ())
            t_ld1 = E.tok("q_sync")
            for dd in range(2):
                d = w * 2 + dd
                dequant_tile(w1q[:, d * 1024:(d + 1) * 1024],
                             stg_i[:, dd * 1024:(dd + 1) * 1024],
                             l0_fc1s[d:d + 1, fofs:fofs + 1024],
                             ones_f, t_ld1)
        t_w1 = E.tok("v")
        for w in range(4):
            for jj in range(2):
                j = w * 2 + jj
                E.dma("gpsimd", stg_i[:, jj * 1024:(jj + 1) * 1024],
                      l0_fc2q[fofs + j * 128:fofs + (j + 1) * 128, :],
                      waits=[E.tok("v")] if jj == 0 else ())
            t_ld2 = E.tok("q_gpsimd")
            for jj in range(2):
                j = w * 2 + jj
                dequant_tile(w2q[:, j * 1024:(j + 1) * 1024],
                             stg_i[:, jj * 1024:(jj + 1) * 1024],
                             l0_fc2s[Q * 8 + j:Q * 8 + j + 1, :],
                             half_f, t_ld2)
        t_w2 = E.tok("v")

        for ci, tc in enumerate(TC):
            for j in range(8):
                ftile = Q * 8 + j
                yreg = pA[:, (j % 2) * 512:(j % 2) * 512 + 512]
                for d in range(NDT):
                    E.do("tensor", lambda e, d=d, j=j, tc=tc, yreg=yreg: e.matmul(
                        yreg, w1q[:, d * 1024 + j * 128: d * 1024 + (j + 1) * 128],
                        hsl(d, tc), start=(d == 0), stop=(d == NDT - 1)),
                        waits=([t_w2, t_h, silu_toks[j % 2]]
                               if d == 0 else ()), inc="mm")
                tym = E.tok("mm")
                silu_toks[j % 2] = E.do(
                    "scalar", lambda e, j=j, yreg=yreg, ftile=ftile: e.activation(
                        out=ybuf[:, j * 512:(j + 1) * 512], in_=yreg,
                        func=AF.Silu, bias=fc1bcol[:, ftile:ftile + 1]),
                    waits=[tym], inc="act")
            t_y = E.tok("act")
            for d in range(NDT):
                zreg = pB[:, 512 + (d % 2) * 512: 1024 + (d % 2) * 512]
                for j in range(8):
                    E.do("tensor", lambda e, d=d, j=j, zreg=zreg: e.matmul(
                        zreg, w2q[:, j * 1024 + d * 128: j * 1024 + (d + 1) * 128],
                        ybuf[:, j * 512:(j + 1) * 512],
                        start=(j == 0), stop=(j == 7)),
                        waits=[t_y, xadd_toks[d % 2]] if j == 0 else (), inc="mm")
                tzm = E.tok("mm")
                xadd_toks[d % 2] = E.do(
                    "vector", lambda e, d=d, tc=tc, zreg=zreg: e.tensor_add(
                        out=xsl(d, tc), in0=xsl(d, tc), in1=zreg),
                    waits=[tzm], inc="v")
    E.flush()
    dtap("ffn", [lambda d=d: E.dma("gpsimd", dbg_out[:, d * 2048:(d + 1) * 2048], xsl(d))
                 for d in range(NDT)])
    if upto == "ffn":
        _finish(E, nc, y_out, xsl)
        return nc

    ar.reset(base_mark)
    stg_f = ar.alloc([128, 2048], F32, "stg_f")
    kvT = ar.alloc([128, 32 * 66], BF16, "kvT")
    wvu_b = ar.alloc([64, 2048], BF16, "wvu_b")
    wo_b = ar.alloc([128, 2 * 1024], BF16, "wo_b")
    P_b = ar.alloc([128, 32 * 512], BF16, "P_b")
    on_b = ar.alloc([128, 2 * 512], BF16, "on_b")
    un_b = ar.alloc([64, 1024], BF16, "un_b")
    rz_bc = ar.alloc([64, 512], F32, "rz_bc")
    rzrow = ar.alloc([1, 512], F32, "rzrow")
    qt_l = ar.alloc([64, T_LOC], BF16, "qt_l")

    tw2 = E.dma("sync", stg_f[0:64, :], l0_wvuT[:, :], waits=[E.tok("v")])
    E.do("vector", lambda e: e.tensor_copy(out=wvu_b[:, :], in_=stg_f[0:64, :]),
         waits=[tw2], inc="v")
    t_wvu = E.tok("v")

    E.do("vector", lambda e: e.memset(kvT[:, :], 1.0), inc="v")
    tg1 = E.dma("gpsimd", kvfull[:, 0:T_LOC].bitcast(I32), ag0_out[0:64, :],
                waits=[tcc0])
    tg2 = E.dma("gpsimd", kvfull[:, T_LOC:T_FULL].bitcast(I32), ag0_out[64:128, :])
    t_kvf = E.do("vector", lambda e: e.engine_nop(), waits=[tg1, tg2], inc="v")
    for kt in range(32):
        reg = pA[:, (kt % 2) * 512:(kt % 2) * 512 + 32].bitcast(BF16)
        E.do("tensor", lambda e, kt=kt, reg=reg: e.transpose(
            reg, kvfull[0:64, kt * 128:(kt + 1) * 128], ident_b[0:64, 0:64]),
            waits=[t_kvf, E.tok("act")], inc="mm")
        E.do("scalar", lambda e, kt=kt, reg=reg: e.activation(
            out=kvT[:, kt * 66:kt * 66 + 64], in_=reg, func=AF.Copy),
            waits=[E.tok("mm")], inc="act")
    t_kvT = E.tok("act")
    E.flush()

    exp_toks = [None, None]
    for hh in range(8):
        for o in range(2):
            E.dma("sync", stg_f[:, o * 1024:(o + 1) * 1024],
                  l0_woT[hh * 256 + o * 128: hh * 256 + (o + 1) * 128, :],
                  waits=[E.tok("v"), E.tok("mm")] if o == 0 else ())
        t_ldo = E.tok("q_sync")
        t_wo = E.do("vector", lambda e: e.tensor_copy(
            out=wo_b[:, :], in_=stg_f[:, :]), waits=[t_ldo], inc="v")
        t_qtl = E.do("vector", lambda e, hh=hh: e.tensor_copy(
            out=qt_l[:, :], in_=qtil_all[(hh % 2) * 64:(hh % 2) * 64 + 64,
                                         (hh // 2) * T_LOC:(hh // 2 + 1) * T_LOC]),
            waits=[E.tok("mm")], inc="v")

        pend = None
        for ci, tc in enumerate(TC):
            uz = pA[0:65, 1024 + (ci % 2) * 512:1536 + (ci % 2) * 512]
            unb = un_b[:, (ci % 2) * 512:(ci % 2) * 512 + 512]
            exp_k = [None] * 32

            def emit_uz(kt, uz=uz, exp_k=exp_k, ew=()):
                E.do("tensor", lambda e, kt=kt, uz=uz: e.matmul(
                    uz, kvT[:, kt * 66:kt * 66 + 65],
                    P_b[:, kt * 512:(kt + 1) * 512],
                    start=(kt == 0), stop=(kt == 31)),
                    waits=[exp_k[kt]] + list(ew), inc="mm")

            for kt in range(32):
                sreg = pA[:, (kt % 2) * 512:(kt % 2) * 512 + 512]
                tm = E.do("tensor", lambda e, kt=kt, tc=tc, sreg=sreg: e.matmul(
                    sreg, kvfull[:, kt * 128:(kt + 1) * 128], qt_l[:, tc],
                    start=True, stop=True),
                    waits=[t_kvf, t_kvT, t_qtl, exp_toks[kt % 2]] +
                          ([E.tok("v")] if kt == 0 else []), inc="mm")
                exp_toks[kt % 2] = exp_k[kt] = E.do(
                    "scalar", lambda e, kt=kt, sreg=sreg: e.activation(
                        out=P_b[:, kt * 512:(kt + 1) * 512], in_=sreg,
                        func=AF.Exp, scale=1.0 / 16.0), waits=[tm], inc="act")
                if kt >= 2:
                    emit_uz(kt - 2, ew=[t_kvT, E.tok("v")] if kt == 2 else ())
            emit_uz(30)
            emit_uz(31)
            t_exp = E.tok("act")
            if dump == "attn1" and hh == 0 and ci == 0:
                E.do("gpsimd", lambda e: e.nop(), waits=[t_exp])
                E.dma("gpsimd", dbg_out[:, 4096:12288].bitcast(BF16), P_b[:, :])
                E.do("gpsimd", lambda e: e.nop(), waits=[E.tok("q_gpsimd")])
            t_uz = E.tok("mm")
            if pend is not None:
                pend()
            trz = E.do("vector", lambda e, uz=uz: e.reciprocal(
                out=rzrow[:, :], in_=uz[64:65, :]), waits=[t_uz], inc="v")
            tbm = E.do("tensor", lambda e: e.matmul(
                pB[0:64, 0:512], ones_f[0:1, 0:64], rzrow[:, :],
                start=True, stop=True), waits=[trz, E.tok("act")], inc="mm")
            trzb = E.do("scalar", lambda e: e.activation(
                out=rz_bc[:, :], in_=pB[0:64, 0:512], func=AF.Copy),
                waits=[tbm, E.tok("v")], inc="act")
            t_un = E.do("vector", lambda e, uz=uz, unb=unb: e.tensor_tensor(
                out=unb, in0=uz[0:64, :], in1=rz_bc[:, :], op=ALU.mult),
                waits=[trzb], inc="v")

            def pend(ci=ci, tc=tc, unb=unb, t_un=t_un, trzb=trzb):
                for dv in range(2):
                    oreg = pB[:, dv * 512:(dv + 1) * 512]
                    E.do("tensor", lambda e, dv=dv, oreg=oreg, unb=unb:
                         e.matmul(
                             oreg,
                             wvu_b[:, hh * 256 + dv * 128: hh * 256 + (dv + 1) * 128],
                             unb, start=True, stop=True),
                         waits=[t_un, t_wvu, trzb] if dv == 0 else (), inc="mm")
                    E.do("scalar", lambda e, dv=dv, oreg=oreg: e.activation(
                        out=on_b[:, dv * 512:(dv + 1) * 512], in_=oreg,
                        func=AF.Copy), waits=[E.tok("mm")], inc="act")
                t_on = E.tok("act")
                for d in range(NDT):
                    wreg = pB[:, 1024 + (d % 2) * 512: 1536 + (d % 2) * 512]
                    for o in range(2):
                        E.do("tensor", lambda e, d=d, o=o, wreg=wreg: e.matmul(
                            wreg,
                            wo_b[:, o * 1024 + d * 128: o * 1024 + (d + 1) * 128],
                            on_b[:, o * 512:(o + 1) * 512],
                            start=(o == 0), stop=(o == 1)),
                            waits=[t_on, t_wo, E.tok("v")] if o == 0 else (),
                            inc="mm")
                    E.do("vector", lambda e, d=d, tc=tc, wreg=wreg: e.tensor_add(
                        out=xsl(d, tc), in0=xsl(d, tc), in1=wreg),
                        waits=[E.tok("mm")], inc="v")
        pend()
    E.flush()

    dtap("l0", [lambda d=d: E.dma("gpsimd", dbg_out[:, d * 2048:(d + 1) * 2048], xsl(d))
                for d in range(NDT)])
    if upto == "l0":
        _finish(E, nc, y_out, xsl)
        return nc

    ar.reset(0)
    xn1_b = ar.alloc([128, NDT * T_LOC], BF16, "xn1_b")

    def xn1sl(d, t=None):
        if t is None:
            return xn1_b[:, d * T_LOC:(d + 1) * T_LOC]
        return xn1_b[:, d * T_LOC + t.start: d * T_LOC + t.stop]

    emit_stats()
    for ci, tc in enumerate(TC):
        tb = emit_ln_bcast(tc, "ab")
        for d in range(NDT):
            E.do("vector", lambda e, d=d, tc=tc: e.tensor_tensor(
                out=sq_a[:, :], in0=xsl(d, tc), in1=bc_a[:, :], op=ALU.mult),
                waits=[tb], inc="v")
            E.do("vector", lambda e, d=d, tc=tc: e.tensor_add(
                out=xn1sl(d, tc), in0=sq_a[:, :], in1=bc_b[:, :]), inc="v")
    t_xn1 = E.tok("v")
    for d in range(NDT):
        E.do("vector", lambda e, d=d: e.tensor_reduce(
            out=sq_a[:, 0:1] if d == 0 else sq_a[:, 1:2], in_=xn1sl(d),
            axis=AX.X, op=ALU.max, apply_absolute_value=True),
            waits=[t_xn1, E.tok("v")] if d == 0 else [E.tok("v")], inc="v")
        if d > 0:
            vch(lambda e: e.tensor_max(
                out=sq_a[:, 0:1], in0=sq_a[:, 0:1], in1=sq_a[:, 1:2]))
    tmx = E.do("vector", lambda e: e.engine_nop(), inc="v")
    tt1 = E.do("tensor", lambda e: e.transpose(
        pB[0:1, 0:128], sq_a[:, 0:1], ident[:, :]),
        waits=[tmx, E.tok("act")], inc="mm")
    tt2 = E.do("scalar", lambda e: e.activation(
        out=brow_s[0:1, 0:128], in_=pB[0:1, 0:128], func=AF.Copy),
        waits=[tt1], inc="act")
    E.do("vector", lambda e: e.tensor_reduce(
        out=scal[0:1, 0:1], in_=brow_s[0:1, 0:128], axis=AX.X, op=ALU.max),
        waits=[tt2], inc="v")
    tg1v = E.tok("v")
    tq1 = E.dma("gpsimd", ar1_in[:, :], scal[0:1, 0:1], waits=[tg1v])
    tar1 = E.do("gpsimd", lambda e: e.collective_compute(
        "AllReduce", ALU.max, replica_groups=[list(range(8))],
        ins=[ar1_in.ap().opt()], outs=[ar1_out.ap().opt()]),
        waits=[tq1], inc="cc")
    mark_aproj = ar.mark()
    wslab = ar.alloc([128, 8 * 128], BF16, "wslab")
    wstg = ar.alloc([128, 512], F32, "wstg")
    pstg = ar.alloc([128, 1024], BF16, "pstg")
    sqv = ar.alloc([128, 512], F32, "sqv")
    nsc = ar.alloc([4, 512], F32, "nsc")
    nv_t = ar.alloc([128, 16], F32, "nv_t")
    wv_b = ar.alloc([128, 8 * 1024], BF16, "wv_b")

    sqv_w = ar.alias(43072, [128, 2048], F32, "sqv_w")
    nsc_w = ar.alias(51264, [4, 2048], F32, "nsc_w")

    t_mult = [None] * 4

    def proj_padded(w_hbm, out_tile_fn):
        t_dma = [None, None]
        for ot in range(4):
            for d in range(NDT):
                tw = E.dma("sync", wstg[:, 0:128],
                           w_hbm[d * 128:(d + 1) * 128, ot * 128:(ot + 1) * 128],
                           waits=[E.tok("v")])
                E.do("vector", lambda e, d=d: e.tensor_copy(
                    out=wslab[:, d * 128:(d + 1) * 128], in_=wstg[:, 0:128]),
                    waits=[tw], inc="v")
            t_ws = E.tok("v")
            for ti, tc in enumerate(TC):
                for d in range(NDT):
                    E.do("tensor", lambda e, d=d, tc=tc, ti=ti: e.matmul(
                        pA[:, ti * 512:(ti + 1) * 512],
                        wslab[:, d * 128:(d + 1) * 128], xn1sl(d, tc),
                        start=(d == 0), stop=(d == NDT - 1)),
                        waits=([t_ws, t_xn1] +
                               ([t_mult[ti]] if t_mult[ti] else
                                [E.tok("act"), E.tok("v")]))
                        if d == 0 else (), inc="mm")
            tpp = E.tok("mm")
            tsq0 = E.do("scalar", lambda e: e.activation(
                out=sqv_w[:, 0:1024], in_=pA[:, 0:1024], func=AF.Square),
                waits=[tpp], inc="act")
            tsq = E.do("scalar", lambda e: e.activation(
                out=sqv_w[:, 1024:2048], in_=pA[:, 1024:2048], func=AF.Square),
                inc="act")
            for ti in range(4):
                E.do("tensor", lambda e, ti=ti: e.matmul(
                    pB[0:4, ti * 512:(ti + 1) * 512], Ekq[:, :],
                    sqv_w[:, ti * 512:(ti + 1) * 512], start=True, stop=True),
                    waits=[tsq0, tsq, E.tok("v")] if ti == 0 else (), inc="mm")
            tnm = E.tok("mm")
            E.do("scalar", lambda e: e.activation(
                out=nsc_w[:, :], in_=pB[0:4, 0:2048], func=AF.Sqrt),
                waits=[tnm], inc="act")
            E.do("vector", lambda e: e.engine_nop(), waits=[E.tok("act")],
                 inc="v")
            vch(lambda e: e.tensor_scalar_max(
                out=nsc_w[:, :], in0=nsc_w[:, :], scalar1=1e-12))
            vch(lambda e: e.reciprocal(out=nsc_w[:, :], in_=nsc_w[:, :]))
            trc = E.tok("v")
            for ti in range(4):
                tex = E.do("tensor", lambda e, ti=ti: e.matmul(
                    pB[:, ti * 512:(ti + 1) * 512], Eexp[:, :],
                    nsc_w[:, ti * 512:(ti + 1) * 512], start=True, stop=True),
                    waits=([trc, E.tok("act")] if ti == 0 else []), inc="mm")
                tev = E.do("scalar", lambda e, ti=ti: e.activation(
                    out=sqv_w[:, ti * 512:(ti + 1) * 512],
                    in_=pB[:, ti * 512:(ti + 1) * 512], func=AF.Copy),
                    waits=[tex], inc="act")
                t_mult[ti] = tno = E.do(
                    "vector", lambda e, ti=ti: e.tensor_tensor(
                        out=pstg[:, (ti % 2) * 512:(ti % 2) * 512 + 512],
                        in0=pA[:, ti * 512:(ti + 1) * 512],
                        in1=sqv_w[:, ti * 512:(ti + 1) * 512], op=ALU.mult),
                    waits=[tev] + ([t_dma[ti % 2]] if t_dma[ti % 2] else []),
                    inc="v")
                t_dma[ti % 2] = out_tile_fn(ot, ti, tno)

    def q_out(ot, ti, tok):
        return E.dma(
            "gpsimd", qn_hbm[ot * 128:(ot + 1) * 128, ti * 512:(ti + 1) * 512],
            pstg[:, (ti % 2) * 512:(ti % 2) * 512 + 512], waits=[tok])

    def k_out(ot, ti, tok):
        return E.dma(
            "gpsimd", ag_kn_in[ot * 128:(ot + 1) * 128, ti * 256:(ti + 1) * 256],
            pstg[:, (ti % 2) * 512:(ti % 2) * 512 + 512].bitcast(I32),
            waits=[tok])

    proj_padded(l1_wqp, q_out)
    proj_padded(l1_wkp, k_out)

    for d in range(NDT):
        tw = E.dma("sync", wstg[:, 0:512],
                   l1_wv[d * 128:(d + 1) * 128, 0:512], waits=[E.tok("v")])
        E.do("vector", lambda e, d=d: e.tensor_copy(
            out=wv_b[:, d * 1024:d * 1024 + 512], in_=wstg[:, 0:512]),
            waits=[tw], inc="v")
        tw = E.dma("sync", wstg[:, 0:512],
                   l1_wv[d * 128:(d + 1) * 128, 512:1024], waits=[E.tok("v")])
        E.do("vector", lambda e, d=d: e.tensor_copy(
            out=wv_b[:, d * 1024 + 512:(d + 1) * 1024], in_=wstg[:, 0:512]),
            waits=[tw], inc="v")
    t_wv = E.tok("v")
    sqv_v = ar.alias(59456, [128, 1024], F32, "sqv_v")
    for vt in range(16):
        for dvc in range(2):
            reg = pA[:, 1024 + dvc * 512:1536 + dvc * 512]
            for d in range(NDT):
                E.do("tensor", lambda e, d=d, vt=vt, dvc=dvc, reg=reg: e.matmul(
                    reg, xn1_b[:, d * T_LOC + vt * 128:d * T_LOC + (vt + 1) * 128],
                    wv_b[:, d * 1024 + dvc * 512:d * 1024 + (dvc + 1) * 512],
                    start=(d == 0), stop=(d == NDT - 1)),
                    waits=[t_wv, t_xn1, E.tok("act"), E.tok("v")]
                    if (dvc == 0 and d == 0) else (), inc="mm")
        tpp = E.tok("mm")
        tsq = E.do("scalar", lambda e: e.activation(
            out=sqv_v[:, :], in_=pA[:, 1024:2048], func=AF.Square),
            waits=[tpp], inc="act")
        E.do("vector", lambda e: e.tensor_reduce(
            out=nv_t[:, 0:16],
            in_=sqv_v[:, :].rearrange("p (a b) -> p a b", b=64),
            axis=AX.X, op=ALU.add),
            waits=[tsq, E.tok("v")], inc="v")
        E.do("scalar", lambda e: e.activation(
            out=pstg[:, 0:1024], in_=pA[:, 1024:2048], func=AF.Copy),
            waits=[E.tok("v")], inc="act")
        E.do("vector", lambda e: e.engine_nop(), waits=[E.tok("act")], inc="v")
        vch(lambda e: e.tensor_copy(out=sqv[:, 0:16], in_=nv_t[:, :]))
        E.do("scalar", lambda e: e.activation(
            out=sqv[:, 0:16], in_=sqv[:, 0:16], func=AF.Sqrt),
            waits=[E.tok("v")], inc="act")
        E.do("vector", lambda e: e.engine_nop(), waits=[E.tok("act")], inc="v")
        vch(lambda e: e.tensor_scalar_max(
            out=sqv[:, 0:16], in0=sqv[:, 0:16], scalar1=1e-12))
        vch(lambda e: e.reciprocal(out=sqv[:, 0:16], in_=sqv[:, 0:16]))
        tno = vch(lambda e: e.tensor_tensor(
            out=pstg[:, :].rearrange("p (a b) -> p a b", b=64),
            in0=pstg[:, :].rearrange("p (a b) -> p a b", b=64),
            in1=sqv[:, 0:16].unsqueeze(2).broadcast_to([128, 16, 64]),
            op=ALU.mult))
        E.dma("gpsimd", ag_vn_in[vt * 128:(vt + 1) * 128, :],
              pstg[:, :].bitcast(I32), waits=[tno])
    t_vdone = E.tok("q_gpsimd")
    tag_k = E.do("gpsimd", lambda e: e.collective_compute(
        "AllGather", ALU.bypass,
        replica_groups=[[0, 1], [2, 3], [4, 5], [6, 7]],
        ins=[ag_kn_in.ap().opt()], outs=[ag_kn_out.ap().opt()]),
        waits=[t_vdone], inc="cc")
    tag_v = E.do("gpsimd", lambda e: e.collective_compute(
        "AllGather", ALU.bypass,
        replica_groups=[[0, 1], [2, 3], [4, 5], [6, 7]],
        ins=[ag_vn_in.ap().opt()], outs=[ag_vn_out.ap().opt()]),
        inc="cc")
    ar.reset(mark_aproj)
    E.flush()

    def sc(i):
        return scal[0:1, i:i + 1]

    def bcast_col(src, col):
        reg = pB[:, 1536:1544]
        tm = E.do("tensor", lambda e, src=src, reg=reg: e.matmul(
            reg[:, 0:1], ones_f[0:1, :], src, start=True, stop=True),
            waits=(E.tok("v"), E.tok("act")), inc="mm")
        return E.do("scalar", lambda e, col=col, reg=reg: e.activation(
            out=scol[:, col:col + 1], in_=reg[:, 0:1], func=AF.Copy),
            waits=[tm], inc="act")

    r1_b = ar.alloc([128, NDT * T_LOC], BF16, "r1_b")
    m_stg6 = ar.mark()
    stg_w = ar.alloc([128, 2 * 512], F32, "stg_w")
    stg_q = ar.alloc([128, 512], I32, "stg_q")
    w_big = ar.alloc([128, 8 * 512], BF16, "w_big")
    ub_s = ar.alias(m_stg6, [128, 2048], BF16, "ub_s")

    E.do("gpsimd", lambda e: e.nop(), waits=[tar1])
    tld = E.dma("gpsimd", sc(0), ar1_out[:, 0:1])
    E.do("vector", lambda e: e.engine_nop(), waits=[tld], inc="v")
    bcast_col(sc(0), 16)
    E.do("vector", lambda e: e.engine_nop(), waits=[E.tok("act")], inc="v")
    vch(lambda e: e.tensor_scalar(
        out=scol[:, 17:18], in0=scol[:, 16:17], scalar1=1.0 / 32767.0,
        scalar2=1e-12, op0=ALU.mult, op1=ALU.max))
    vch(lambda e: e.reciprocal(out=scol[:, 0:1], in_=scol[:, 17:18]))
    t_c1 = E.tok("v")
    stg_r1 = ar.alias(m_stg6, [128, 2048], I32, "stg_r1")
    for d in range(NDT):
        E.do("vector", lambda e, d=d: e.tensor_scalar(
            out=stg_r1[:, :], in0=xn1sl(d), scalar1=scol[:, 0:1],
            scalar2=None, op0=ALU.mult),
            waits=[t_c1] if d == 0 else (), inc="v")
        E.do("vector", lambda e, d=d: e.tensor_copy(
            out=r1_b[:, d * T_LOC:(d + 1) * T_LOC], in_=stg_r1[:, :]), inc="v")
    t_r1 = E.tok("v")
    if dump == "l1sc":
        E.do("gpsimd", lambda e: e.nop(), waits=[t_r1, E.tok("act")])
        E.dma("gpsimd", dbg_out[4:5, 0:32], scal[0:1, 0:32])
        E.do("gpsimd", lambda e: e.nop(), waits=[E.tok("q_gpsimd")])
    sTu = ar.alias(0, [32, 3072], F32, "sTu")
    rTu = ar.alias(12 * 1024 + 32, [32, 3072], F32, "rTu")

    def lpbq_scales(w_rm, sT, rT, nrt, first_wait=()):
        t_red = [None, None]
        t_tt = [None, None]
        for rt in range(nrt):
            pp = rt % 2
            hs = slice(pp * 1024, (pp + 1) * 1024)
            tw = E.dma("sync", stg_f2[:, hs], w_rm[rt * 128:(rt + 1) * 128, :],
                       waits=((E.tok("v"), E.tok("mm")) + tuple(first_wait))
                       if rt < 2 else [t_red[pp]])
            t_red[pp] = E.do("vector", lambda e, pp=pp, hs=hs: e.tensor_reduce(
                out=sq_a[:, pp * 32:(pp + 1) * 32],
                in_=stg_f2[:, hs].rearrange("p (a b) -> p a b", b=32),
                axis=AX.X, op=ALU.max, apply_absolute_value=True),
                waits=[tw], inc="v")
            E.do("vector", lambda e, pp=pp: e.tensor_scalar(
                out=sq_a[:, 64 + pp * 32:64 + (pp + 1) * 32],
                in0=sq_a[:, pp * 32:(pp + 1) * 32], scalar1=1.0 / 7.0,
                scalar2=1e-12, op0=ALU.mult, op1=ALU.max),
                waits=[E.tok("v")] + ([t_tt[pp]] if t_tt[pp] else []), inc="v")
            tv = E.tok("v")
            t_tt[pp] = tt = E.do("tensor", lambda e, pp=pp: e.transpose(
                pB[0:32, 0:128], sq_a[:, 64 + pp * 32:64 + (pp + 1) * 32],
                ident[:, :]), waits=[tv, E.tok("act")], inc="mm")
            E.do("scalar", lambda e, rt=rt, sT=sT: e.activation(
                out=sT[0:32, rt * 128:(rt + 1) * 128], in_=pB[0:32, 0:128],
                func=AF.Copy), waits=[tt], inc="act")
        tse = E.tok("act")
        return E.do("vector", lambda e, sT=sT, rT=rT: e.reciprocal(
            out=rT[:, :], in_=sT[:, :]), waits=[tse], inc="v")

    E.do("vector", lambda e: e.memset(scol[:, 1:3], 0.0), inc="v")

    def l1_mm_pass(wT_hbm, sT, rT, hbm_dst, maxcol, t_sc):
        t_ubst = None
        for oc in range(6):
            for it in range(NDT):
                tw = E.dma("sync", stg_w[:, 0:512],
                           wT_hbm[it * 128:(it + 1) * 128,
                                  oc * 512:(oc + 1) * 512],
                           waits=([E.tok("v"), E.tok("act")] +
                                  ([t_ubst] if t_ubst else [E.tok("q_gpsimd")]))
                           if it == 0 else [E.tok("v")])
                rreg = pB[:, 1024:1536]
                sreg2 = pB[:, 1536:2048]
                tm1 = E.do("tensor", lambda e, it=it, oc=oc, rT=rT, rreg=rreg:
                           e.matmul(rreg, Esel[:, it * 128:(it + 1) * 128],
                                    rT[:, oc * 512:(oc + 1) * 512],
                                    start=True, stop=True),
                           waits=[t_sc, E.tok("v"), E.tok("act")], inc="mm")
                tm2 = E.do("tensor", lambda e, it=it, oc=oc, sT=sT, sreg2=sreg2:
                           e.matmul(sreg2, Esel[:, it * 128:(it + 1) * 128],
                                    sT[:, oc * 512:(oc + 1) * 512],
                                    start=True, stop=True), inc="mm")
                E.do("vector", lambda e, rreg=rreg: e.tensor_tensor(
                    out=stg_w[:, 512:1024], in0=stg_w[:, 0:512], in1=rreg,
                    op=ALU.mult), waits=[tm1, tw], inc="v")
                E.do("vector", lambda e: e.tensor_copy(
                    out=stg_q[:, :], in_=stg_w[:, 512:1024]), inc="v")
                tse2 = E.do("scalar", lambda e, sreg2=sreg2: e.activation(
                    out=stg_w[:, 512:1024], in_=sreg2, func=AF.Copy),
                    waits=[tm2, E.tok("v")], inc="act")
                E.do("vector", lambda e, it=it: e.tensor_tensor(
                    out=w_big[:, it * 512:(it + 1) * 512], in0=stg_q[:, :],
                    in1=stg_w[:, 512:1024], op=ALU.mult),
                    waits=[tse2], inc="v")
            t_wb = E.tok("v")
            for otl in range(4):
                ot = oc * 4 + otl
                for ci, tc in enumerate(TC):
                    reg = pA[:, (ci % 2) * 512:(ci % 2) * 512 + 512]
                    for it in range(NDT):
                        E.do("tensor", lambda e, it=it, otl=otl, tc=tc, reg=reg:
                             e.matmul(reg,
                                      w_big[:, it * 512 + otl * 128:
                                            it * 512 + (otl + 1) * 128],
                                      r1_b[:, it * T_LOC + tc.start:
                                           it * T_LOC + tc.stop],
                                      start=(it == 0), stop=(it == NDT - 1)),
                             waits=([t_wb, t_r1, E.tok("act")]
                                    if it == 0 else ()), inc="mm")
                    E.do("scalar", lambda e, tc=tc, reg=reg: e.activation(
                        out=ub_s[:, tc], in_=reg, func=AF.Copy),
                        waits=[E.tok("mm")] +
                        ([t_ubst] if (t_ubst and ci == 0) else []), inc="act")
                tev = E.tok("act")
                E.do("vector", lambda e: e.tensor_reduce(
                    out=sq_a[:, 0:1], in_=ub_s[:, :], axis=AX.X, op=ALU.max,
                    apply_absolute_value=True), waits=[tev, E.tok("v")], inc="v")
                vch(lambda e, maxcol=maxcol: e.tensor_max(
                    out=scol[:, maxcol:maxcol + 1],
                    in0=scol[:, maxcol:maxcol + 1], in1=sq_a[:, 0:1]))
                t_ubst = E.dma("gpsimd", hbm_dst[ot * 128:(ot + 1) * 128, :],
                               ub_s[:, :], waits=[tev])
        return E.tok("v")

    t_sc_u = lpbq_scales(l1_up_rm, sTu, rTu, 24, first_wait=[t_r1])
    t_up = l1_mm_pass(l1_upT, sTu, rTu, up_hbm, 1, t_sc_u)
    if dump == "l1sc":
        E.do("gpsimd", lambda e: e.nop(), waits=[t_up, E.tok("act"), E.tok("mm")])
        E.dma("gpsimd", dbg_out[5:6, 0:32], scal[0:1, 0:32])
        E.do("gpsimd", lambda e: e.nop(), waits=[E.tok("q_gpsimd")])
    t_sc_g = lpbq_scales(l1_gate_rm, sTu, rTu, 24, first_wait=[t_up])
    t_gate = l1_mm_pass(l1_gateT, sTu, rTu, gate_hbm, 2, t_sc_g)

    def partition_max_to(srccol, dst, wait):
        tt1 = E.do("tensor", lambda e, srccol=srccol: e.transpose(
            pB[0:1, 0:128], srccol, ident[:, :]),
            waits=[wait, E.tok("act")], inc="mm")
        tt2 = E.do("scalar", lambda e: e.activation(
            out=brow_s[0:1, 0:128], in_=pB[0:1, 0:128], func=AF.Copy),
            waits=[tt1], inc="act")
        return E.do("vector", lambda e, dst=dst: e.tensor_reduce(
            out=dst, in_=brow_s[0:1, 0:128], axis=AX.X, op=ALU.max),
            waits=[tt2], inc="v")

    tv4 = partition_max_to(scol[:, 1:2], sc(4), t_gate)
    tq3 = E.dma("gpsimd", ar2_in[:, :], sc(4), waits=[tv4])
    tar2a = E.do("gpsimd", lambda e: e.collective_compute(
        "AllReduce", ALU.max, replica_groups=[list(range(8))],
        ins=[ar2_in.ap().opt()], outs=[ar2_out.ap().opt()]),
        waits=[tq3], inc="cc")
    tv5 = partition_max_to(scol[:, 2:3], sc(5), t_gate)
    tq3b = E.dma("gpsimd", ar2b_in[:, :], sc(5), waits=[tv5])
    tar2 = E.do("gpsimd", lambda e: e.collective_compute(
        "AllReduce", ALU.max, replica_groups=[list(range(8))],
        ins=[ar2b_in.ap().opt()], outs=[ar2b_out.ap().opt()]),
        waits=[tq3b], inc="cc")
    E.flush()
    if dump == "l1ug":
        E.do("gpsimd", lambda e: e.nop(), waits=[E.tok("cc")])
        E.dma("gpsimd", dbg_out[0:1, 0:1], ar2_out[:, :])
        E.dma("gpsimd", dbg_out[0:1, 1:2], ar2b_out[:, :])
        E.dma("gpsimd", dbg_out[:, 4096:5120].bitcast(BF16), up_hbm[0:128, :])
        E.dma("gpsimd", dbg_out[:, 5120:6144].bitcast(BF16),
              gate_hbm[20 * 128:21 * 128, :])
        E.do("gpsimd", lambda e: e.nop(), waits=[E.tok("q_gpsimd")])

    ta4 = E.dma("gpsimd", sc(4), ar2_out[:, :], waits=[tar2a])
    tld = E.dma("gpsimd", sc(5), ar2b_out[:, :], waits=[tar2])
    E.do("vector", lambda e: e.engine_nop(), waits=[ta4, tld], inc="v")
    bcast_col(sc(4), 18)
    bcast_col(sc(5), 19)
    E.do("vector", lambda e: e.engine_nop(), waits=[E.tok("act")], inc="v")
    vch(lambda e: e.tensor_tensor(
        out=scol[:, 18:19], in0=scol[:, 18:19], in1=scol[:, 17:18], op=ALU.mult))
    vch(lambda e: e.tensor_scalar(
        out=scol[:, 20:21], in0=scol[:, 18:19], scalar1=1.0 / 32767.0,
        scalar2=1e-12, op0=ALU.mult, op1=ALU.max))
    vch(lambda e: e.reciprocal(out=scol[:, 3:4], in_=scol[:, 20:21]))
    vch(lambda e: e.tensor_tensor(
        out=scol[:, 3:4], in0=scol[:, 3:4], in1=scol[:, 17:18], op=ALU.mult))
    vch(lambda e: e.tensor_tensor(
        out=scol[:, 19:20], in0=scol[:, 19:20], in1=scol[:, 17:18], op=ALU.mult))
    vch(lambda e: e.tensor_scalar(
        out=scol[:, 5:6], in0=scol[:, 19:20], scalar1=1.0 / 32767.0,
        scalar2=1e-12, op0=ALU.mult, op1=ALU.max))
    vch(lambda e: e.reciprocal(out=scol[:, 4:5], in_=scol[:, 5:6]))
    vch(lambda e: e.tensor_tensor(
        out=scol[:, 4:5], in0=scol[:, 4:5], in1=scol[:, 17:18], op=ALU.mult))
    t_sc2 = E.tok("v")

    if dump == "l1sc":
        E.do("gpsimd", lambda e: e.nop(), waits=[E.tok("v"), E.tok("act")])
        E.dma("gpsimd", dbg_out[6:7, 0:32], scal[0:1, 0:32])
        E.do("gpsimd", lambda e: e.nop(), waits=[E.tok("q_gpsimd")])
    pa_g2 = [ar.alias(0, [128, 2048], BF16, "pa_g2a"),
             ar.alias(4096, [128, 2048], BF16, "pa_g2b")]
    pb_u2 = [ar.alias(8192, [128, 2048], BF16, "pb_u2a"),
             ar.alias(12288, [128, 2048], BF16, "pb_u2b")]
    pa_sg2 = [ar.alias(16384, [128, 2048], F32, "pa_sg2a"),
              ar.alias(24576, [128, 2048], F32, "pa_sg2b")]
    pa_rg2 = [ar.alias(32768, [128, 2048], I32, "pa_rg2a"),
              ar.alias(40960, [128, 2048], I32, "pa_rg2b")]
    pa_m32 = [ar.alias(49152, [128, 2048], F32, "pa_m32a"),
              ar.alias(57344, [128, 2048], F32, "pa_m32b")]
    pa_q2 = ar.alias(65536, [128, 2048], I32, "pa_q2")
    pa_m3 = ar.alias(24704, [128, 2048], F32, "pa_m3")
    stg_q2 = ar.alias(41 * 1024 + 128, [128, 2048], I32, "stg_q2")

    t_g_rd = [None, None]
    t_m3st = [None, None]
    for rt in range(24):
        pp = rt % 2
        twg = E.dma("sync", pa_g2[pp][:, :], gate_hbm[rt * 128:(rt + 1) * 128, :],
                    waits=([E.tok("mm"), E.tok("v"), E.tok("act"),
                            E.tok("q_gpsimd")] if rt == 0 else
                           ([t_g_rd[pp]] if t_g_rd[pp] else [])))
        t_g_rd[pp] = E.do("vector", lambda e, pp=pp: e.tensor_scalar(
            out=pa_rg2[pp][:, :], in0=pa_g2[pp][:, :], scalar1=scol[:, 4:5],
            scalar2=None, op0=ALU.mult), waits=[twg, t_sc2], inc="v")
        tsg = E.do("scalar", lambda e, pp=pp: e.activation(
            out=pa_sg2[pp][:, :], in_=pa_rg2[pp][:, :], func=AF.Sigmoid,
            scale=scol[:, 5:6]), waits=[t_g_rd[pp]], inc="act")
        E.do("vector", lambda e, pp=pp: e.tensor_scalar(
            out=pa_q2[:, :], in0=pa_sg2[pp][:, :], scalar1=65536.0,
            scalar2=65535.0, op0=ALU.mult, op1=ALU.min),
            waits=[tsg] + ([E.tok("q_gpsimd")] if rt == 0 else []), inc="v")
        E.do("vector", lambda e, pp=pp: e.tensor_tensor(
            out=pa_m32[pp][:, :], in0=pa_rg2[pp][:, :], in1=pa_q2[:, :],
            op=ALU.mult), waits=[t_m3st[pp]] if t_m3st[pp] else (), inc="v")
        tva = E.tok("v")
        E.do("vector", lambda e, rt=rt, pp=pp: e.tensor_reduce(
            out=sq_a[:, rt:rt + 1], in_=pa_m32[pp][:, :], axis=AX.X, op=ALU.max,
            apply_absolute_value=True), inc="v")
        t_m3st[pp] = E.dma("gpsimd", m3_hbm[rt * 128:(rt + 1) * 128, :],
                           pa_m32[pp][:, :], waits=[tva])
    E.do("vector", lambda e: e.tensor_reduce(
        out=sq_b[:, 0:1], in_=sq_a[:, 0:24], axis=AX.X, op=ALU.max), inc="v")
    tt1 = E.do("tensor", lambda e: e.transpose(
        pB[0:1, 0:128], sq_b[:, 0:1], ident[:, :]),
        waits=[E.tok("v"), E.tok("act")], inc="mm")
    tt2 = E.do("scalar", lambda e: e.activation(
        out=brow_s[0:1, 0:128], in_=pB[0:1, 0:128], func=AF.Copy),
        waits=[tt1], inc="act")
    E.do("vector", lambda e: e.tensor_reduce(
        out=sc(10), in_=brow_s[0:1, 0:128], axis=AX.X, op=ALU.max),
        waits=[tt2], inc="v")
    tq4 = E.dma("gpsimd", ar3_in[:, :], sc(10), waits=[E.tok("v")])
    tar3 = E.do("gpsimd", lambda e: e.collective_compute(
        "AllReduce", ALU.max, replica_groups=[list(range(8))],
        ins=[ar3_in.ap().opt()], outs=[ar3_out.ap().opt()]),
        waits=[tq4], inc="cc")

    tld = E.dma("gpsimd", sc(10), ar3_out[:, :], waits=[tar3])
    E.do("vector", lambda e: e.engine_nop(), waits=[tld], inc="v")
    bcast_col(sc(10), 21)
    E.do("vector", lambda e: e.engine_nop(), waits=[E.tok("act")], inc="v")
    vch(lambda e: e.tensor_tensor(
        out=scol[:, 22:23], in0=scol[:, 21:22], in1=scol[:, 5:6], op=ALU.mult))
    vch(lambda e: e.tensor_scalar(
        out=scol[:, 22:23], in0=scol[:, 22:23], scalar1=1.0 / 65536.0,
        scalar2=None, op0=ALU.mult))
    vch(lambda e: e.tensor_scalar(
        out=scol[:, 23:24], in0=scol[:, 22:23], scalar1=1.0 / 32767.0,
        scalar2=1e-12, op0=ALU.mult, op1=ALU.max))
    vch(lambda e: e.reciprocal(out=scol[:, 24:25], in_=scol[:, 23:24]))
    vch(lambda e: e.tensor_scalar(
        out=scol[:, 7:8], in0=scol[:, 5:6], scalar1=1.0 / 65536.0,
        scalar2=None, op0=ALU.mult))
    vch(lambda e: e.tensor_tensor(
        out=scol[:, 7:8], in0=scol[:, 7:8], in1=scol[:, 24:25], op=ALU.mult))
    t_c3 = E.tok("v")

    t_m4st = [t_m3st[0], t_m3st[1]]
    t_b_rd = [None, None]
    t_q_rd = [None, None]
    for rt in range(24):
        pp = rt % 2
        tw3 = E.dma("sync", pa_sg2[pp][:, :], m3_hbm[rt * 128:(rt + 1) * 128, :],
                    waits=([E.tok("v"), E.tok("q_gpsimd")] if rt < 2 else
                           [t_q_rd[pp]]))
        twu = E.dma("sync", pb_u2[pp][:, :], up_hbm[rt * 128:(rt + 1) * 128, :],
                    waits=[t_b_rd[pp]] if t_b_rd[pp] else ())
        t_b_rd[pp] = E.do("vector", lambda e, pp=pp: e.tensor_scalar(
            out=pa_rg2[pp][:, :], in0=pb_u2[pp][:, :], scalar1=scol[:, 3:4],
            scalar2=None, op0=ALU.mult), waits=[twu], inc="v")
        t_q_rd[pp] = E.do("vector", lambda e, pp=pp: e.tensor_scalar(
            out=pa_q2[:, :], in0=pa_sg2[pp][:, :], scalar1=scol[:, 7:8],
            scalar2=None, op0=ALU.mult), waits=[tw3, t_c3], inc="v")
        E.do("vector", lambda e, pp=pp: e.tensor_tensor(
            out=pa_m32[pp][:, :], in0=pa_q2[:, :], in1=pa_rg2[pp][:, :],
            op=ALU.mult), waits=[t_m4st[pp]], inc="v")
        E.do("vector", lambda e, rt=rt, pp=pp: e.tensor_reduce(
            out=sq_a[:, rt:rt + 1], in_=pa_m32[pp][:, :], axis=AX.X, op=ALU.max,
            apply_absolute_value=True), inc="v")
        t_m4st[pp] = E.dma("gpsimd", m4_hbm[rt * 128:(rt + 1) * 128, :],
                           pa_m32[pp][:, :], waits=[E.tok("v")])
    E.do("vector", lambda e: e.tensor_reduce(
        out=sq_b[:, 0:1], in_=sq_a[:, 0:24], axis=AX.X, op=ALU.max), inc="v")
    tt1 = E.do("tensor", lambda e: e.transpose(
        pB[0:1, 0:128], sq_b[:, 0:1], ident[:, :]),
        waits=[E.tok("v"), E.tok("act")], inc="mm")
    tt2 = E.do("scalar", lambda e: e.activation(
        out=brow_s[0:1, 0:128], in_=pB[0:1, 0:128], func=AF.Copy),
        waits=[tt1], inc="act")
    E.do("vector", lambda e: e.tensor_reduce(
        out=sc(14), in_=brow_s[0:1, 0:128], axis=AX.X, op=ALU.max),
        waits=[tt2], inc="v")
    tq5 = E.dma("gpsimd", ar4_in[:, :], sc(14), waits=[E.tok("v")])
    tar4 = E.do("gpsimd", lambda e: e.collective_compute(
        "AllReduce", ALU.max, replica_groups=[list(range(8))],
        ins=[ar4_in.ap().opt()], outs=[ar4_out.ap().opt()]),
        waits=[tq5], inc="cc")

    sTd = ar.alias(0, [32, 3 * 1024], F32, "sTd")
    rTd_c = [ar.alias(19072, [32, 1024], F32, "rTd0"),
             ar.alias(32896, [32, 1024], F32, "rTd1"),
             ar.alias(36992, [32, 1024], F32, "rTd2")]
    stg_d2a = ar.alias(24576, [128, 2048], F32, "stg_d2a")
    stg_d2b = ar.alias(40960, [128, 1024], F32, "stg_d2b")
    stg_d = [(stg_f2[:, 0:2048], stg_w[:, 0:1024]),
             (stg_d2a[:, :], stg_d2b[:, :])]
    t_redd = [None, None]
    t_ttd = [None, None]
    for rt in range(NDT):
        pp = rt % 2
        sA, sB = stg_d[pp]
        tw = E.dma("sync", sA, l1_down_rm[rt * 128:(rt + 1) * 128, 0:2048],
                   waits=[E.tok("v"), E.tok("mm")] if rt < 2 else [t_redd[pp]])
        tw2b = E.dma("sync", sB, l1_down_rm[rt * 128:(rt + 1) * 128, 2048:3072])
        E.do("vector", lambda e, pp=pp, sA=sA: e.tensor_reduce(
            out=sq_a[:, pp * 96:pp * 96 + 64],
            in_=sA.rearrange("p (a b) -> p a b", b=32),
            axis=AX.X, op=ALU.max, apply_absolute_value=True),
            waits=[tw], inc="v")
        t_redd[pp] = E.do("vector", lambda e, pp=pp, sB=sB: e.tensor_reduce(
            out=sq_a[:, pp * 96 + 64:pp * 96 + 96],
            in_=sB.rearrange("p (a b) -> p a b", b=32),
            axis=AX.X, op=ALU.max, apply_absolute_value=True),
            waits=[tw2b], inc="v")
        E.do("vector", lambda e, pp=pp: e.tensor_scalar(
            out=sq_a[:, 192 + pp * 96:192 + (pp + 1) * 96],
            in0=sq_a[:, pp * 96:(pp + 1) * 96], scalar1=1.0 / 7.0,
            scalar2=1e-12, op0=ALU.mult, op1=ALU.max),
            waits=[E.tok("v")] + ([t_ttd[pp]] if t_ttd[pp] else []), inc="v")
        tv = E.tok("v")
        t_ttd[pp] = tt = E.do("tensor", lambda e, pp=pp: e.transpose(
            pB[0:96, 0:128], sq_a[:, 192 + pp * 96:192 + (pp + 1) * 96],
            ident[:, :]), waits=[tv, E.tok("act")], inc="mm")
        for g3 in range(3):
            E.do("scalar", lambda e, rt=rt, g3=g3: e.activation(
                out=sTd[0:32, g3 * 1024 + rt * 128: g3 * 1024 + (rt + 1) * 128],
                in_=pB[g3 * 32:(g3 + 1) * 32, 0:128], func=AF.Copy),
                waits=[tt], inc="act")
    tse = E.tok("act")
    for g3 in range(3):
        E.do("vector", lambda e, g3=g3: e.reciprocal(
            out=rTd_c[g3][:, :], in_=sTd[:, g3 * 1024:(g3 + 1) * 1024]),
            waits=[tse] if g3 == 0 else (), inc="v")
    t_sc_d = E.tok("v")

    tld = E.dma("gpsimd", sc(14), ar4_out[:, :], waits=[tar4])
    E.do("vector", lambda e: e.engine_nop(), waits=[tld], inc="v")
    bcast_col(sc(14), 25)
    E.do("vector", lambda e: e.engine_nop(), waits=[E.tok("act")], inc="v")
    vch(lambda e: e.tensor_tensor(
        out=scol[:, 26:27], in0=scol[:, 23:24], in1=scol[:, 20:21], op=ALU.mult))
    vch(lambda e: e.tensor_tensor(
        out=scol[:, 27:28], in0=scol[:, 26:27], in1=scol[:, 25:26], op=ALU.mult))
    vch(lambda e: e.tensor_scalar(
        out=scol[:, 10:11], in0=scol[:, 27:28], scalar1=1.0 / 32767.0,
        scalar2=1e-12, op0=ALU.mult, op1=ALU.max))
    vch(lambda e: e.reciprocal(out=scol[:, 9:10], in_=scol[:, 10:11]))
    vch(lambda e: e.tensor_tensor(
        out=scol[:, 9:10], in0=scol[:, 9:10], in1=scol[:, 26:27], op=ALU.mult))
    t_c4 = E.tok("v")
    t_s4 = t_c4

    if dump == "l1sc":
        E.do("gpsimd", lambda e: e.nop(), waits=[E.tok("v"), E.tok("act")])
        E.dma("gpsimd", dbg_out[0:1, 0:32], scol[0:1, 0:32])
        E.dma("gpsimd", dbg_out[0:1, 4096:6144], m3_hbm[0:1, :])
        E.dma("gpsimd", dbg_out[0:1, 6144:8192].bitcast(I32) if False else dbg_out[1:2, 4096:6144], r2_hbm[0:1, :])
        E.dma("gpsimd", dbg_out[2:3, 4096:6144], m4_hbm[0:1, :])
        E.do("gpsimd", lambda e: e.nop(), waits=[E.tok("q_gpsimd")])

    o_bc4 = ar.alias(59 * 1024, [128, 4 * 2048], BF16, "o_bc4")
    wd_b = ar.alias(50 * 1024, [128, 4 * 1024], BF16, "wd_b")
    stg_wC = ar.alias(12 * 1024 + 512, [128, 1024], F32, "stg_wC")
    stg_qC = ar.alias(16 * 1024 + 640, [128, 512], I32, "stg_qC")
    t_mmC = None
    for oc in range(6):
        for itl in range(4):
            rt = oc * 4 + itl
            tw = E.dma("sync", pa_m3[:, :], m4_hbm[rt * 128:(rt + 1) * 128, :],
                       waits=[E.tok("v"), E.tok("mm"), E.tok("q_gpsimd")]
                       if oc == 0 and itl == 0 else [E.tok("v")])
            E.do("vector", lambda e: e.tensor_scalar(
                out=stg_q2[:, :], in0=pa_m3[:, :], scalar1=scol[:, 9:10],
                scalar2=None, op0=ALU.mult), waits=[tw, t_c4, t_s4], inc="v")
            E.do("vector", lambda e, itl=itl: e.tensor_copy(
                out=o_bc4[:, itl * 2048:(itl + 1) * 2048], in_=stg_q2[:, :]),
                waits=[t_mmC] if (t_mmC and itl == 0) else (), inc="v")
        for itl in range(4):
            it = oc * 4 + itl
            for hc in range(2):
                tw = E.dma("sync", stg_wC[:, 0:512],
                           l1_downT[it * 128:(it + 1) * 128,
                                    hc * 512:(hc + 1) * 512],
                           waits=[E.tok("v")])
                rreg = pB[:, 1024:1536]
                sreg2 = pB[:, 1536:2048]
                tm1 = E.do("tensor", lambda e, it=it, hc=hc, rreg=rreg:
                           e.matmul(rreg, Esel[:, (it % 8) * 128:(it % 8 + 1) * 128],
                                    rTd_c[it // 8][:, hc * 512:(hc + 1) * 512],
                                    start=True, stop=True),
                           waits=[t_sc_d, E.tok("v"), E.tok("act")], inc="mm")
                tm2 = E.do("tensor", lambda e, it=it, hc=hc, sreg2=sreg2:
                           e.matmul(sreg2, Esel[:, (it % 8) * 128:(it % 8 + 1) * 128],
                                    sTd[:, (it // 8) * 1024 + hc * 512:
                                        (it // 8) * 1024 + (hc + 1) * 512],
                                    start=True, stop=True), inc="mm")
                E.do("vector", lambda e, rreg=rreg: e.tensor_tensor(
                    out=stg_wC[:, 512:1024], in0=stg_wC[:, 0:512], in1=rreg,
                    op=ALU.mult), waits=[tm1, tw], inc="v")
                E.do("vector", lambda e: e.tensor_copy(
                    out=stg_qC[:, :], in_=stg_wC[:, 512:1024]), inc="v")
                tse2 = E.do("scalar", lambda e, sreg2=sreg2: e.activation(
                    out=stg_wC[:, 512:1024], in_=sreg2, func=AF.Copy),
                    waits=[tm2, E.tok("v")], inc="act")
                E.do("vector", lambda e, itl=itl, hc=hc: e.tensor_tensor(
                    out=wd_b[:, itl * 1024 + hc * 512:itl * 1024 + (hc + 1) * 512],
                    in0=stg_qC[:, :], in1=stg_wC[:, 512:1024], op=ALU.mult),
                    waits=[tse2], inc="v")
        t_wd = E.tok("v")
        for ci, tc in enumerate(TC):
            for d in range(NDT):
                zreg = pA[:, 512 + (d % 2) * 512:1024 + (d % 2) * 512] if False \
                    else pA[:, (d % 2) * 512:(d % 2) * 512 + 512]
                for itl in range(4):
                    E.do("tensor", lambda e, d=d, itl=itl, tc=tc, zreg=zreg:
                         e.matmul(zreg,
                                  wd_b[:, itl * 1024 + d * 128:
                                       itl * 1024 + (d + 1) * 128],
                                  o_bc4[:, itl * 2048 + tc.start:
                                        itl * 2048 + tc.stop],
                                  start=(itl == 0), stop=(itl == 3)),
                         waits=[t_wd, E.tok("v")] if itl == 0 else (), inc="mm")
                E.do("vector", lambda e, d=d, tc=tc, zreg=zreg:
                     e.scalar_tensor_tensor(
                         out=xsl(d, tc), in0=zreg, scalar=scol[:, 10:11],
                         in1=xsl(d, tc), op0=ALU.mult, op1=ALU.add),
                     waits=[E.tok("mm")], inc="v")
        t_mmC = E.tok("mm")
    E.flush()
    if dump == "l1ffn":
        for d in range(NDT):
            E.dma("gpsimd", dbg_out[:, d * 2048:(d + 1) * 2048], xsl(d))
        E.do("gpsimd", lambda e: e.nop(), waits=[E.tok("q_gpsimd")])
    if upto == "l1ffn":
        _finish(E, nc, y_out, xsl)
        return nc

    for d in range(NDT):
        E.dma("gpsimd", x_spill[d * 128:(d + 1) * 128, :], xsl(d),
              waits=[E.tok("v"), E.tok("mm")] if d == 0 else ())
    t_spill = E.tok("q_gpsimd")

    ar.reset(0)
    qn_s = ar.alloc([128, T_LOC], BF16, "qn_s")
    kn_s = ar.alloc([128, T_FULL], BF16, "kn_s")
    P2q = ar.alloc([128, 4 * 1024], BF16, "P2q")
    rstg2 = ar.alloc([128, 2 * 1024], BF16, "rstg2")
    sqv8 = ar.alloc([128, 512], F32, "sqv8")
    fsc = ar.alloc([2, 512], F32, "fsc")
    fsc2 = ar.alloc([2, 512], F32, "fsc2")
    fex = ar.alloc([128, 512], F32, "fex")
    on_st = ar.alloc([128, 512], BF16, "on_st")
    vn_b = nc.alloc_sbuf_tensor_at("vn_b", [128, 32 * 1024], BF16,
                                   offset=nc.lookup_mloc(xs).addr)

    for r in range(2):
        for vt in range(16):
            E.dma("sync",
                  vn_b[:, (r * 16 + vt) * 1024:(r * 16 + vt + 1) * 1024]
                  .bitcast(I32),
                  ag_vn_out[r * 2048 + vt * 128: r * 2048 + (vt + 1) * 128, :],
                  waits=[tag_v, t_spill] if (r == 0 and vt == 0) else ())
    t_kvld = E.tok("q_sync")
    E.do("vector", lambda e: e.engine_nop(), waits=[t_kvld], inc="v")

    wout_b = ar.alloc([128, 8 * 1024], BF16, "wout_b")
    onld = ar.alloc([128, 8 * 512], BF16, "onld")
    wstg8b = ar.alloc([128, 1024], F32, "wstg8b")
    onld2 = [onld, ar.alias(4096, [128, 8 * 512], BF16, "onld_b")]
    for kt in range(NDT):
        tw = E.dma("gpsimd", wstg8b[:, :], l1_wout[kt * 128:(kt + 1) * 128, :],
                   waits=[E.tok("v")])
        E.do("vector", lambda e, kt=kt: e.tensor_copy(
            out=wout_b[:, kt * 1024:(kt + 1) * 1024], in_=wstg8b[:, :]),
            waits=[tw], inc="v")
    t_wo8 = E.tok("v")

    QC4 = [slice(i * 512, (i + 1) * 512) for i in range(4)]
    t_onst = None
    relu_toks8 = [None, None]
    rstg_toks = [None, None]
    pv_toks = [None, None, None, None]
    for pr in range(8):
        slab = pr // 2
        ha = slab * 4 + (pr % 2) * 2
        sa = 32 * (ha % 4)
        sb = sa + 32
        if pr % 2 == 0:
            E.dma("sync", qn_s[:, :], qn_hbm[slab * 128:(slab + 1) * 128, :],
                  waits=[E.tok("mm"), E.tok("v")])
            for r in range(2):
                E.dma("sync",
                      kn_s[:, r * T_LOC:(r + 1) * T_LOC].bitcast(I32),
                      ag_kn_out[r * 512 + slab * 128:
                                r * 512 + (slab + 1) * 128, :],
                      waits=[tag_k] if (pr == 0 and r == 0) else ())
            t_ld8v = E.tok("q_sync")
            E.do("vector", lambda e: e.engine_nop(), waits=[t_ld8v], inc="v")
        t_ld8 = E.tok("v")
        for ci, qc in enumerate(QC4):
            ov = pB[:, 512:1024]
            sq_toks = [None] * 4

            def emit_pv(j, ha=ha, ov=ov, sq_toks=sq_toks):
                sl = j % 4
                for hj in range(2):
                    pv_toks[sl] = E.do(
                        "tensor", lambda e, j=j, hj=hj, sl=sl, ha=ha, ov=ov:
                        e.matmul(
                            ov[hj * 64:(hj + 1) * 64, :],
                            vn_b[:, j * 1024 + (ha + hj) * 64:
                                 j * 1024 + (ha + hj + 1) * 64],
                            P2q[:, sl * 1024 + hj * 512:
                                sl * 1024 + (hj + 1) * 512],
                            start=(j == 0), stop=(j == 31),
                            tile_position=(0, hj * 64)),
                        waits=[sq_toks[sl]] if hj == 0 else (), inc="mm")

            for kt in range(32):
                par = kt % 2
                sl = kt % 4
                for hj, sp in ((0, sa), (1, sb)):
                    reg = pA[:, par * 1024 + hj * 512:
                             par * 1024 + (hj + 1) * 512]
                    E.do("tensor", lambda e, kt=kt, sp=sp, qc=qc, reg=reg:
                         e.matmul(
                             reg,
                             kn_s[sp:sp + 32, kt * 128:(kt + 1) * 128],
                             qn_s[sp:sp + 32, qc],
                             start=True, stop=True, tile_position=(sp, 0)),
                         waits=(([t_ld8] if kt == 0 else []) +
                                ([relu_toks8[par]] if relu_toks8[par] else []))
                         if hj == 0 else (), inc="mm")
                tmm = E.tok("mm")
                relu_toks8[par] = tr = E.do(
                    "scalar", lambda e, par=par: e.activation(
                        out=rstg2[:, par * 1024:(par + 1) * 1024],
                        in_=pA[:, par * 1024:(par + 1) * 1024], func=AF.Relu),
                    waits=[tmm] + ([rstg_toks[par]] if rstg_toks[par] else []),
                    inc="act")
                E.do("vector", lambda e, par=par, sl=sl: e.tensor_tensor(
                    out=P2q[:, sl * 1024:sl * 1024 + 512],
                    in0=rstg2[:, par * 1024:par * 1024 + 512],
                    in1=rstg2[:, par * 1024:par * 1024 + 512], op=ALU.mult),
                    waits=[tr] + ([pv_toks[sl]] if pv_toks[sl] else []),
                    inc="v")
                rstg_toks[par] = sq_toks[sl] = E.do(
                    "vector", lambda e, par=par, sl=sl: e.tensor_tensor(
                        out=P2q[:, sl * 1024 + 512:(sl + 1) * 1024],
                        in0=rstg2[:, par * 1024 + 512:(par + 1) * 1024],
                        in1=rstg2[:, par * 1024 + 512:(par + 1) * 1024],
                        op=ALU.mult), inc="v")
                if kt >= 2:
                    emit_pv(kt - 2)
            emit_pv(30)
            emit_pv(31)
            t_pv = E.tok("mm")
            tsq = E.do("scalar", lambda e, ov=ov: e.activation(
                out=sqv8[:, :], in_=ov, func=AF.Square), waits=[t_pv], inc="act")
            tnm = E.do("tensor", lambda e: e.matmul(
                pB[0:2, 0:512], E64[:, :], sqv8[:, :], start=True, stop=True),
                waits=[tsq, E.tok("v")], inc="mm")
            E.do("scalar", lambda e: e.activation(
                out=fsc[:, :], in_=pB[0:2, 0:512], func=AF.Sqrt),
                waits=[tnm], inc="act")
            E.do("vector", lambda e: e.engine_nop(), waits=[E.tok("act")], inc="v")
            vch(lambda e: e.tensor_scalar_max(
                out=fsc2[:, :], in0=fsc[:, :], scalar1=1e-12))
            vch(lambda e: e.reciprocal(out=fsc2[:, :], in_=fsc2[:, :]))
            tth = E.do("scalar", lambda e: e.activation(
                out=fsc[:, :], in_=fsc[:, :], func=AF.Tanh),
                waits=[E.tok("v")], inc="act")
            E.do("vector", lambda e: e.engine_nop(), waits=[tth], inc="v")
            vch(lambda e: e.tensor_tensor(
                out=fsc[:, :], in0=fsc[:, :], in1=fsc2[:, :], op=ALU.mult))
            tf = E.tok("v")
            tfe = E.do("tensor", lambda e: e.matmul(
                pB[:, 1024:1536], E2[:, :], fsc[:, :], start=True, stop=True),
                waits=[tf, E.tok("act")], inc="mm")
            tfc = E.do("scalar", lambda e: e.activation(
                out=fex[:, :], in_=pB[:, 1024:1536], func=AF.Copy),
                waits=[tfe], inc="act")
            ton = E.do("vector", lambda e, ov=ov: e.tensor_tensor(
                out=on_st[:, :], in0=ov, in1=fex[:, :], op=ALU.mult),
                waits=[tfc] + ([t_onst] if t_onst else []), inc="v")
            t_onst = E.dma("gpsimd",
                           outn_hbm[pr * 128:(pr + 1) * 128,
                                    ci * 512:(ci + 1) * 512],
                           on_st[:, :], waits=[ton])
    E.flush()

    for d in range(NDT):
        E.dma("sync", xsl(d), x_spill[d * 128:(d + 1) * 128, :],
              waits=[E.tok("q_gpsimd"), E.tok("mm"), E.tok("v")]
              if d == 0 else ())
    t_xre = E.tok("q_sync")
    t_mmw = [None, None]
    for ci, tc in enumerate(TC):
        ob = onld2[ci % 2]
        for kt in range(NDT):
            E.dma("sync", ob[:, kt * 512:(kt + 1) * 512],
                  outn_hbm[kt * 128:(kt + 1) * 128, tc],
                  waits=([E.tok("v")] +
                         ([t_mmw[ci % 2]] if t_mmw[ci % 2] else [E.tok("mm")]))
                  if kt == 0 else ())
        t_on8 = E.tok("q_sync")
        for d in range(NDT):
            reg = pB[:, 1024 + (d % 2) * 512:1536 + (d % 2) * 512]
            for kt in range(NDT):
                E.do("tensor", lambda e, d=d, kt=kt, reg=reg, ob=ob: e.matmul(
                    reg, wout_b[:, kt * 1024 + d * 128: kt * 1024 + (d + 1) * 128],
                    ob[:, kt * 512:(kt + 1) * 512],
                    start=(kt == 0), stop=(kt == NDT - 1)),
                    waits=[t_on8, t_wo8, t_xre, E.tok("v")]
                    if kt == 0 else (), inc="mm")
            E.do("vector", lambda e, d=d, tc=tc, reg=reg: e.tensor_add(
                out=xsl(d, tc), in0=xsl(d, tc), in1=reg),
                waits=[E.tok("mm")], inc="v")
        t_mmw[ci % 2] = E.tok("mm")
    E.flush()

    _finish(E, nc, y_out, xsl)
    return nc

def _finish(E, nc, y_out, xsl):
    for d in range(NDT):
        E.dma("gpsimd", y_out[d * 128:(d + 1) * 128, :], xsl(d),
              waits=[E.tok("v"), E.tok("act"), E.tok("mm")] if d == 0 else ())
    tst = E.tok("q_gpsimd")
    E.do("gpsimd", lambda e: e.nop(), waits=[tst])
    E.flush()


_CACHE = {}


def _consts():
    ident = np.eye(128, dtype=np.float32)
    Ekq = np.zeros((128, 4), np.float32)
    for p in range(128):
        if p % 32 < 16:
            Ekq[p, p // 32] = 1.0
    Eexp = np.zeros((4, 128), np.float32)
    for p in range(128):
        Eexp[p // 32, p] = 1.0
    E64 = np.zeros((128, 2), np.float32)
    E64[np.arange(128), np.arange(128) // 64] = 1.0
    E2 = np.zeros((2, 128), np.float32)
    E2[np.arange(128) // 64, np.arange(128)] = 1.0
    Esel = np.zeros((32, 1024), np.float32)
    for it in range(8):
        for p in range(128):
            Esel[4 * it + p // 32, it * 128 + p] = 1.0
    return ident, Ekq, Eexp, E64, E2, Esel


def _colmajor(v):
    return np.ascontiguousarray(v.reshape(-1, 128).T)


def kernel(L0_attn_x, L0_attn_Wq, L0_attn_Wkv_down, L0_attn_Wk_up, L0_attn_Wv_up,
           L0_attn_Wo, L0_ffn_ln_gamma, L0_ffn_ln_beta, L0_ffn_fc1_q,
           L0_ffn_fc1_scales, L0_ffn_fc1_bias, L0_ffn_fc2_q, L0_ffn_fc2_scales,
           L0_ffn_fc2_bias, L1_attn_Wq, L1_attn_Wkv, L1_attn_Wout,
           L1_ffn_w_gate, L1_ffn_w_up, L1_ffn_w_down, _dump=None, _upto=None):
    f32 = np.float32
    asc = np.ascontiguousarray

    ident, Ekq, Eexp, E64, E2, Esel = _consts()
    shared = {
        "l0_wq": asc(np.asarray(L0_attn_Wq, f32)),
        "l0_wkvdT": asc(np.asarray(L0_attn_Wkv_down, f32).T),
        "l0_wku": asc(np.asarray(L0_attn_Wk_up, f32)),
        "l0_wvuT": asc(np.asarray(L0_attn_Wv_up, f32).T),
        "l0_woT": asc(np.asarray(L0_attn_Wo, f32).T),
        "l0_fc1q": asc(np.asarray(L0_ffn_fc1_q, np.int32)),
        "l0_fc2q": asc(np.asarray(L0_ffn_fc2_q, np.int32)),
        "l0_fc1s": asc(np.asarray(L0_ffn_fc1_scales, f32)),
        "l0_fc2s": asc(np.asarray(L0_ffn_fc2_scales, f32)),
        "ln0_g": _colmajor(np.asarray(L0_ffn_ln_gamma, f32)),
        "ln0_b": _colmajor(np.asarray(L0_ffn_ln_beta, f32)),
        "fc1b_c": _colmajor(np.asarray(L0_ffn_fc1_bias, f32)),
        "fc2b_c": _colmajor(np.asarray(L0_ffn_fc2_bias, f32)),
        "l1_wout": asc(np.asarray(L1_attn_Wout, f32)),
        "l1_upT": asc(np.asarray(L1_ffn_w_up, f32).T),
        "l1_gateT": asc(np.asarray(L1_ffn_w_gate, f32).T),
        "l1_downT": asc(np.asarray(L1_ffn_w_down, f32).T),
        "l1_up_rm": asc(np.asarray(L1_ffn_w_up, f32)),
        "l1_gate_rm": asc(np.asarray(L1_ffn_w_gate, f32)),
        "l1_down_rm": asc(np.asarray(L1_ffn_w_down, f32)),
        "c_ident": ident, "c_Ekq": Ekq, "c_Eexp": Eexp, "c_E64": E64,
        "c_E2": E2, "c_Esel": Esel,
    }
    Wq1 = np.asarray(L1_attn_Wq, f32)
    Wkv1 = np.asarray(L1_attn_Wkv, f32)
    wqp = np.zeros((D, 512), f32)
    wkp = np.zeros((D, 512), f32)
    wv = np.zeros((D, 1024), f32)
    for h in range(16):
        wqp[:, h * 32:h * 32 + 16] = Wq1[:, h * 16:(h + 1) * 16]
        wkp[:, h * 32:h * 32 + 16] = Wkv1[:, h * 80:h * 80 + 16]
        wv[:, h * 64:(h + 1) * 64] = Wkv1[:, h * 80 + 16:(h + 1) * 80]
    shared["l1_wqp"] = wqp
    shared["l1_wkp"] = wkp
    shared["l1_wv"] = wv

    x = np.asarray(L0_attn_x, f32)
    in_maps = []
    for c in range(8):
        b, half = c // 2, c % 2
        m = dict(shared)
        m["x_in"] = asc(x[b, half * T_LOC:(half + 1) * T_LOC, :].T)
        in_maps.append(m)

    key = (_dump, _upto)
    if key not in _CACHE:
        _CACHE[key] = build_program(dump=_dump, upto=_upto)
    res = run_bass_kernel_spmd(_CACHE[key], in_maps, core_ids=list(range(8)))

    out = np.empty((4, T_FULL, D), f32)
    for c in range(8):
        b, half = c // 2, c % 2
        out[b, half * T_LOC:(half + 1) * T_LOC, :] = res.results[c]["y_out"].T
    if _dump:
        return out, [res.results[c].get("dbg_out") for c in range(8)]
    return out
